# version 18
# baseline (speedup 1.0000x reference)
import math
import numpy as np
import ml_dtypes
import concourse.bass as bass
import concourse.dve_ops as _dve_ops
import concourse.mybir as mybir
from concourse.bass_types import AP
from concourse.bass_utils import run_bass_kernel_spmd

F32 = mybir.dt.float32
BF16 = mybir.dt.bfloat16
I32 = mybir.dt.int32
AF = mybir.ActivationFunctionType
ALU = mybir.AluOpType
AX = mybir.AxisListType

B, T, D = 4, 4096, 1024
TQ = 2048
DFF = 4096
NG = 8
NKC = 32
NQT = 4
NU = 64
LAMBDA_INIT = float(0.8 - 0.6 * math.exp(-0.3 * 12))
SCALING = float(32 ** -0.5)

BASE = 16512
OFF_A = BASE
OFF_B = OFF_A + 66560
OFF_C = OFF_B + 65536
OFF_D = OFF_C + 32768
OFF_E = OFF_D + 24576
OFF_F = OFF_E + 4096
OFF_G = OFF_F + 8192


class Plan:
    def __init__(self, nc):
        self.nc = nc
        self.steps = {k: [] for k in ("sync", "gpsimd", "scalar", "tensor", "vector")}
        self.cnt = {}
        self.sems = {}

    def sem(self, name):
        if name not in self.sems:
            self.sems[name] = self.nc.alloc_semaphore(name)
            self.cnt[name] = 0
        return self.sems[name]

    def n(self, name):
        self.sem(name)
        return self.cnt[name]

    def step(self, eng, fn, waits=(), inc=None, inc_n=1, dr=False):
        ws = [(self.sem(s), v) for (s, v) in waits if v > 0]
        if inc is None:
            incs = []
        elif isinstance(inc, str):
            incs = [(inc, inc_n)]
        else:
            incs = list(inc)
        self.steps[eng].append((fn, ws, [(self.sem(s), n) for s, n in incs], dr))
        for s, n in incs:
            self.cnt[s] += n

    def dma(self, eng, fn, waits=(), inc=None):
        self.step(eng, fn, waits, inc, 16)

    def emit(self, name, eng):
        for fn, ws, incs, dr in self.steps[name]:
            for s, v in ws:
                eng.wait_ge(s, v)
            if dr:
                eng.drain()
            ins = fn(eng)
            if incs:
                ins.then_inc(*incs[0])
                for s, n in incs[1:]:
                    eng.drain().then_inc(s, n)


def build_nc():
    nc = bass.Bass("TRN2", target_bir_lowering=False, monotonic_sem_count=0)

    dx = nc.dram_tensor("x", [T, D], F32, kind="ExternalInput").ap()
    dwqT = nc.dram_tensor("wqT", [D, D], F32, kind="ExternalInput").ap()
    dwkT = nc.dram_tensor("wkT", [D, D], F32, kind="ExternalInput").ap()
    dwvT = nc.dram_tensor("wvT", [D, D], F32, kind="ExternalInput").ap()
    dwoT = nc.dram_tensor("woT", [D, D], F32, kind="ExternalInput").ap()
    dwin = nc.dram_tensor("win", [D, DFF], F32, kind="ExternalInput").ap()
    dwout = nc.dram_tensor("wout", [DFF, D], F32, kind="ExternalInput").ap()
    dbin = nc.dram_tensor("b_in_pp", [128, 32], F32, kind="ExternalInput").ap()
    dbout = nc.dram_tensor("b_out_pp", [128, 8], F32, kind="ExternalInput").ap()
    dlam = nc.dram_tensor("lam_r", [1, 128], F32, kind="ExternalInput").ap()
    dsub = nc.dram_tensor("subln_r", [1, 64], F32, kind="ExternalInput").ap()
    dib = nc.dram_tensor("ident_bf", [128, 128], BF16, kind="ExternalInput").ap()
    dif = nc.dram_tensor("ident_f", [128, 128], F32, kind="ExternalInput").ap()
    dout = nc.dram_tensor("out", [TQ, D], F32, kind="ExternalOutput").ap()
    d_h = nc.dram_tensor("sp_h", [TQ, D], F32).ap()
    d_hid = nc.dram_tensor("sp_hid", [DFF, TQ], BF16).ap()
    d_cc = [nc.dram_tensor(f"cc{i}", [1, 8], F32).ap() for i in range(4)]

    def at(name, shape, dtype, off):
        return nc.alloc_sbuf_tensor_at(name, shape, dtype, offset=off)

    ln1 = at("ln1", [128, 32768], BF16, OFF_A)
    v_all = at("v_all", [128, 33280], BF16, OFF_A)
    ln2 = at("ln2", [128, 16384], F32, OFF_A)
    win_i = at("win_i", [128, 32768], BF16, OFF_A)
    ln1T = at("ln1T", [128, 32768], BF16, OFF_B)
    wobf = at("wobf", [128, 8192], BF16, OFF_B)
    wout_i = at("wout_i", [128, 32768], BF16, OFF_B)
    xs5 = at("xs5", [128, 4096], F32, OFF_B + 16384)
    xC = at("xC", [128, 4096], F32, OFF_C)
    afin = at("afin", [128, 16384], BF16, OFF_C)
    xsqT = at("xsqT", [128, 16384], BF16, OFF_C)
    outTs = at("outTs", [128, 4096], F32, OFF_C)
    kq0 = at("kq0", [128, 4096], BF16, OFF_D)
    kq1 = at("kq1", [128, 4096], BF16, OFF_D + 8192)
    qg0 = at("qg0", [128, 2048], BF16, OFF_D + 16384)
    qg1 = at("qg1", [128, 2048], BF16, OFF_D + 20480)
    woq = at("woq", [128, 4096], F32, OFF_D)
    htile = at("htile", [128, 2048], F32, OFF_D + 16384)
    wst5 = at("wst5", [128, 2048], F32, OFF_D)
    wsc5 = at("wsc5", [128, 256], F32, OFF_D + 8192)
    wi32 = at("wi32", [128, 128], I32, OFF_D + 9216)
    hidb = at("hidb", [128, 8192], BF16, OFF_D)
    wo_i32 = at("wo_i32", [128, 1024], I32, OFF_D + 16384)
    hin7 = at("hin7", [128, 4096], F32, OFF_D)
    outs7 = at("outs7", [128, 4096], F32, OFF_D + 8192)
    rhs7 = at("rhs7", [128, 1024], BF16, OFF_D + 16384)
    xsq_b = at("xsq_b", [128, 1024], BF16, OFF_D + 18432)
    xsq_i = at("xsq_i", [128, 1024], I32, OFF_D + 20480)
    P_AB = at("P_AB", [128, 2048], BF16, OFF_E)
    sc_row = at("sc_row", [1, 128], F32, OFF_E)
    hraw7 = at("hraw7", [128, 1024], BF16, OFF_E)
    hq_i32 = at("hq_i32", [128, 512], I32, OFF_E + 2048)
    wstF = at("wstF", [128, 1024], F32, OFF_F)
    wbf = at("wbf", [128, 1024], BF16, OFF_F + 4096)
    lamst = at("lamst", [1, 128], F32, OFF_F + 6144)
    sqs = at("sqs", [128, 512], BF16, OFF_F + 6144)
    scSC = at("scSC", [128, 128], F32, OFF_D + 9728)
    scr5 = at("scr5", [128, 1024], F32, OFF_F)
    wstF2 = at("wstF2", [128, 1024], F32, OFF_F + 4096)
    g = OFF_G
    def G(name, shape, dtype):
        nonlocal g
        t = at(name, shape, dtype, g)
        g += ((shape[1] * mybir.dt.size(dtype) + 31) // 32) * 32
        return t
    ident_b = G("ident_b", [128, 128], BF16)
    ident_f = G("ident_f", [128, 128], F32)
    ones64 = G("ones64", [64, 1], F32)
    lam_s = G("lam_s", [1, 8], F32)
    sub_r = G("sub_r", [1, 64], F32)
    ones_any = G("ones_any", [128, 128], F32)
    mneg = G("mneg", [128, 8], F32)

    IND = G("IND", [128, 4], BF16)
    mgq4a = G("mgq4a", [4, 1], F32)
    mgq4b = G("mgq4b", [4, 1], F32)
    mgq4t = G("mgq4t", [4, 1], F32)
    o0s = G("o0s", [128, 512], F32)
    o1s = G("o1s", [128, 512], F32)
    Abc = G("Abc", [128, 512], F32)
    Bbc = G("Bbc", [128, 512], F32)
    lamcol = G("lamcol", [128, 1], F32)
    epsb = G("epsb", [128, 1], F32)
    b_in = G("b_in", [128, 32], F32)
    b_out = G("b_out", [128, 8], F32)
    amax1 = G("amax1", [128, 1], F32)
    amax2 = amax1
    amx32 = G("amx32", [128, 32], F32)
    amx16 = amx32[0:128, 0:16]
    mg_q = amx32[0:1, 0:16]
    ccs = G("ccs", [1, 8], F32)
    s2col = G("s2col", [128, 1], F32)
    s3col = G("s3col", [128, 1], F32)
    gsc = G("gsc", [128, 32], F32)
    invh = G("invh", [128, 32], F32)
    acolw = G("acolw", [128, 32], F32)
    rocol = G("rocol", [128, 32], F32)
    sc_t = G("sc_t", [128, 1], F32)
    sc_t2 = G("sc_t2", [128, 2], F32)
    onesb = G("onesb", [128, 1], BF16)
    wi32b = at("wi32b", [128, 1024], I32, OFF_D + 10240)
    scbb = at("scbb", [128, 128], F32, OFF_D + 14336)
    hq7b = at("hq7b", [128, 512], I32, OFF_F)
    stat_s8 = amx32[0:128, 16:24]

    stat_m8 = invh[0:128, 0:8]
    stat_q8 = invh[0:128, 8:16]
    stat_r8 = invh[0:128, 16:24]
    assert g <= 229344, f"SBUF overflow {g}"

    SA = nc.alloc_psum_tensor("SA", [128, 1024], F32)
    SB = nc.alloc_psum_tensor("SB", [128, 1024], F32)
    O0 = nc.alloc_psum_tensor("O0", [128, 512], F32)
    O1 = nc.alloc_psum_tensor("O1", [128, 512], F32)
    EPI = nc.alloc_psum_tensor("EPI", [128, 512], F32)
    PRJ = nc.alloc_psum_tensor("PRJ", [128, 512], F32)
    PB = [SA[:, 0:512], SA[:, 512:1024], SB[:, 0:512], SB[:, 512:1024],
          O0[:, :], O1[:, :], EPI[:, :], PRJ[:, :]]

    P3B = [P_AB[:, 0:1024], P_AB[:, 1024:2048], afin[:, 15360:16384]]

    p = Plan(nc)
    W = lambda s, v: (s, v)

    def lnT_rhs(e, tt):
        return AP(tensor=ln1T, offset=1024 * (4 * tt) + 128 * e,
                  ap=[[32768, 128], [1024, 4], [1, 128]])

    for dst, src in ((ident_b[:, :], dib), (ident_f[:, :], dif),
                     (lamst[:, :], dlam), (sub_r[:, :], dsub),
                     (b_in[:, :], dbin), (b_out[:, :], dbout)):
        p.dma("sync", lambda sy, d=dst, s=src: sy.dma_start(out=d, in_=s), inc="d_c")
    p.step("vector", lambda v: v.memset(ones64[:, :], 1.0))
    p.step("vector", lambda v: v.memset(epsb[:, :], 1e-5))
    p.step("vector", lambda v: v.memset(amax1[:, :], 0.0))
    p.step("vector", lambda v: v.memset(ccs[:, :], 0.0))
    p.step("vector", lambda v: v.memset(IND[:, :], 0.0))
    for h in range(4):
        p.step("vector", lambda v, h_=h: v.memset(IND[32 * h_:32 * (h_ + 1),
                                                      h_:h_ + 1], 1.0), dr=True)
    p.step("vector", lambda v: v.tensor_tensor(mg_q[0:1, 0:8], lamst[0:1, 0:8],
                                               lamst[0:1, 0:8], ALU.mult),
           waits=[W("d_c", 16 * 6)])
    p.step("vector", lambda v: v.tensor_tensor(mg_q[0:1, 0:8], lamst[0:1, 0:32],
                                               lamst[0:1, 32:64], ALU.mult) if False
           else v.tensor_tensor(sc_row[0:1, 0:32], lamst[0:1, 0:32],
                                lamst[0:1, 32:64], ALU.mult))
    p.step("vector", lambda v: v.reduce_sum(lam_s[0:1, 4:5], sc_row[0:1, 0:32], AX.X), dr=True)
    p.step("vector", lambda v: v.tensor_tensor(sc_row[0:1, 0:32], lamst[0:1, 64:96],
                                               lamst[0:1, 96:128], ALU.mult), dr=True)
    p.step("vector", lambda v: v.reduce_sum(lam_s[0:1, 5:6], sc_row[0:1, 0:32], AX.X),
           inc="c_dots", dr=True)
    p.step("scalar", lambda s: s.activation(lam_s[0:1, 4:5], lam_s[0:1, 4:5], AF.Exp),
           waits=[W("c_dots", 1)])
    p.step("scalar", lambda s: (s.drain(), s.activation(
        lam_s[0:1, 5:6], lam_s[0:1, 5:6], AF.Exp))[1], inc="c_dots")
    p.step("vector", lambda v: v.tensor_tensor(lam_s[0:1, 0:1], lam_s[0:1, 4:5],
                                               lam_s[0:1, 5:6], ALU.subtract),
           waits=[W("c_dots", 2)])
    p.step("vector", lambda v: v.tensor_scalar(lam_s[0:1, 0:1], lam_s[0:1, 0:1],
                                               LAMBDA_INIT, None, ALU.add), dr=True)
    p.step("vector", lambda v: v.tensor_scalar(lam_s[0:1, 1:2], lam_s[0:1, 0:1],
                                               -1.0, None, ALU.mult), inc="c_lam", dr=True)
    p.step("vector", lambda v: v.tensor_scalar(sub_r[0:1, :], sub_r[0:1, :],
                                               1.0 - LAMBDA_INIT, None, ALU.mult),
           inc="c_lam")
    p.step("vector", lambda v: v.memset(ones_any[:, :], 1.0), inc="c_or")
    p.step("vector", lambda v: v.memset(onesb[:, :], 1.0))
    p.step("tensor", lambda t: t.matmul(
        PRJ[0:128, 0:1], ones_any[0:1, :], lam_s[0:1, 1:2],
        start=True, stop=True, skip_group_check=True),
        waits=[W("c_lam", 1), W("c_or", 1)], inc="t_lc")
    p.step("vector", lambda v: v.tensor_copy(lamcol[:, 0:1], PRJ[0:128, 0:1]),
           waits=[W("t_lc", 1)], inc=[("c_or", 1), ("v_prj", 1)])

    for i in range(32):
        g_, j_ = i // 4, i % 4
        b4 = 4 * (g_ % 2)
        xt = xC[:, 1024 * j_:1024 * (j_ + 1)]
        lt = ln1[:, 1024 * i:1024 * (i + 1)]
        wts = [W("v_ln", i - 3), W("a_sq1", i - 3)] if i >= 4 else []
        xsem = f"d_x{i % 4}"
        p.dma("sync", lambda sy, d=xt, r0=128 * i: sy.dma_start(
            out=d, in_=dx[r0:r0 + 128, :]), waits=wts, inc=xsem)
        sq_w = [W(xsem, 16 * (g_ + 1))]
        if j_ == 0 and g_ >= 2:
            sq_w.append(W("v_q8", g_ - 1))
        p.step("scalar", lambda s, x_=xt, c_=b4 + j_: (s.drain(), s.activation(
            scr5[:, :], x_, AF.Square, accum_out=stat_q8[:, c_:c_ + 1]))[1],
            waits=sq_w, inc="a_sq1")
        p.step("vector", lambda v, x_=xt, c_=b4 + j_: v.reduce_sum(
            stat_s8[:, c_:c_ + 1], x_, AX.X),
            waits=[W(xsem, 16 * (g_ + 1))], dr=(j_ == 0))
        if j_ < 3:
            continue
        p.step("vector", lambda v, b_=b4: v.tensor_scalar(
            stat_m8[:, b_:b_ + 4], stat_s8[:, b_:b_ + 4],
            1.0 / 1024.0, None, ALU.mult), dr=True)
        p.step("vector", lambda v, b_=b4: v.tensor_scalar(
            stat_q8[:, b_:b_ + 4], stat_q8[:, b_:b_ + 4],
            1.0 / 1024.0, None, ALU.mult),
            waits=[W("a_sq1", 4 * (g_ + 1))], inc="v_q8")
        p.step("vector", lambda v, b_=b4: v.tensor_tensor(
            stat_r8[:, b_:b_ + 4], stat_m8[:, b_:b_ + 4],
            stat_m8[:, b_:b_ + 4], ALU.mult), dr=True)
        p.step("vector", lambda v, b_=b4: v.tensor_tensor(
            stat_r8[:, b_:b_ + 4], stat_q8[:, b_:b_ + 4],
            stat_r8[:, b_:b_ + 4], ALU.subtract), inc="v_var", dr=True)
        sqw = [W("v_var", g_ + 1)]
        if g_ >= 2:
            sqw.append(W("v_ln", 4 * (g_ - 1)))
        p.step("scalar", lambda s, b_=b4: s.activation(
            stat_r8[:, b_:b_ + 4], stat_r8[:, b_:b_ + 4],
            AF.Sqrt, bias=epsb[:, 0:1]), waits=sqw, inc="a_sd")
        p.step("vector", lambda v, b_=b4: v.reciprocal(
            stat_r8[:, b_:b_ + 4], stat_r8[:, b_:b_ + 4]),
            waits=[W("a_sd", g_ + 1)], dr=True)
        for jj in range(4):
            ii = 4 * g_ + jj
            p.step("vector", lambda v, jj_=jj, ii_=ii, c_=b4 + jj: v.tensor_scalar(
                ln1[:, 1024 * ii_:1024 * (ii_ + 1)],
                xC[:, 1024 * jj_:1024 * (jj_ + 1)],
                stat_m8[:, c_:c_ + 1], stat_r8[:, c_:c_ + 1],
                ALU.subtract, ALU.mult), inc="v_ln", dr=(jj == 0))
        for it in range(4 * g_, 4 * g_ + 4):
            for e in range(8):
                t_id = it * 8 + e
                bank = PB[t_id % 8].bitcast(BF16)
                wts = [W("v_ln", it + 1)]
                if t_id >= 8:
                    prev = t_id - 8
                    wts.append(W("v_e2" if prev % 2 == 0 else "a_e2",
                                 prev // 2 + 1))
                p.step("tensor", lambda t, b_=bank, i_=it, e_=e: t.transpose(
                    b_[0:128, 0:128],
                    ln1[:, 1024 * i_ + 128 * e_:1024 * i_ + 128 * (e_ + 1)],
                    ident_b[:, :]), waits=wts, inc="t_tr2")
                dst = ln1T[:, 1024 * it + 128 * e:1024 * it + 128 * (e + 1)]
                if t_id % 2 == 0:
                    p.step("vector", lambda v, d_=dst, b_=bank: v.tensor_copy(
                        d_, b_[0:128, 0:128]), waits=[W("t_tr2", t_id + 1)],
                        inc="v_e2")
                else:
                    p.step("scalar", lambda s, d_=dst, b_=bank: s.activation(
                        d_, b_[0:128, 0:128], AF.Identity),
                        waits=[W("t_tr2", t_id + 1)], inc="a_e2")

    n_wc = 0
    n_vp = 0
    for sb in range(8):
        for e in range(8):
            wts = []
            if e == 0:
                wts.append(W("a_sq1", 32))
                if sb >= 1:
                    wts.append(W("v_wc", n_wc))
            p.dma("sync", lambda sy, e_=e, c0=sb * 128: sy.dma_start(
                out=wstF[:, 128 * e_:128 * (e_ + 1)],
                in_=dwvT[128 * e_:128 * (e_ + 1), c0:c0 + 128]),
                waits=wts, inc="d_w")
        cast_w = [W("d_w", 16 * p.n("d_w") // 16 if False else 16 * (p.n("d_w") // 16))]
        if sb >= 1:
            cast_w.append(W("t_vp", n_vp))
        p.step("vector", lambda v: v.tensor_copy(wbf[:, :], wstF[:, :]),
               waits=cast_w, inc="v_wc")
        n_wc += 1
        for ct in range(32):
            bank = PB[n_vp % 4]
            wts = [W("v_wc", n_wc), W("t_tr2", min(256, 8 * (ct + 1)))]
            if n_vp < 4:
                wts += [W("v_e2", 128), W("a_e2", 128)]
            if n_vp >= 4:
                wts.append(W("v_vp", n_vp - 3))
            for e in range(8):
                p.step("tensor", lambda t, b_=bank, e_=e, ct_=ct: t.matmul(
                    b_[0:128, 0:128],
                    ln1T[:, 1024 * ct_ + 128 * e_:1024 * ct_ + 128 * (e_ + 1)],
                    wbf[:, 128 * e_:128 * (e_ + 1)],
                    start=(e_ == 0), stop=(e_ == 7), skip_group_check=True),
                    waits=wts if e == 0 else [],
                    inc="t_vp" if e == 7 else None)
            dst = AP(tensor=v_all, offset=1040 * ct + 65 * (2 * sb),
                     ap=[[33280, 128], [65, 2], [1, 64]])
            p.step("vector", lambda v, d_=dst, b_=bank: v.tensor_copy(
                d_, b_[0:128, 0:128]),
                waits=[W("t_vp", n_vp + 1), W("t_tr2", min(256, 8 * (ct + 3)))],
                inc="v_vp")
            n_vp += 1
    vones = AP(tensor=v_all, offset=64, ap=[[33280, 128], [1040, 32], [65, 16]])
    p.step("vector", lambda v: v.memset(vones, 1.0),
           waits=[W("t_tr2", 256), W("v_vp", 256)], inc="t_lc")

    n_kq = 0
    proj_state = {}

    def plan_proj_chunk(gi, j):
        nonlocal n_wc, n_kq
        kqb = kq0 if gi % 2 == 0 else kq1
        qgb = qg0 if gi % 2 == 0 else qg1
        wsel = 0 if j < 8 else 1
        tt = j if j < 8 else j - 8
        if (wsel == 0 and tt == 0) or (wsel == 1 and tt == 0):
            dsrc = dwkT if wsel == 0 else dwqT
            for e in range(8):
                wts = []
                if e == 0:
                    wts.append(W("v_wc", n_wc))
                p.dma("sync", lambda sy, e_=e, c0=gi * 128, ds=dsrc: sy.dma_start(
                    out=wstF[:, 128 * e_:128 * (e_ + 1)],
                    in_=ds[128 * e_:128 * (e_ + 1), c0:c0 + 128]),
                    waits=wts, inc="d_w")
            cw = [W("d_w", 16 * (p.n("d_w") // 16))]
            if n_kq >= 1 or gi >= 1 or wsel == 1:
                cw.append(W("t_kq", n_kq))
            else:
                cw.append(W("t_vp", 256))
            p.step("vector", lambda v: v.tensor_copy(wbf[:, :], wstF[:, :]),
                   waits=cw, inc="v_wc")
            n_wc += 1
        wts = [W("v_wc", n_wc), W("v_prj", p.n("v_prj"))]
        for e in range(8):
            rhs = lnT_rhs(e, tt)
            p.step("tensor", lambda t, e_=e, r_=rhs: t.matmul(
                PRJ[:, :], wbf[:, 128 * e_:128 * (e_ + 1)], r_,
                start=(e_ == 0), stop=(e_ == 7), skip_group_check=True),
                waits=wts if e == 0 else [],
                inc="t_kq" if e == 7 else None)
        dst = (kqb[:, 512 * tt:512 * (tt + 1)] if wsel == 0
               else qgb[:, 512 * tt:512 * (tt + 1)])
        if wsel == 0:
            p.step("vector", lambda v, d_=dst: v.tensor_copy(d_, PRJ[:, :]),
                   waits=[W("t_kq", n_kq + 1)], inc=[("v_kq", 1), ("v_prj", 1)])
        else:
            p.step("vector", lambda v, d_=dst: v.tensor_scalar(
                d_, PRJ[:, :], SCALING, None, ALU.mult),
                waits=[W("t_kq", n_kq + 1)], inc=[("v_kq", 1), ("v_prj", 1)])
        n_kq += 1

    def plan_proj_stage(gi, j):
        nonlocal n_wc
        wsel = 0 if j < 8 else 1
        dsrc = dwkT if wsel == 0 else dwqT
        for e in range(8):
            wts = []
            if e == 0:
                wts.append(W("v_wc", n_wc))
            p.dma("sync", lambda sy, e_=e, c0=gi * 128, ds=dsrc: sy.dma_start(
                out=wstF[:, 128 * e_:128 * (e_ + 1)],
                in_=ds[128 * e_:128 * (e_ + 1), c0:c0 + 128]),
                waits=wts, inc="d_w")
        cw = [W("d_w", 16 * (p.n("d_w") // 16)), W("t_kq", n_kq)]
        p.step("vector", lambda v: v.tensor_copy(wbf[:, :], wstF[:, :]),
               waits=cw, inc="v_wc")
        n_wc += 1

    def plan_proj_mm(gi, j, e):
        tt = j if j < 8 else j - 8
        rhs = lnT_rhs(e, tt)
        wts = [W("v_wc", n_wc), W("v_prj", p.n("v_prj"))] if e == 0 else []
        p.step("tensor", lambda t, e_=e, r_=rhs: t.matmul(
            PRJ[:, :], wbf[:, 128 * e_:128 * (e_ + 1)], r_,
            start=(e_ == 0), stop=(e_ == 7), skip_group_check=True),
            waits=wts, inc="t_kq" if e == 7 else None)

    def plan_proj_evac(gi, j):
        nonlocal n_kq
        kqb = kq0 if gi % 2 == 0 else kq1
        qgb = qg0 if gi % 2 == 0 else qg1
        wsel = 0 if j < 8 else 1
        tt = j if j < 8 else j - 8
        dst = (kqb[:, 512 * tt:512 * (tt + 1)] if wsel == 0
               else qgb[:, 512 * tt:512 * (tt + 1)])
        if wsel == 0:
            p.step("vector", lambda v, d_=dst: v.tensor_copy(d_, PRJ[:, :]),
                   waits=[W("t_kq", n_kq + 1)], inc=[("v_kq", 1), ("v_prj", 1)])
        else:
            p.step("vector", lambda v, d_=dst: v.tensor_scalar(
                d_, PRJ[:, :], SCALING, None, ALU.mult),
                waits=[W("t_kq", n_kq + 1)], inc=[("v_kq", 1), ("v_prj", 1)])
        n_kq += 1

    def make_mg(gi):
        kqb = kq0 if gi % 2 == 0 else kq1
        qgb = qg0 if gi % 2 == 0 else qg1
        items = []

        def init():
            p.step("vector", lambda v: v.memset(mgq4a[:, :], 0.0),
                   waits=[W("v_kq", 12 * (gi + 1))])
            p.step("vector", lambda v: v.memset(mgq4b[:, :], 0.0))
        items.append(init)
        chunks = ([(qgb, c, mgq4a) for c in range(4)] +
                  [(kqb, c, mgq4b) for c in range(8)])
        for (srcb, c, acc) in chunks:
            def mk(srcb=srcb, c=c, acc=acc):
                p.step("vector", lambda v: v.tensor_tensor(
                    sqs[:, :], srcb[:, 512 * c:512 * (c + 1)],
                    srcb[:, 512 * c:512 * (c + 1)], ALU.mult),
                    waits=[W("t_mq", p.n("t_mq"))], inc="v_sq", dr=True)
                p.step("tensor", lambda t: t.matmul(
                    PRJ[0:4, 0:512], IND[:, :], sqs[:, :],
                    start=True, stop=True, skip_group_check=True),
                    waits=[W("v_sq", p.n("v_sq")), W("v_prj", p.n("v_prj"))],
                    inc="t_mq")
                p.step("vector", lambda v: v.reduce_max(
                    mgq4t[:, 0:1], PRJ[0:4, 0:512], AX.X),
                    waits=[W("t_mq", p.n("t_mq"))], inc=[("v_prj", 1)], dr=True)
                p.step("vector", lambda v, a_=acc: v.tensor_tensor(
                    a_[:, 0:1], a_[:, 0:1], mgq4t[:, 0:1], ALU.max), inc="v_mq",
                    dr=True)
            items.append(mk)

        def fin1():
            p.step("vector", lambda v: v.tensor_tensor(
                mgq4a[:, 0:1], mgq4a[:, 0:1], mgq4b[:, 0:1], ALU.mult),
                inc="v_mq", dr=True)
        items.append(fin1)

        def fin2():
            p.step("tensor", lambda t: t.transpose(
                PRJ[0:1, 0:4], mgq4a[0:4, 0:1], ident_f[0:4, 0:4]),
                waits=[W("v_mq", p.n("v_mq")), W("v_prj", p.n("v_prj"))],
                inc="t_mq")
            p.step("vector", lambda v: v.tensor_copy(mg_q[0:1, 0:4],
                                                     PRJ[0:1, 0:4]),
                   waits=[W("t_mq", p.n("t_mq"))], inc=[("v_prj", 1)])
            p.step("vector", lambda v: v.reduce_max(mg_q[0:1, 12:13],
                                                    mg_q[0:1, 0:4], AX.X),
                   inc="v_mg2", dr=True)
        items.append(fin2)

        def fin3():
            p.step("scalar", lambda s: s.activation(mg_q[0:1, 12:13],
                                                    mg_q[0:1, 12:13], AF.Ln),
                   waits=[W("v_mg2", p.n("v_mg2"))])
            p.step("scalar", lambda s: (s.drain(), s.activation(
                mg_q[0:1, 12:13], mg_q[0:1, 12:13], AF.Exp, scale=0.5))[1],
                inc="a_mg")
            nw = [W("a_mg", p.n("a_mg"))]
            if gi >= 1:
                nw.append(W("t_mgb", gi))
            p.step("vector", lambda v: v.tensor_scalar(
                mg_q[0:1, 13:14], mg_q[0:1, 12:13], -15.0, 0.0,
                ALU.add, ALU.max), waits=nw, dr=True)
            p.step("vector", lambda v: v.tensor_scalar(
                lam_s[0:1, 2:3], mg_q[0:1, 13:14], -1.0, None, ALU.mult),
                inc="v_mg3", dr=True)
            p.step("tensor", lambda t: t.matmul(
                PRJ[0:128, 0:1], ones_any[0:1, :], lam_s[0:1, 2:3],
                start=True, stop=True, skip_group_check=True),
                waits=[W("v_mg3", p.n("v_mg3")), W("v_prj", p.n("v_prj")),
                       W("c_or", 1)],
                inc="t_mgb")
            p.step("vector", lambda v, gi_=gi: v.tensor_copy(
                mneg[:, gi_:gi_ + 1], PRJ[0:128, 0:1]),
                waits=[W("t_mgb", p.n("t_mgb"))], inc=[("c_mg", 1), ("v_prj", 1)])
        items.append(fin3)
        return items

    for j in range(12):
        plan_proj_chunk(0, j)
    for it in make_mg(0):
        it()
    mg_pending = []

    finst = Bbc.bitcast(BF16)[0:64, 512:1024]
    Bbc_bf = Bbc.bitcast(BF16)[0:128, 0:512]

    def plan_fin(u):
        pr, qt = u // 4, u % 4
        cols = slice(2048 * (pr // 2) + 512 * qt, 2048 * (pr // 2) + 512 * (qt + 1))
        if pr % 2 == 0:
            p.step("vector", lambda v, c_=cols: v.tensor_tensor(
                afin[0:64, c_], Abc[0:64, :], EPI[0:64, 0:512], ALU.mult),
                waits=[W("t_C", u + 1)], inc="v_fin")
        else:
            p.step("vector", lambda v: v.tensor_tensor(
                finst, Abc[0:64, :], EPI[0:64, 0:512], ALU.mult),
                waits=[W("t_C", u + 1)], inc="v_fin")
            p.dma("gpsimd", lambda gp, c_=cols: gp.dma_start(
                out=afin[64:128, c_], in_=finst),
                waits=[W("v_fin", p.n("v_fin"))], inc="d_fin")

    SQ64E = 64.0e-5 ** 0.5

    def plan_sq(pu):
        p.step("scalar", lambda s: s.activation(
            Bbc_bf[64:65, :], Abc[64:65, :], AF.Square, scale=SQ64E),
            waits=[W("v_w64", pu + 1), W("t_ms", pu)], inc="a_e")
        p.step("scalar", lambda s: s.activation(
            Bbc_bf[0:64, :], Abc[0:64, :], AF.Square),
            waits=[W("v_ab", pu + 1), W("t_ms", pu)], inc="a_e")

    def plan_AB(pu):
        p.step("tensor", lambda t: t.matmul(
            EPI[0:64, 0:512], ones_any[64:65, 0:64], o1s[64:65, :],
            start=True, stop=True, tile_position=(64, 0),
            skip_group_check=True),
            waits=[W("v_Oe", 2 * (pu + 1)), W("v_fin", p.n("v_fin")),
                   W("d_fin", 16 * (p.n("d_fin") // 16)), W("c_or", 1)])
        p.step("tensor", lambda t: t.matmul(
            EPI[64:128, 0:512], ones_any[64:65, 0:64], o0s[64:65, :],
            start=True, stop=True, tile_position=(64, 64),
            skip_group_check=True), inc="t_AB")

    def plan_ms(pu):
        p.step("tensor", lambda t: t.matmul(
            EPI[0:1, 0:512], onesb[0:65, 0:1], Bbc_bf[0:65, :],
            start=True, stop=True, skip_group_check=True),
            waits=[W("a_e", 2 * (pu + 1))], inc="t_ms")

    def plan_lnexp(pu):
        p.step("scalar", lambda s: s.activation(
            o0s[0:1, :], EPI[0:1, 0:512], AF.Ln,
            scale=1.0 / 64.0),
            waits=[W("t_ms", pu + 1)])
        p.step("scalar", lambda s: (s.drain(), s.activation(
            o1s[0:1, :], o0s[0:1, :], AF.Exp, scale=-0.5))[1],
            inc="a_c")

    def plan_C(pu):
        p.step("tensor", lambda t: t.matmul(
            EPI[0:64, 0:512], sub_r[0:1, 0:64], o1s[0:1, :],
            start=True, stop=True, skip_group_check=True),
            waits=[W("a_c", pu + 1), W("v_ab", pu + 1), W("c_lam", 2)],
            inc="t_C")

    def plan_wo_stage(oc):
        nonlocal n_wc
        for e in range(8):
            wts = [W("v_wc", n_wc)] if e == 0 else []
            p.dma("sync", lambda sy, e_=e, oc_=oc: sy.dma_start(
                out=wstF[:, 128 * e_:128 * (e_ + 1)],
                in_=dwoT[128 * e_:128 * (e_ + 1), 128 * oc_:128 * (oc_ + 1)]),
                waits=wts, inc="d_w")
        wts = [W("d_w", 16 * (p.n("d_w") // 16))]
        if oc == 0:
            wts += [W("t_kq", 96), W("t_vp", 256)]
        p.step("vector", lambda v, oc_=oc: v.tensor_copy(
            AP(tensor=wobf, offset=128 * oc_, ap=[[8192, 128], [1024, 8], [1, 128]]),
            wstF[:, :]), waits=wts, inc="v_wc")
        n_wc += 1

    for gi in range(NG):
        for pi in range(2):
            pr = gi * 2 + pi
            for qt in range(NQT):
                u = pr * 4 + qt
                ur = u - 8 * gi
                kqb = kq0 if gi % 2 == 0 else kq1
                qgb = qg0 if gi % 2 == 0 else qg1
                h0, h1 = 2 * pi, 2 * pi + 1
                def plan_S(r):
                    R = u * NKC + r
                    buf = R % 2
                    Sbuf = SA if buf == 0 else SB
                    wts = []
                    if R >= 2:
                        wts.append(W("a_P", R - 1))
                    if r == 0 and ur == 0:
                        wts.append(W("v_kq", 12 * (gi + 1)))
                    p.step("tensor", lambda t, S_=Sbuf, k_=kqb, q_=qgb,
                           h_=h0, r_=r, qt_=qt: t.matmul(
                        S_[:, 0:512],
                        k_[32 * h_:32 * (h_ + 1), 128 * r_:128 * (r_ + 1)],
                        q_[32 * h_:32 * (h_ + 1), 512 * qt_:512 * (qt_ + 1)],
                        start=True, stop=True, tile_position=(32 * h_, 0),
                        skip_group_check=True), waits=wts)
                    p.step("tensor", lambda t, S_=Sbuf, k_=kqb, q_=qgb,
                           h_=h1, r_=r, qt_=qt: t.matmul(
                        S_[:, 512:1024],
                        k_[32 * h_:32 * (h_ + 1), 128 * r_:128 * (r_ + 1)],
                        q_[32 * h_:32 * (h_ + 1), 512 * qt_:512 * (qt_ + 1)],
                        start=True, stop=True, tile_position=(32 * h_, 0),
                        skip_group_check=True), inc="t_S")
                    ew = [W("t_S", R + 1)]
                    if r == 0:
                        ew.append(W("c_mg", gi + 1))
                    pdst = (P3B[R % 3] if u <= 55
                            else P_AB[:, 1024 * buf:1024 * (buf + 1)])
                    p.step("scalar", lambda s, S_=Sbuf, d_=pdst, g_=gi: s.activation(
                        d_, S_[:, :], AF.Exp,
                        bias=mneg[:, g_:g_ + 1]), waits=ew, inc="a_P")

                def plan_V(r):
                    R = u * NKC + r
                    psrc = (P3B[R % 3] if u <= 55
                            else P_AB[:, 1024 * (R % 2):1024 * (R % 2 + 1)])
                    vw = [W("a_P", R + 1)]
                    if r == 0:
                        vw.append(W("v_Oe", 2 * u))
                        if u == 0:
                            vw.append(W("t_lc", 2))
                    p.step("tensor", lambda t, r_=r, pr_=pr, p_=psrc: t.matmul(
                        O0[0:65, :],
                        v_all[:, 1040 * r_ + 65 * pr_:1040 * r_ + 65 * pr_ + 65],
                        p_[:, 0:512],
                        start=(r_ == 0), stop=(r_ == NKC - 1),
                        skip_group_check=True), waits=vw)
                    p.step("tensor", lambda t, r_=r, pr_=pr, p_=psrc: t.matmul(
                        O1[0:65, :],
                        v_all[:, 1040 * r_ + 65 * pr_:1040 * r_ + 65 * pr_ + 65],
                        p_[:, 512:1024],
                        start=(r_ == 0), stop=(r_ == NKC - 1),
                        skip_group_check=True),
                        inc="t_V" if r == NKC - 1 else None)

                vlag = 2 if u <= 55 else 1
                for r in range(NKC):
                    plan_S(r)
                    if gi + 1 < NG and ur <= 5:
                        if r == 1 and ur in (0, 4):
                            plan_proj_stage(gi + 1, 2 * ur)
                        if 8 <= r <= 15:
                            plan_proj_mm(gi + 1, 2 * ur, r - 8)
                        if 20 <= r <= 27:
                            plan_proj_mm(gi + 1, 2 * ur + 1, r - 20)
                    if r >= vlag:
                        plan_V(r - vlag)
                    if r == NKC - 1:
                        for rv in range(NKC - vlag, NKC):
                            plan_V(rv)
                    if gi + 1 < NG:
                        if ur <= 5 and r == 16:
                            plan_proj_evac(gi + 1, 2 * ur)
                        if ur <= 5 and r == 28:
                            plan_proj_evac(gi + 1, 2 * ur + 1)
                        if ur == 6 and r == 4:
                            mg_pending.extend(make_mg(gi + 1))
                        if ur == 6 and r >= 16 and mg_pending:
                            mg_pending.pop(0)()
                    if u >= 1:
                        pu = u - 1
                        if r == 2:
                            plan_AB(pu)
                        if r == 4:
                            plan_sq(pu)
                        if r == 6:
                            plan_ms(pu)
                        if r == 8:
                            plan_lnexp(pu)
                        if r == 12:
                            plan_C(pu)
                p.step("vector", lambda v: v.tensor_copy(o0s[0:65, :], O0[0:65, :]),
                       waits=[W("t_V", u + 1), W("a_c", u), W("t_C", u),
                              W("t_AB", u)], inc="v_Oe", dr=True)
                p.step("scalar", lambda s: s.activation(
                    o1s[0:65, :], O1[0:65, :], AF.Identity),
                    waits=[W("t_V", u + 1), W("t_C", u), W("t_AB", u)],
                    inc="v_Oe")
                p.step("vector", lambda v: v.tensor_tensor(
                    Abc[64:65, :], o0s[64:65, :], o1s[64:65, :], ALU.mult),
                    waits=[W("v_Oe", 2 * (u + 1))], inc="v_w64", dr=True)
                if u >= 1:
                    plan_fin(u - 1)
                p.step("vector", lambda v: v.tensor_tensor(
                    Abc[0:64, :], o0s[0:64, :], EPI[0:64, 0:512], ALU.mult),
                    waits=[W("t_AB", u + 1)], dr=True)
                p.step("vector", lambda v: v.scalar_tensor_tensor(
                    Bbc[0:64, :], o1s[0:64, :], lamcol[0:64, 0:1],
                    EPI[64:128, 0:512], ALU.mult, ALU.mult),
                    waits=[W("c_or", 2), W("t_ms", u),
                           W("d_fin", 16 * (p.n("d_fin") // 16))])
                p.step("vector", lambda v: v.tensor_tensor(
                    Abc[0:64, :], Abc[0:64, :], Bbc[0:64, :], ALU.add),
                    inc="v_ab", dr=True)
                if u >= 56:
                    plan_wo_stage(u - 56)
    plan_AB(NU - 1)
    plan_sq(NU - 1)
    plan_ms(NU - 1)
    plan_lnexp(NU - 1)
    plan_C(NU - 1)
    plan_fin(NU - 1)

    n_wo = 0
    n_t5 = 0
    for qt in range(NQT):
        for oc in range(8):
            bank = PB[n_wo % 2]
            wts = [W("v_wc", 24 + oc + 1)]
            if n_wo == 0:
                wts += [W("a_P", NU * NKC),
                        W("v_fin", NU), W("d_fin", 16 * 32)]
            if n_wo >= 2:
                wts.append(W("v_woe", n_wo - 1))
            for cc in range(8):
                p.step("tensor", lambda t, b_=bank, cc_=cc, oc_=oc, qt_=qt: t.matmul(
                    b_[0:128, 0:512],
                    wobf[:, 1024 * cc_ + 128 * oc_:1024 * cc_ + 128 * (oc_ + 1)],
                    afin[:, 2048 * cc_ + 512 * qt_:2048 * cc_ + 512 * (qt_ + 1)],
                    start=(cc_ == 0), stop=(cc_ == 7), skip_group_check=True),
                    waits=wts if cc == 0 else [],
                    inc="t_wo" if cc == 7 else None)
            n_wo += 1
            p.step("vector", lambda v, b_=bank, oc_=oc: v.tensor_copy(
                woq[:, 512 * oc_:512 * (oc_ + 1)], b_[0:128, 0:512]),
                waits=[W("t_wo", n_wo)], inc="v_woe")
        for j in range(4):
            jg = qt * 4 + j
            xw = []
            if jg == 0:
                xw += [W("t_kq", 96), W("t_vp", 256)]
            if jg >= 2:
                xw.append(W("v_hp", 8 * (jg - 1)))
            p.dma("sync", lambda sy, jg_=jg: sy.dma_start(
                out=xs5[:, 1024 * (jg_ % 2):1024 * (jg_ % 2 + 1)],
                in_=dx[128 * jg_:128 * (jg_ + 1), :]), waits=xw,
                inc=f"d_x5{jg % 2}")
            for oc in range(8):
                bank = PB[4 + (n_t5 % 2)]
                wts = [W("v_woe", 8 * (qt + 1))]
                if n_t5 == 0:
                    wts.append(W("v_Oe", NU))
                if n_t5 >= 2:
                    wts.append(W("v_hp", p.n("v_hp") - 1))
                p.step("tensor", lambda t, b_=bank, oc_=oc, j_=j: t.transpose(
                    b_[0:128, 0:128],
                    woq[:, 512 * oc_ + 128 * j_:512 * oc_ + 128 * (j_ + 1)],
                    ident_f[:, :]), waits=wts, inc="t_tr5")
                n_t5 += 1
                hw = [W("t_tr5", n_t5), W(f"d_x5{jg % 2}", 16 * (jg // 2 + 1))]
                if oc == 0:
                    if jg == 0:
                        hw.append(W("t_V", NU))
                    if jg >= 2:
                        hw += [W(f"d_hs{jg % 2}", 16 * (jg // 2)),
                               W("v_ln2", 2 * (jg // 2))]
                p.step("vector", lambda v, b_=bank, oc_=oc, jg_=jg: v.tensor_tensor(
                    htile[:, 1024 * (jg_ % 2) + 128 * oc_:
                          1024 * (jg_ % 2) + 128 * (oc_ + 1)],
                    xs5[:, 1024 * (jg_ % 2) + 128 * oc_:
                        1024 * (jg_ % 2) + 128 * (oc_ + 1)],
                    b_[0:128, 0:128], ALU.add), waits=hw,
                    inc=[("v_hp", 1), ("v_h", 1)] if oc == 7 else "v_hp")
            p.dma("gpsimd", lambda gp, jg_=jg: gp.dma_start(
                out=d_h[128 * jg_:128 * (jg_ + 1), :],
                in_=htile[:, 1024 * (jg_ % 2):1024 * (jg_ % 2 + 1)]),
                waits=[W("v_h", jg + 1)], inc=f"d_hs{jg % 2}")
            ht = htile[:, 1024 * (jg % 2):1024 * (jg % 2 + 1)]
            pr2 = jg // 2
            b2 = 2 * (pr2 % 2)
            c2 = b2 + jg % 2
            sq_w = [W("v_h", jg + 1), W("v_wc", 32)]
            if jg % 2 == 0 and pr2 >= 2:
                sq_w.append(W("v_q85", pr2 - 1))
            p.step("scalar", lambda s, h_=ht, c_=c2: (s.drain(), s.activation(
                scr5[:, :], h_, AF.Square, accum_out=stat_q8[:, c_:c_ + 1]))[1],
                waits=sq_w, inc="a_sq5")
            p.step("vector", lambda v, h_=ht, c_=c2: v.reduce_sum(
                stat_s8[:, c_:c_ + 1], h_, AX.X),
                waits=[W("v_h", jg + 1)], dr=(jg % 2 == 0))
            if jg % 2 == 0:
                continue
            p.step("vector", lambda v, b_=b2: v.tensor_scalar(
                stat_m8[:, b_:b_ + 2], stat_s8[:, b_:b_ + 2],
                1.0 / 1024.0, None, ALU.mult), dr=True)
            p.step("vector", lambda v, b_=b2: v.tensor_scalar(
                stat_q8[:, b_:b_ + 2], stat_q8[:, b_:b_ + 2],
                1.0 / 1024.0, None, ALU.mult),
                waits=[W("a_sq5", jg + 1)], inc="v_q85")
            p.step("vector", lambda v, b_=b2: v.tensor_tensor(
                stat_r8[:, b_:b_ + 2], stat_m8[:, b_:b_ + 2],
                stat_m8[:, b_:b_ + 2], ALU.mult), dr=True)
            p.step("vector", lambda v, b_=b2: v.tensor_tensor(
                stat_r8[:, b_:b_ + 2], stat_q8[:, b_:b_ + 2],
                stat_r8[:, b_:b_ + 2], ALU.subtract), inc="v_var5", dr=True)
            sqw5 = [W("v_var5", pr2 + 1)]
            if pr2 >= 2:
                sqw5.append(W("v_ln2", 2 * (pr2 - 1)))
            p.step("scalar", lambda s, b_=b2: s.activation(
                stat_r8[:, b_:b_ + 2], stat_r8[:, b_:b_ + 2],
                AF.Sqrt, bias=epsb[:, 0:1]), waits=sqw5, inc="a_sd5")
            p.step("vector", lambda v, b_=b2: v.reciprocal(
                stat_r8[:, b_:b_ + 2], stat_r8[:, b_:b_ + 2]),
                waits=[W("a_sd5", pr2 + 1)], dr=True)
            for jj in (jg - 1, jg):
                hs = htile[:, 1024 * (jj % 2):1024 * (jj % 2 + 1)]
                p.step("vector", lambda v, h_=hs, jj_=jj, c_=b2 + jj % 2:
                       v.tensor_scalar(
                    ln2[:, 1024 * jj_:1024 * (jj_ + 1)], h_,
                    stat_m8[:, c_:c_ + 1], stat_r8[:, c_:c_ + 1],
                    ALU.subtract, ALU.mult), dr=(jj == jg - 1))
            for jj in (jg - 1, jg):
                p.step("vector", lambda v, jj_=jj: v.reduce_max(
                    amx16[:, jj_:jj_ + 1], ln2[:, 1024 * jj_:1024 * (jj_ + 1)],
                    AX.X, apply_absolute_value=True), inc="v_ln2",
                    dr=(jj == jg - 1))

    p.step("vector", lambda v: v.reduce_max(amax1[:, 0:1], amx16[:, :], AX.X),
           waits=[W("v_ln2", 16)], inc="v_am0", dr=True)
    p.step("tensor", lambda t: t.transpose(EPI[0:1, 0:128], amax1[:, 0:1],
                                           ident_f[:, :]),
           waits=[W("v_am0", 1)], inc="t_am1")
    p.step("vector", lambda v: v.tensor_copy(sc_row[0:1, :], EPI[0:1, 0:128]),
           waits=[W("t_am1", 1)])
    p.step("vector", lambda v: v.reduce_max(ccs[0:1, 0:1], sc_row[0:1, :], AX.X),
           inc="v_am1", dr=True)
    p.dma("gpsimd", lambda gp: gp.dma_start(out=d_cc[0], in_=ccs[:, :]),
          waits=[W("v_am1", 1)], inc="d_cs")
    p.step("gpsimd", lambda gp: gp.collective_compute(
        "AllReduce", ALU.max, replica_groups=[list(range(8))],
        ins=[d_cc[0]], outs=[d_cc[1]]),
        waits=[W("d_cs", 16)], inc="cc")
    p.dma("gpsimd", lambda gp: gp.dma_start(out=ccs[:, :], in_=d_cc[1]),
          waits=[W("cc", 1)], inc="d_ccr")
    p.step("vector", lambda v: v.reciprocal(sc_row[0:1, 0:1], ccs[0:1, 0:1]),
           waits=[W("d_ccr", 16)], dr=True)
    p.step("vector", lambda v: v.tensor_scalar(sc_row[0:1, 0:1], sc_row[0:1, 0:1],
                                               32767.0, None, ALU.mult), inc="v_am1",
           dr=True)
    p.step("tensor", lambda t: t.matmul(
        EPI[0:128, 0:1], ones_any[0:1, :], sc_row[0:1, 0:1],
        start=True, stop=True, skip_group_check=True),
        waits=[W("v_am1", 2), W("v_fin", NU)], inc="t_am1")
    p.step("vector", lambda v: v.tensor_copy(s2col[:, 0:1], EPI[0:128, 0:1]),
           waits=[W("t_am1", 2)], inc="g_s2")

    n_xt = 0
    for jg in range(16):
        wts = [W("g_s2", 1)]
        if jg == 0:
            wts += [W("d_hs0", 16 * 8), W("d_hs1", 16 * 8)]
        p.step("vector", lambda v, jg_=jg: v.tensor_scalar(
            xsq_i[:, :], ln2[:, 1024 * jg_:1024 * (jg_ + 1)], s2col[:, 0:1],
            None, ALU.mult), waits=wts, dr=True)
        p.step("vector", lambda v: v.tensor_copy(xsq_b[:, :], xsq_i[:, :]),
               inc="v_xq", dr=True)
        for ec in range(8):
            bank = PB[n_xt % 4].bitcast(BF16)
            wts = [W("v_xq", jg + 1)]
            if n_xt >= 4:
                wts.append(W("v_xqe", n_xt - 3))
            if n_xt == 0:
                wts.append(W("v_woe", 32))
            p.step("tensor", lambda t, b_=bank, e_=ec: t.transpose(
                b_[0:128, 0:128], xsq_b[:, 128 * e_:128 * (e_ + 1)], ident_b[:, :]),
                waits=wts, inc="t_xqt")
            p.step("vector", lambda v, b_=bank, e_=ec, jg_=jg: v.tensor_copy(
                xsqT[:, 2048 * e_ + 128 * jg_:2048 * e_ + 128 * (jg_ + 1)],
                b_[0:128, 0:128]), waits=[W("t_xqt", n_xt + 1)], inc="v_xqe")
            n_xt += 1

    for cb in range(32):
        b0 = 1024 * (cb % 2)
        for e in range(8):
            wts = []
            if e == 0:
                if cb == 0:
                    wts.append(W("v_h", 16))
                if cb >= 2:
                    wts.append(W("v_wq5d", 8 * (cb - 1)))
            p.dma("sync", lambda sy, e_=e, cb_=cb, b_=b0: sy.dma_start(
                out=wst5[:, b_ + 128 * e_:b_ + 128 * (e_ + 1)],
                in_=dwin[128 * e_:128 * (e_ + 1), 128 * cb_:128 * (cb_ + 1)]),
                waits=wts, inc=f"d_wi{cb % 2}")
        p.step("vector", lambda v, b_=b0: v.tensor_reduce(
            wsc5[:, 0:128],
            AP(tensor=wst5, offset=b_, ap=[[2048, 128], [1, 128], [128, 8]]),
            AX.X, ALU.max, apply_absolute_value=True),
            waits=[W(f"d_wi{cb % 2}", 128 * (cb // 2 + 1))],
            inc="v_wq5", dr=True)
        tw = [W("v_wq5", p.n("v_wq5"))]
        if cb == 0:
            tw.append(W("v_hp", 128))
        if cb >= 1:
            tw.append(W("v_wq5b", cb))
        p.step("tensor", lambda t: t.transpose(PB[4][0:128, 0:128], wsc5[:, 0:128],
                                               ident_f[:, :]), waits=tw, inc="t_wq5")
        p.step("vector", lambda v: v.reduce_max(sc_t[:, 0:1], PB[4][0:128, 0:128],
                                                AX.X),
               waits=[W("t_wq5", p.n("t_wq5"))], dr=True)
        p.step("vector", lambda v, cb_=cb: v.tensor_scalar(
            acolw[:, cb_:cb_ + 1], sc_t[:, 0:1], 1e-8, 1.0 / 127.0,
            ALU.max, ALU.mult),
            inc="v_wq5b" if cb % 2 == 0 else None, dr=True)
        if cb % 2 == 0:
            continue
        p.step("vector", lambda v, cb_=cb: v.reciprocal(
            sc_t2[:, 0:2], acolw[:, cb_ - 1:cb_ + 1]),
            inc="v_wq5b", dr=True)
        for tx, cbx in ((0, cb - 1), (1, cb)):
            bx0 = 1024 * (cbx % 2)
            t2w = [W("v_wq5b", cb + 1)]
            if cbx >= 1:
                t2w.append(W("v_wq5c", cbx))
            p.step("tensor", lambda t, t_=tx: t.transpose(
                PB[5][0:1, 0:128], sc_t2[:, t_:t_ + 1],
                ident_f[:, :]), waits=t2w, inc="t_wq5b")
            p.step("vector", lambda v: v.tensor_copy(sc_row[0:1, :],
                                                     PB[5][0:1, 0:128]),
                   waits=[W("t_wq5b", p.n("t_wq5b"))], inc="v_wq5c")
            gw = [W("v_wq5c", p.n("v_wq5c"))]
            if cbx >= 1:
                gw.append(W("v_wq5d", 8 * cbx))
            if cbx == 0:
                gw += [W("g_s2", 1), W("c_or", 2)]
            p.step("tensor", lambda t: t.matmul(
                EPI[0:128, 0:128], ones_any[0:1, :], sc_row[0:1, 0:128],
                start=True, stop=True, skip_group_check=True),
                waits=gw, inc="g_wq5")
            p.step("vector", lambda v: v.tensor_copy(scbb[:, :],
                                                     EPI[0:128, 0:128]),
                   waits=[W("g_wq5", cbx + 1)], dr=True)
            p.step("vector", lambda v, b_=bx0: v.tensor_tensor(
                wi32b[:, :], wst5[:, b_:b_ + 1024],
                AP(tensor=scbb, offset=0, ap=[[128, 128], [0, 8], [1, 128]]),
                ALU.mult), dr=True)
            p.step("vector", lambda v, cb_=cbx: v.tensor_copy(
                AP(tensor=win_i, offset=128 * cb_,
                   ap=[[32768, 128], [4096, 8], [1, 128]]),
                wi32b[:, :]), inc=[("v_wq5d", 8)], dr=True)

    p.step("vector", lambda v: v.reciprocal(sc_t[:, 0:1], s2col[:, 0:1]),
           waits=[W("g_s2", 1), W("v_wq5d", 256)], dr=True)
    p.step("vector", lambda v: v.tensor_scalar(gsc[:, :], acolw[:, :], sc_t[:, 0:1],
                                               None, ALU.mult), inc="v_gsc",
           dr=True)
    n_f1 = 0
    for fc in range(32):
        for qt in range(NQT):
            bank = PB[n_f1 % 4]
            wts = [W("v_wq5d", 256), W("v_xqe", 128)]
            if n_f1 >= 4:
                wts.append(W("a_gel", n_f1 - 3))
            for e in range(8):
                p.step("tensor", lambda t, b_=bank, e_=e, fc_=fc, qt_=qt: t.matmul(
                    b_[0:128, 0:512],
                    win_i[:, 4096 * e_ + 128 * fc_:4096 * e_ + 128 * (fc_ + 1)],
                    xsqT[:, 2048 * e_ + 512 * qt_:2048 * e_ + 512 * (qt_ + 1)],
                    start=(e_ == 0), stop=(e_ == 7), skip_group_check=True),
                    waits=wts if e == 0 else [],
                    inc="t_f1" if e == 7 else None)
            gw = [W("t_f1", n_f1 + 1), W("v_gsc", 1)]
            if fc >= 2 and qt == 0:
                gw.append(W(f"d_hd{fc % 2}", 16 * (fc // 2)))
            p.step("scalar", lambda s, b_=bank, fc_=fc, qt_=qt: s.activation(
                hidb[:, 2048 * (fc_ % 2) + 512 * qt_:
                     2048 * (fc_ % 2) + 512 * (qt_ + 1)],
                b_[0:128, 0:512], AF.Gelu, scale=gsc[:, fc_:fc_ + 1],
                bias=b_in[:, fc_:fc_ + 1]), waits=gw, inc="a_gel")
            n_f1 += 1
        p.step("vector", lambda v, fc_=fc: v.reduce_max(
            amx32[:, fc_:fc_ + 1], hidb[:, 2048 * (fc_ % 2):2048 * (fc_ % 2 + 1)],
            AX.X, apply_absolute_value=True), waits=[W("a_gel", 4 * (fc + 1))],
            inc="v_ha")
        p.dma("gpsimd", lambda gp, fc_=fc: gp.dma_start(
            out=d_hid[128 * fc_:128 * (fc_ + 1), :],
            in_=hidb[:, 2048 * (fc_ % 2):2048 * (fc_ % 2 + 1)]),
            waits=[W("v_ha", fc + 1)], inc=f"d_hd{fc % 2}")
        buf = wstF if fc % 2 == 0 else wstF2
        dw_w = []
        if fc == 0:
            dw_w += [W("a_sq5", 16), W("v_wc", 32)]
        if fc >= 2:
            dw_w.append(W("v_wo6", fc - 1))
        p.dma("sync", lambda sy, fc_=fc, b_=buf: sy.dma_start(
            out=b_[:, :], in_=dwout[128 * fc_:128 * (fc_ + 1), :]),
            waits=dw_w, inc=f"d_wo{fc % 2}")
        p.step("vector", lambda v, b_=buf: v.reduce_max(
            sc_t[:, 0:1], b_[:, :], AX.X, apply_absolute_value=True),
            waits=[W(f"d_wo{fc % 2}", 16 * (fc // 2 + 1))], dr=True)
        p.step("vector", lambda v, fc_=fc: v.tensor_scalar(
            rocol[:, fc_:fc_ + 1], sc_t[:, 0:1], 1e-8, 1.0 / 127.0,
            ALU.max, ALU.mult), dr=True)
        if fc % 2 == 0:
            continue
        p.step("vector", lambda v, fc_=fc: v.reciprocal(
            sc_t2[:, 0:2], rocol[:, fc_ - 1:fc_ + 1]), dr=True)
        for tx, fcx in ((0, fc - 1), (1, fc)):
            bufx = wstF if fcx % 2 == 0 else wstF2
            wq_w = []
            if fcx == 0:
                wq_w += [W("v_ln2", 16), W("d_hs0", 16 * 8), W("d_hs1", 16 * 8)]
            p.step("vector", lambda v, b_=bufx, t_=tx: v.tensor_scalar(
                wo_i32[:, :], b_[:, :], sc_t2[:, t_:t_ + 1], None, ALU.mult),
                waits=wq_w, dr=True)
            p.step("vector", lambda v, fc_=fcx: v.tensor_copy(
                wout_i[:, 1024 * fc_:1024 * (fc_ + 1)], wo_i32[:, :]),
                inc="v_wo6", dr=True)

    p.step("vector", lambda v: v.reduce_max(amax2[:, 0:1], amx32[:, :], AX.X),
           waits=[W("v_ha", 32)], inc="v_am0", dr=True)
    p.step("tensor", lambda t: t.transpose(EPI[0:1, 0:128], amax2[:, 0:1],
                                           ident_f[:, :]),
           waits=[W("v_am0", 2), W("v_am1", 1)], inc="t_am2")
    p.step("vector", lambda v: v.tensor_copy(sc_row[0:1, :], EPI[0:1, 0:128]),
           waits=[W("t_am2", 1)])
    p.step("vector", lambda v: v.reduce_max(ccs[0:1, 0:1], sc_row[0:1, :], AX.X),
           inc="v_am2", dr=True)
    p.dma("gpsimd", lambda gp: gp.dma_start(out=d_cc[2], in_=ccs[:, :]),
          waits=[W("v_am2", 1)], inc="d_cs")
    p.step("gpsimd", lambda gp: gp.collective_compute(
        "AllReduce", ALU.max, replica_groups=[list(range(8))],
        ins=[d_cc[2]], outs=[d_cc[3]]),
        waits=[W("d_cs", 32)], inc="cc")
    p.dma("gpsimd", lambda gp: gp.dma_start(out=ccs[:, :], in_=d_cc[3]),
          waits=[W("cc", 2)], inc="d_ccr")
    p.step("vector", lambda v: v.reciprocal(sc_row[0:1, 1:2], ccs[0:1, 0:1]),
           waits=[W("d_ccr", 32)], dr=True)
    p.step("vector", lambda v: v.tensor_scalar(sc_row[0:1, 1:2], sc_row[0:1, 1:2],
                                               32767.0, None, ALU.mult), inc="v_s3",
           dr=True)
    p.step("tensor", lambda t: t.matmul(
        EPI[0:128, 0:1], ones_any[0:1, :], sc_row[0:1, 1:2],
        start=True, stop=True, skip_group_check=True),
        waits=[W("v_s3", 1), W("v_am2", 1)], inc="t_am2")
    p.step("vector", lambda v: v.tensor_copy(s3col[:, 0:1], EPI[0:128, 0:1]),
           waits=[W("t_am2", 2)], inc="g_s3")
    p.step("vector", lambda v: v.reciprocal(sc_t[:, 0:1], s3col[:, 0:1]),
           waits=[W("g_s3", 1), W("v_wo6", 32)], dr=True)
    p.step("vector", lambda v: v.tensor_scalar(invh[:, :], rocol[:, :], sc_t[:, 0:1],
                                               None, ALU.mult), inc="v_gsc",
           dr=True)

    n_t7 = 0
    for qt in range(NQT):
        for fc in range(32):
            k = qt * 32 + fc
            wts = []
            if k == 0:
                wts += [W("d_hd0", 16 * 16), W("d_hd1", 16 * 16),
                        W("d_hs0", 16 * 8), W("d_hs1", 16 * 8),
                        W("g_s3", 1)]
            if k >= 2:
                wts.append(W("v_f2e", k - 1))
            p.dma("sync", lambda sy, fc_=fc, qt_=qt, k_=k: sy.dma_start(
                out=hraw7[:, 512 * (k_ % 2):512 * (k_ % 2 + 1)],
                in_=d_hid[128 * fc_:128 * (fc_ + 1), 512 * qt_:512 * (qt_ + 1)]),
                waits=wts, inc=f"d_h7{k % 2}")
            qw = [W(f"d_h7{k % 2}", 16 * (k // 2 + 1)), W("v_gsc", 2)]
            if k >= 2:
                qw.append(W("v_f2e", k - 1))
            hqb = hq_i32 if k % 2 == 0 else hq7b
            p.step("vector", lambda v, k_=k, h_=hqb: v.tensor_scalar(
                h_[:, :], hraw7[:, 512 * (k_ % 2):512 * (k_ % 2 + 1)],
                s3col[:, 0:1], None, ALU.mult), waits=qw, inc="v_q7",
                dr=(k == 0))
            a2w = [W("v_q7", p.n("v_q7"))]
            if k >= 2:
                a2w.append(W("t_f2", 8 * (k - 1)))
            p.step("scalar", lambda s, k_=k, fc_=fc, h_=hqb: s.activation(
                rhs7[:, 512 * (k_ % 2):512 * (k_ % 2 + 1)], h_[:, :],
                AF.Identity, scale=invh[:, fc_:fc_ + 1]),
                waits=a2w, inc="v_f2e")
            mw = [W("v_f2e", k + 1)]
            if fc == 0:
                if qt == 0:
                    mw += [W("a_gel", 128), W("v_am2", 1), W("v_wq5c", 32)]
                else:
                    mw.append(W("v_f2o", 8 * qt))
            for oc in range(8):
                p.step("tensor", lambda t, oc_=oc, fc_=fc, k_=k: t.matmul(
                    PB[oc_][0:128, 0:512],
                    wout_i[:, 1024 * fc_ + 128 * oc_:1024 * fc_ + 128 * (oc_ + 1)],
                    rhs7[:, 512 * (k_ % 2):512 * (k_ % 2 + 1)],
                    start=(fc_ == 0), stop=(fc_ == 31), skip_group_check=True),
                    waits=mw if oc == 0 else [], inc="t_f2")
        for oc in range(8):
            ew = [W("t_f2", 8 * 32 * (qt + 1))]
            if qt == 0 and oc == 0:
                ew.append(W("t_f1", 128))
            p.step("vector", lambda v, oc_=oc: v.tensor_scalar(
                outTs[:, 512 * oc_:512 * (oc_ + 1)], PB[oc_][0:128, 0:512],
                b_out[:, oc_:oc_ + 1], None, ALU.add), waits=ew, inc="v_f2o")
        for j in range(4):
            jg = qt * 4 + j
            hw = []
            if jg >= 2:
                hw.append(W("v_out", 8 * (jg - 1)))
            p.dma("sync", lambda sy, jg_=jg: sy.dma_start(
                out=hin7[:, 1024 * (jg_ % 2):1024 * (jg_ % 2 + 1)],
                in_=d_h[128 * jg_:128 * (jg_ + 1), :]), waits=hw,
                inc=f"d_hn{jg % 2}")
            for oc in range(8):
                bank = PB[n_t7 % 2]
                wts = [W("v_f2o", 8 * (qt + 1))]
                if n_t7 >= 2:
                    wts.append(W("v_out", n_t7 - 1))
                p.step("tensor", lambda t, b_=bank, oc_=oc, j_=j: t.transpose(
                    b_[0:128, 0:128],
                    outTs[:, 512 * oc_ + 128 * j_:512 * oc_ + 128 * (j_ + 1)],
                    ident_f[:, :]), waits=wts, inc="t_tr7")
                n_t7 += 1
                ow = [W("t_tr7", n_t7), W(f"d_hn{jg % 2}", 16 * (jg // 2 + 1))]
                if oc == 0 and jg >= 2:
                    ow.append(W(f"d_o{jg % 2}", 16 * (jg // 2)))
                p.step("vector", lambda v, b_=bank, oc_=oc, jg_=jg: v.tensor_tensor(
                    outs7[:, 1024 * (jg_ % 2) + 128 * oc_:
                          1024 * (jg_ % 2) + 128 * (oc_ + 1)],
                    hin7[:, 1024 * (jg_ % 2) + 128 * oc_:
                         1024 * (jg_ % 2) + 128 * (oc_ + 1)],
                    b_[0:128, 0:128], ALU.add), waits=ow, inc="v_out")
            p.dma("gpsimd", lambda gp, jg_=jg: gp.dma_start(
                out=dout[128 * jg_:128 * (jg_ + 1), :],
                in_=outs7[:, 1024 * (jg_ % 2):1024 * (jg_ % 2 + 1)]),
                waits=[W("v_out", 8 * (jg + 1))], inc=f"d_o{jg % 2}")
    p.step("gpsimd", lambda gp: (gp.wait_ge(p.sem("d_o0"), 16 * 8),
                                 gp.wait_ge(p.sem("d_o1"), 16 * 8)) and None)

    with nc.Block() as block:
        @block.sync
        def _(sy):
            p.emit("sync", sy)

        @block.gpsimd
        def _(gp):
            p.emit("gpsimd", gp)

        @block.scalar
        def _(sc):
            p.emit("scalar", sc)

        @block.tensor
        def _(te):
            p.emit("tensor", te)

        @block.vector
        def _(ve):
            p.emit("vector", ve)

    return nc


_NC_CACHE = None


def kernel(L0_attn_query, L0_attn_Wq, L0_attn_Wk, L0_attn_Wv, L0_attn_Wo,
           L0_attn_lambda_q1, L0_attn_lambda_k1, L0_attn_lambda_q2,
           L0_attn_lambda_k2, L0_attn_subln_w, L0_ffn_W_in, L0_ffn_b_in,
           L0_ffn_W_out, L0_ffn_b_out):
    global _NC_CACHE
    f32 = np.float32
    x = np.asarray(L0_attn_query, f32)
    wqT = np.ascontiguousarray(np.asarray(L0_attn_Wq, f32).T)
    wkT = np.ascontiguousarray(np.asarray(L0_attn_Wk, f32).T)
    wvT = np.ascontiguousarray(np.asarray(L0_attn_Wv, f32).T)
    woT = np.ascontiguousarray(np.asarray(L0_attn_Wo, f32).T)
    win = np.ascontiguousarray(np.asarray(L0_ffn_W_in, f32))
    wout = np.ascontiguousarray(np.asarray(L0_ffn_W_out, f32))
    b_in_pp = np.ascontiguousarray(np.asarray(L0_ffn_b_in, f32).reshape(32, 128).T)
    b_out_pp = np.ascontiguousarray(np.asarray(L0_ffn_b_out, f32).reshape(8, 128).T)
    lam_r = np.ascontiguousarray(np.concatenate(
        [np.asarray(a, f32).ravel() for a in
         (L0_attn_lambda_q1, L0_attn_lambda_k1,
          L0_attn_lambda_q2, L0_attn_lambda_k2)])[None, :])
    subln_r = np.ascontiguousarray(np.asarray(L0_attn_subln_w, f32)[None, :])
    ident_bf = np.eye(128, dtype=ml_dtypes.bfloat16)
    ident_f = np.eye(128, dtype=f32)

    if _NC_CACHE is None:
        _NC_CACHE = build_nc()
    nc = _NC_CACHE

    in_maps = []
    for c in range(8):
        b, half = c // 2, c % 2
        xb = x[b]
        xp = np.ascontiguousarray(np.concatenate(
            [xb[TQ * half:TQ * (half + 1)], xb[TQ * (1 - half):TQ * (2 - half)]],
            axis=0))
        in_maps.append({
            "x": xp, "wqT": wqT, "wkT": wkT, "wvT": wvT, "woT": woT,
            "win": win, "wout": wout, "b_in_pp": b_in_pp, "b_out_pp": b_out_pp,
            "lam_r": lam_r, "subln_r": subln_r,
            "ident_bf": ident_bf, "ident_f": ident_f,
        })
    res = run_bass_kernel_spmd(nc, in_maps, core_ids=list(range(8)))
    out = np.empty((B, T, D), f32)
    for c in range(8):
        b, half = c // 2, c % 2
        out[b, TQ * half:TQ * (half + 1)] = res.results[c]["out"]
    return out


# version 19
# speedup vs baseline: 1.0035x; 1.0035x over previous
import math
import numpy as np
import ml_dtypes
import concourse.bass as bass
import concourse.dve_ops as _dve_ops
import concourse.mybir as mybir
from concourse.bass_types import AP
from concourse.bass_utils import run_bass_kernel_spmd

F32 = mybir.dt.float32
BF16 = mybir.dt.bfloat16
I32 = mybir.dt.int32
AF = mybir.ActivationFunctionType
ALU = mybir.AluOpType
AX = mybir.AxisListType

B, T, D = 4, 4096, 1024
TQ = 2048
DFF = 4096
NG = 8
NKC = 32
NQT = 4
NU = 64
LAMBDA_INIT = float(0.8 - 0.6 * math.exp(-0.3 * 12))
SCALING = float(32 ** -0.5)

BASE = 16512
OFF_A = BASE
OFF_B = OFF_A + 66560
OFF_C = OFF_B + 65536
OFF_D = OFF_C + 32768
OFF_E = OFF_D + 24576
OFF_F = OFF_E + 4096
OFF_G = OFF_F + 8192


class Plan:
    def __init__(self, nc):
        self.nc = nc
        self.steps = {k: [] for k in ("sync", "gpsimd", "scalar", "tensor", "vector")}
        self.cnt = {}
        self.sems = {}

    def sem(self, name):
        if name not in self.sems:
            self.sems[name] = self.nc.alloc_semaphore(name)
            self.cnt[name] = 0
        return self.sems[name]

    def n(self, name):
        self.sem(name)
        return self.cnt[name]

    def step(self, eng, fn, waits=(), inc=None, inc_n=1, dr=False):
        ws = [(self.sem(s), v) for (s, v) in waits if v > 0]
        if inc is None:
            incs = []
        elif isinstance(inc, str):
            incs = [(inc, inc_n)]
        else:
            incs = list(inc)
        self.steps[eng].append((fn, ws, [(self.sem(s), n) for s, n in incs], dr))
        for s, n in incs:
            self.cnt[s] += n

    def dma(self, eng, fn, waits=(), inc=None):
        self.step(eng, fn, waits, inc, 16)

    def emit(self, name, eng):
        for fn, ws, incs, dr in self.steps[name]:
            for s, v in ws:
                eng.wait_ge(s, v)
            if dr:
                eng.drain()
            ins = fn(eng)
            if incs:
                ins.then_inc(*incs[0])
                for s, n in incs[1:]:
                    eng.drain().then_inc(s, n)


def build_nc():
    nc = bass.Bass("TRN2", target_bir_lowering=False, monotonic_sem_count=0)

    dx = nc.dram_tensor("x", [T, D], F32, kind="ExternalInput").ap()
    dwqT = nc.dram_tensor("wqT", [D, D], F32, kind="ExternalInput").ap()
    dwkT = nc.dram_tensor("wkT", [D, D], F32, kind="ExternalInput").ap()
    dwvT = nc.dram_tensor("wvT", [D, D], F32, kind="ExternalInput").ap()
    dwoT = nc.dram_tensor("woT", [D, D], F32, kind="ExternalInput").ap()
    dwin = nc.dram_tensor("win", [D, DFF], F32, kind="ExternalInput").ap()
    dwout = nc.dram_tensor("wout", [DFF, D], F32, kind="ExternalInput").ap()
    dbin = nc.dram_tensor("b_in_pp", [128, 32], F32, kind="ExternalInput").ap()
    dbout = nc.dram_tensor("b_out_pp", [128, 8], F32, kind="ExternalInput").ap()
    dlam = nc.dram_tensor("lam_r", [1, 128], F32, kind="ExternalInput").ap()
    dsub = nc.dram_tensor("subln_r", [1, 64], F32, kind="ExternalInput").ap()
    dib = nc.dram_tensor("ident_bf", [128, 128], BF16, kind="ExternalInput").ap()
    dif = nc.dram_tensor("ident_f", [128, 128], F32, kind="ExternalInput").ap()
    dout = nc.dram_tensor("out", [TQ, D], F32, kind="ExternalOutput").ap()
    d_h = nc.dram_tensor("sp_h", [TQ, D], F32).ap()
    d_hid = nc.dram_tensor("sp_hid", [DFF, TQ], BF16).ap()
    d_cc = [nc.dram_tensor(f"cc{i}", [1, 8], F32).ap() for i in range(4)]

    def at(name, shape, dtype, off):
        return nc.alloc_sbuf_tensor_at(name, shape, dtype, offset=off)

    ln1 = at("ln1", [128, 32768], BF16, OFF_A)
    v_all = at("v_all", [128, 33280], BF16, OFF_A)
    ln2 = at("ln2", [128, 16384], F32, OFF_A)
    win_i = at("win_i", [128, 32768], BF16, OFF_A)
    ln1T = at("ln1T", [128, 32768], BF16, OFF_B)
    wobf = at("wobf", [128, 8192], BF16, OFF_B)
    wout_i = at("wout_i", [128, 32768], BF16, OFF_B)
    xs5 = at("xs5", [128, 4096], F32, OFF_B + 16384)
    xC = at("xC", [128, 4096], F32, OFF_C)
    afin = at("afin", [128, 16384], BF16, OFF_C)
    xsqT = at("xsqT", [128, 16384], BF16, OFF_C)
    outTs = at("outTs", [128, 4096], F32, OFF_C)
    kq0 = at("kq0", [128, 4096], BF16, OFF_D)
    kq1 = at("kq1", [128, 4096], BF16, OFF_D + 8192)
    qg0 = at("qg0", [128, 2048], BF16, OFF_D + 16384)
    qg1 = at("qg1", [128, 2048], BF16, OFF_D + 20480)
    woq = at("woq", [128, 4096], F32, OFF_D)
    htile = at("htile", [128, 2048], F32, OFF_D + 16384)
    wst5 = at("wst5", [128, 2048], F32, OFF_D)
    wsc5 = at("wsc5", [128, 256], F32, OFF_D + 8192)
    wi32 = at("wi32", [128, 128], I32, OFF_D + 9216)
    hidb = at("hidb", [128, 8192], BF16, OFF_D)
    wo_i32 = at("wo_i32", [128, 1024], I32, OFF_D + 16384)
    hin7 = at("hin7", [128, 4096], F32, OFF_D)
    outs7 = at("outs7", [128, 4096], F32, OFF_D + 8192)
    rhs7 = at("rhs7", [128, 1024], BF16, OFF_D + 16384)
    xsq_b = at("xsq_b", [128, 1024], BF16, OFF_D + 18432)
    xsq_i = at("xsq_i", [128, 1024], I32, OFF_D + 20480)
    P_AB = at("P_AB", [128, 2048], BF16, OFF_E)
    sc_row = at("sc_row", [1, 128], F32, OFF_E)
    hraw7 = at("hraw7", [128, 1024], BF16, OFF_E)
    hq_i32 = at("hq_i32", [128, 512], I32, OFF_E + 2048)
    wstF = at("wstF", [128, 1024], F32, OFF_F)
    wbf = at("wbf", [128, 1024], BF16, OFF_F + 4096)
    lamst = at("lamst", [1, 128], F32, OFF_F + 6144)
    sqs = at("sqs", [128, 512], BF16, OFF_F + 6144)
    scSC = at("scSC", [128, 128], F32, OFF_D + 9728)
    scr5 = at("scr5", [128, 1024], F32, OFF_F)
    wstF2 = at("wstF2", [128, 1024], F32, OFF_F + 4096)
    g = OFF_G
    def G(name, shape, dtype):
        nonlocal g
        t = at(name, shape, dtype, g)
        g += ((shape[1] * mybir.dt.size(dtype) + 31) // 32) * 32
        return t
    ident_b = G("ident_b", [128, 128], BF16)
    ident_f = G("ident_f", [128, 128], F32)
    ones64 = G("ones64", [64, 1], F32)
    lam_s = G("lam_s", [1, 8], F32)
    sub_r = G("sub_r", [1, 64], F32)
    ones_any = G("ones_any", [128, 128], F32)
    mneg = G("mneg", [128, 8], F32)

    IND = G("IND", [128, 4], BF16)
    mgq4a = G("mgq4a", [4, 1], F32)
    mgq4b = G("mgq4b", [4, 1], F32)
    mgq4t = G("mgq4t", [4, 1], F32)
    o0s = G("o0s", [128, 512], F32)
    o1s = G("o1s", [128, 512], F32)
    Abc = G("Abc", [128, 512], F32)
    Bbc = G("Bbc", [128, 512], F32)
    lamcol = G("lamcol", [128, 1], F32)
    epsb = G("epsb", [128, 1], F32)
    b_in = G("b_in", [128, 32], F32)
    b_out = G("b_out", [128, 8], F32)
    amax1 = G("amax1", [128, 1], F32)
    amax2 = amax1
    amx32 = G("amx32", [128, 32], F32)
    amx16 = amx32[0:128, 0:16]
    mg_q = amx32[0:1, 0:16]
    ccs = G("ccs", [1, 8], F32)
    s2col = G("s2col", [128, 1], F32)
    s3col = G("s3col", [128, 1], F32)
    gsc = G("gsc", [128, 32], F32)
    invh = G("invh", [128, 32], F32)
    acolw = G("acolw", [128, 32], F32)
    rocol = G("rocol", [128, 32], F32)
    sc_t = G("sc_t", [128, 1], F32)
    sc_t2 = G("sc_t2", [128, 2], F32)
    onesb = G("onesb", [128, 1], BF16)
    wi32b = at("wi32b", [128, 1024], I32, OFF_D + 10240)
    scbb = at("scbb", [128, 128], F32, OFF_D + 14336)
    hq7b = at("hq7b", [128, 512], I32, OFF_F)
    stat_s8 = amx32[0:128, 16:24]

    stat_m8 = invh[0:128, 0:8]
    stat_q8 = invh[0:128, 8:16]
    stat_r8 = invh[0:128, 16:24]
    assert g <= 229344, f"SBUF overflow {g}"

    SA = nc.alloc_psum_tensor("SA", [128, 1024], F32)
    SB = nc.alloc_psum_tensor("SB", [128, 1024], F32)
    O0 = nc.alloc_psum_tensor("O0", [128, 512], F32)
    O1 = nc.alloc_psum_tensor("O1", [128, 512], F32)
    EPI = nc.alloc_psum_tensor("EPI", [128, 512], F32)
    PRJ = nc.alloc_psum_tensor("PRJ", [128, 512], F32)
    PB = [SA[:, 0:512], SA[:, 512:1024], SB[:, 0:512], SB[:, 512:1024],
          O0[:, :], O1[:, :], EPI[:, :], PRJ[:, :]]

    P3B = [P_AB[:, 0:1024], P_AB[:, 1024:2048], afin[:, 15360:16384]]

    p = Plan(nc)
    W = lambda s, v: (s, v)

    def lnT_rhs(e, tt):
        return AP(tensor=ln1T, offset=1024 * (4 * tt) + 128 * e,
                  ap=[[32768, 128], [1024, 4], [1, 128]])

    for dst, src in ((ident_b[:, :], dib), (ident_f[:, :], dif),
                     (lamst[:, :], dlam), (sub_r[:, :], dsub),
                     (b_in[:, :], dbin), (b_out[:, :], dbout)):
        p.dma("sync", lambda sy, d=dst, s=src: sy.dma_start(out=d, in_=s), inc="d_c")
    p.step("vector", lambda v: v.memset(ones64[:, :], 1.0))
    p.step("vector", lambda v: v.memset(epsb[:, :], 1e-5))
    p.step("vector", lambda v: v.memset(amax1[:, :], 0.0))
    p.step("vector", lambda v: v.memset(ccs[:, :], 0.0))
    p.step("vector", lambda v: v.memset(IND[:, :], 0.0))
    for h in range(4):
        p.step("vector", lambda v, h_=h: v.memset(IND[32 * h_:32 * (h_ + 1),
                                                      h_:h_ + 1], 1.0), dr=True)
    p.step("vector", lambda v: v.tensor_tensor(mg_q[0:1, 0:8], lamst[0:1, 0:8],
                                               lamst[0:1, 0:8], ALU.mult),
           waits=[W("d_c", 16 * 6)])
    p.step("vector", lambda v: v.tensor_tensor(mg_q[0:1, 0:8], lamst[0:1, 0:32],
                                               lamst[0:1, 32:64], ALU.mult) if False
           else v.tensor_tensor(sc_row[0:1, 0:32], lamst[0:1, 0:32],
                                lamst[0:1, 32:64], ALU.mult))
    p.step("vector", lambda v: v.reduce_sum(lam_s[0:1, 4:5], sc_row[0:1, 0:32], AX.X), dr=True)
    p.step("vector", lambda v: v.tensor_tensor(sc_row[0:1, 0:32], lamst[0:1, 64:96],
                                               lamst[0:1, 96:128], ALU.mult), dr=True)
    p.step("vector", lambda v: v.reduce_sum(lam_s[0:1, 5:6], sc_row[0:1, 0:32], AX.X),
           inc="c_dots", dr=True)
    p.step("scalar", lambda s: s.activation(lam_s[0:1, 4:5], lam_s[0:1, 4:5], AF.Exp),
           waits=[W("c_dots", 1)])
    p.step("scalar", lambda s: (s.drain(), s.activation(
        lam_s[0:1, 5:6], lam_s[0:1, 5:6], AF.Exp))[1], inc="c_dots")
    p.step("vector", lambda v: v.tensor_tensor(lam_s[0:1, 0:1], lam_s[0:1, 4:5],
                                               lam_s[0:1, 5:6], ALU.subtract),
           waits=[W("c_dots", 2)])
    p.step("vector", lambda v: v.tensor_scalar(lam_s[0:1, 0:1], lam_s[0:1, 0:1],
                                               LAMBDA_INIT, None, ALU.add), dr=True)
    p.step("vector", lambda v: v.tensor_scalar(lam_s[0:1, 1:2], lam_s[0:1, 0:1],
                                               -1.0, None, ALU.mult), inc="c_lam", dr=True)
    p.step("vector", lambda v: v.tensor_scalar(sub_r[0:1, :], sub_r[0:1, :],
                                               1.0 - LAMBDA_INIT, None, ALU.mult),
           inc="c_lam")
    p.step("vector", lambda v: v.memset(ones_any[:, :], 1.0), inc="c_or")
    p.step("vector", lambda v: v.memset(onesb[:, :], 1.0))
    p.step("tensor", lambda t: t.matmul(
        PRJ[0:128, 0:1], ones_any[0:1, :], lam_s[0:1, 1:2],
        start=True, stop=True, skip_group_check=True),
        waits=[W("c_lam", 1), W("c_or", 1)], inc="t_lc")
    p.step("vector", lambda v: v.tensor_copy(lamcol[:, 0:1], PRJ[0:128, 0:1]),
           waits=[W("t_lc", 1)], inc=[("c_or", 1), ("v_prj", 1)])

    for i in range(32):
        g_, j_ = i // 4, i % 4
        b4 = 4 * (g_ % 2)
        xt = xC[:, 1024 * j_:1024 * (j_ + 1)]
        lt = ln1[:, 1024 * i:1024 * (i + 1)]
        wts = [W("v_ln", i - 3), W("a_sq1", i - 3)] if i >= 4 else []
        xsem = f"d_x{i % 4}"
        p.dma("sync", lambda sy, d=xt, r0=128 * i: sy.dma_start(
            out=d, in_=dx[r0:r0 + 128, :]), waits=wts, inc=xsem)
        sq_w = [W(xsem, 16 * (g_ + 1))]
        if j_ == 0 and g_ >= 2:
            sq_w.append(W("v_q8", g_ - 1))
        p.step("scalar", lambda s, x_=xt, c_=b4 + j_: (s.drain(), s.activation(
            scr5[:, :], x_, AF.Square, accum_out=stat_q8[:, c_:c_ + 1]))[1],
            waits=sq_w, inc="a_sq1")
        p.step("vector", lambda v, x_=xt, c_=b4 + j_: v.reduce_sum(
            stat_s8[:, c_:c_ + 1], x_, AX.X),
            waits=[W(xsem, 16 * (g_ + 1))], dr=(j_ == 0))
        if j_ < 3:
            continue
        p.step("vector", lambda v, b_=b4: v.tensor_scalar(
            stat_m8[:, b_:b_ + 4], stat_s8[:, b_:b_ + 4],
            1.0 / 1024.0, None, ALU.mult), dr=True)
        p.step("vector", lambda v, b_=b4: v.tensor_scalar(
            stat_q8[:, b_:b_ + 4], stat_q8[:, b_:b_ + 4],
            1.0 / 1024.0, None, ALU.mult),
            waits=[W("a_sq1", 4 * (g_ + 1))], inc="v_q8")
        p.step("vector", lambda v, b_=b4: v.tensor_tensor(
            stat_r8[:, b_:b_ + 4], stat_m8[:, b_:b_ + 4],
            stat_m8[:, b_:b_ + 4], ALU.mult), dr=True)
        p.step("vector", lambda v, b_=b4: v.tensor_tensor(
            stat_r8[:, b_:b_ + 4], stat_q8[:, b_:b_ + 4],
            stat_r8[:, b_:b_ + 4], ALU.subtract), inc="v_var", dr=True)
        sqw = [W("v_var", g_ + 1)]
        if g_ >= 2:
            sqw.append(W("v_ln", 4 * (g_ - 1)))
        p.step("scalar", lambda s, b_=b4: s.activation(
            stat_r8[:, b_:b_ + 4], stat_r8[:, b_:b_ + 4],
            AF.Sqrt, bias=epsb[:, 0:1]), waits=sqw, inc="a_sd")
        p.step("vector", lambda v, b_=b4: v.reciprocal(
            stat_r8[:, b_:b_ + 4], stat_r8[:, b_:b_ + 4]),
            waits=[W("a_sd", g_ + 1)], dr=True)
        for jj in range(4):
            ii = 4 * g_ + jj
            p.step("vector", lambda v, jj_=jj, ii_=ii, c_=b4 + jj: v.tensor_scalar(
                ln1[:, 1024 * ii_:1024 * (ii_ + 1)],
                xC[:, 1024 * jj_:1024 * (jj_ + 1)],
                stat_m8[:, c_:c_ + 1], stat_r8[:, c_:c_ + 1],
                ALU.subtract, ALU.mult), inc="v_ln", dr=(jj == 0))
        for it in range(4 * g_, 4 * g_ + 4):
            for e in range(8):
                t_id = it * 8 + e
                bank = PB[t_id % 8].bitcast(BF16)
                wts = [W("v_ln", it + 1)]
                if t_id >= 8:
                    prev = t_id - 8
                    wts.append(W("v_e2" if prev % 2 == 0 else "a_e2",
                                 prev // 2 + 1))
                p.step("tensor", lambda t, b_=bank, i_=it, e_=e: t.transpose(
                    b_[0:128, 0:128],
                    ln1[:, 1024 * i_ + 128 * e_:1024 * i_ + 128 * (e_ + 1)],
                    ident_b[:, :]), waits=wts, inc="t_tr2")
                dst = ln1T[:, 1024 * it + 128 * e:1024 * it + 128 * (e + 1)]
                if t_id % 2 == 0:
                    p.step("vector", lambda v, d_=dst, b_=bank: v.tensor_copy(
                        d_, b_[0:128, 0:128]), waits=[W("t_tr2", t_id + 1)],
                        inc="v_e2")
                else:
                    p.step("scalar", lambda s, d_=dst, b_=bank: s.activation(
                        d_, b_[0:128, 0:128], AF.Identity),
                        waits=[W("t_tr2", t_id + 1)], inc="a_e2")

    n_wc = 0
    n_vp = 0
    for sb in range(8):
        for e in range(8):
            wts = []
            if e == 0:
                wts.append(W("a_sq1", 32))
                if sb >= 1:
                    wts.append(W("v_wc", n_wc))
            p.dma("sync", lambda sy, e_=e, c0=sb * 128: sy.dma_start(
                out=wstF[:, 128 * e_:128 * (e_ + 1)],
                in_=dwvT[128 * e_:128 * (e_ + 1), c0:c0 + 128]),
                waits=wts, inc="d_w")
        cast_w = [W("d_w", 16 * p.n("d_w") // 16 if False else 16 * (p.n("d_w") // 16))]
        if sb >= 1:
            cast_w.append(W("t_vp", n_vp))
        p.step("vector", lambda v: v.tensor_copy(wbf[:, :], wstF[:, :]),
               waits=cast_w, inc="v_wc")
        n_wc += 1
        for ct in range(32):
            bank = PB[n_vp % 4]
            wts = [W("v_wc", n_wc), W("t_tr2", min(256, 8 * (ct + 1)))]
            if n_vp < 4:
                wts += [W("v_e2", 128), W("a_e2", 128)]
            if n_vp >= 4:
                wts.append(W("v_vp", n_vp - 3))
            for e in range(8):
                p.step("tensor", lambda t, b_=bank, e_=e, ct_=ct: t.matmul(
                    b_[0:128, 0:128],
                    ln1T[:, 1024 * ct_ + 128 * e_:1024 * ct_ + 128 * (e_ + 1)],
                    wbf[:, 128 * e_:128 * (e_ + 1)],
                    start=(e_ == 0), stop=(e_ == 7), skip_group_check=True),
                    waits=wts if e == 0 else [],
                    inc="t_vp" if e == 7 else None)
            dst = AP(tensor=v_all, offset=1040 * ct + 65 * (2 * sb),
                     ap=[[33280, 128], [65, 2], [1, 64]])
            p.step("vector", lambda v, d_=dst, b_=bank: v.tensor_copy(
                d_, b_[0:128, 0:128]),
                waits=[W("t_vp", n_vp + 1), W("t_tr2", min(256, 8 * (ct + 3)))],
                inc="v_vp")
            n_vp += 1
    vones = AP(tensor=v_all, offset=64, ap=[[33280, 128], [1040, 32], [65, 16]])
    p.step("vector", lambda v: v.memset(vones, 1.0),
           waits=[W("t_tr2", 256), W("v_vp", 256)], inc="t_lc")

    n_kq = 0
    proj_state = {}

    def plan_proj_chunk(gi, j):
        nonlocal n_wc, n_kq
        kqb = kq0 if gi % 2 == 0 else kq1
        qgb = qg0 if gi % 2 == 0 else qg1
        wsel = 0 if j < 8 else 1
        tt = j if j < 8 else j - 8
        if (wsel == 0 and tt == 0) or (wsel == 1 and tt == 0):
            dsrc = dwkT if wsel == 0 else dwqT
            for e in range(8):
                wts = []
                if e == 0:
                    wts.append(W("v_wc", n_wc))
                p.dma("sync", lambda sy, e_=e, c0=gi * 128, ds=dsrc: sy.dma_start(
                    out=wstF[:, 128 * e_:128 * (e_ + 1)],
                    in_=ds[128 * e_:128 * (e_ + 1), c0:c0 + 128]),
                    waits=wts, inc="d_w")
            cw = [W("d_w", 16 * (p.n("d_w") // 16))]
            if n_kq >= 1 or gi >= 1 or wsel == 1:
                cw.append(W("t_kq", n_kq))
            else:
                cw.append(W("t_vp", 256))
            p.step("vector", lambda v: v.tensor_copy(wbf[:, :], wstF[:, :]),
                   waits=cw, inc="v_wc")
            n_wc += 1
        wts = [W("v_wc", n_wc), W("v_prj", p.n("v_prj"))]
        for e in range(8):
            rhs = lnT_rhs(e, tt)
            p.step("tensor", lambda t, e_=e, r_=rhs: t.matmul(
                PRJ[:, :], wbf[:, 128 * e_:128 * (e_ + 1)], r_,
                start=(e_ == 0), stop=(e_ == 7), skip_group_check=True),
                waits=wts if e == 0 else [],
                inc="t_kq" if e == 7 else None)
        dst = (kqb[:, 512 * tt:512 * (tt + 1)] if wsel == 0
               else qgb[:, 512 * tt:512 * (tt + 1)])
        if wsel == 0:
            p.step("vector", lambda v, d_=dst: v.tensor_copy(d_, PRJ[:, :]),
                   waits=[W("t_kq", n_kq + 1)], inc=[("v_kq", 1), ("v_prj", 1)])
        else:
            p.step("vector", lambda v, d_=dst: v.tensor_scalar(
                d_, PRJ[:, :], SCALING, None, ALU.mult),
                waits=[W("t_kq", n_kq + 1)], inc=[("v_kq", 1), ("v_prj", 1)])
        n_kq += 1

    def plan_proj_stage(gi, j):
        nonlocal n_wc
        wsel = 0 if j < 8 else 1
        dsrc = dwkT if wsel == 0 else dwqT
        for e in range(8):
            wts = []
            if e == 0:
                wts.append(W("v_wc", n_wc))
            p.dma("sync", lambda sy, e_=e, c0=gi * 128, ds=dsrc: sy.dma_start(
                out=wstF[:, 128 * e_:128 * (e_ + 1)],
                in_=ds[128 * e_:128 * (e_ + 1), c0:c0 + 128]),
                waits=wts, inc="d_w")
        cw = [W("d_w", 16 * (p.n("d_w") // 16)), W("t_kq", n_kq)]
        p.step("vector", lambda v: v.tensor_copy(wbf[:, :], wstF[:, :]),
               waits=cw, inc="v_wc")
        n_wc += 1

    def plan_proj_mm(gi, j, e):
        tt = j if j < 8 else j - 8
        rhs = lnT_rhs(e, tt)
        wts = [W("v_wc", n_wc), W("v_prj", p.n("v_prj"))] if e == 0 else []
        p.step("tensor", lambda t, e_=e, r_=rhs: t.matmul(
            PRJ[:, :], wbf[:, 128 * e_:128 * (e_ + 1)], r_,
            start=(e_ == 0), stop=(e_ == 7), skip_group_check=True),
            waits=wts, inc="t_kq" if e == 7 else None)

    def plan_proj_evac(gi, j):
        nonlocal n_kq
        kqb = kq0 if gi % 2 == 0 else kq1
        qgb = qg0 if gi % 2 == 0 else qg1
        wsel = 0 if j < 8 else 1
        tt = j if j < 8 else j - 8
        dst = (kqb[:, 512 * tt:512 * (tt + 1)] if wsel == 0
               else qgb[:, 512 * tt:512 * (tt + 1)])
        if wsel == 0:
            p.step("vector", lambda v, d_=dst: v.tensor_copy(d_, PRJ[:, :]),
                   waits=[W("t_kq", n_kq + 1)], inc=[("v_kq", 1), ("v_prj", 1)])
        else:
            p.step("vector", lambda v, d_=dst: v.tensor_scalar(
                d_, PRJ[:, :], SCALING, None, ALU.mult),
                waits=[W("t_kq", n_kq + 1)], inc=[("v_kq", 1), ("v_prj", 1)])
        n_kq += 1

    def make_mg(gi):
        kqb = kq0 if gi % 2 == 0 else kq1
        qgb = qg0 if gi % 2 == 0 else qg1
        items = []

        def init():
            p.step("vector", lambda v: v.memset(mgq4a[:, :], 0.0),
                   waits=[W("v_kq", 12 * (gi + 1))])
            p.step("vector", lambda v: v.memset(mgq4b[:, :], 0.0))
        items.append(init)
        chunks = ([(qgb, c, mgq4a) for c in range(4)] +
                  [(kqb, c, mgq4b) for c in range(8)])
        for (srcb, c, acc) in chunks:
            def mk(srcb=srcb, c=c, acc=acc):
                p.step("vector", lambda v: v.tensor_tensor(
                    sqs[:, :], srcb[:, 512 * c:512 * (c + 1)],
                    srcb[:, 512 * c:512 * (c + 1)], ALU.mult),
                    waits=[W("t_mq", p.n("t_mq"))], inc="v_sq", dr=True)
                p.step("tensor", lambda t: t.matmul(
                    PRJ[0:4, 0:512], IND[:, :], sqs[:, :],
                    start=True, stop=True, skip_group_check=True),
                    waits=[W("v_sq", p.n("v_sq")), W("v_prj", p.n("v_prj"))],
                    inc="t_mq")
                p.step("vector", lambda v: v.reduce_max(
                    mgq4t[:, 0:1], PRJ[0:4, 0:512], AX.X),
                    waits=[W("t_mq", p.n("t_mq"))], inc=[("v_prj", 1)], dr=True)
                p.step("vector", lambda v, a_=acc: v.tensor_tensor(
                    a_[:, 0:1], a_[:, 0:1], mgq4t[:, 0:1], ALU.max), inc="v_mq",
                    dr=True)
            items.append(mk)

        def fin1():
            p.step("vector", lambda v: v.tensor_tensor(
                mgq4a[:, 0:1], mgq4a[:, 0:1], mgq4b[:, 0:1], ALU.mult),
                inc="v_mq", dr=True)
        items.append(fin1)

        def fin2():
            p.step("tensor", lambda t: t.transpose(
                PRJ[0:1, 0:4], mgq4a[0:4, 0:1], ident_f[0:4, 0:4]),
                waits=[W("v_mq", p.n("v_mq")), W("v_prj", p.n("v_prj"))],
                inc="t_mq")
            p.step("vector", lambda v: v.tensor_copy(mg_q[0:1, 0:4],
                                                     PRJ[0:1, 0:4]),
                   waits=[W("t_mq", p.n("t_mq"))], inc=[("v_prj", 1)])
            p.step("vector", lambda v: v.reduce_max(mg_q[0:1, 12:13],
                                                    mg_q[0:1, 0:4], AX.X),
                   inc="v_mg2", dr=True)
        items.append(fin2)

        def fin3():
            p.step("scalar", lambda s: s.activation(mg_q[0:1, 12:13],
                                                    mg_q[0:1, 12:13], AF.Ln),
                   waits=[W("v_mg2", p.n("v_mg2"))])
            p.step("scalar", lambda s: (s.drain(), s.activation(
                mg_q[0:1, 12:13], mg_q[0:1, 12:13], AF.Exp, scale=0.5))[1],
                inc="a_mg")
            nw = [W("a_mg", p.n("a_mg"))]
            if gi >= 1:
                nw.append(W("t_mgb", gi))
            p.step("vector", lambda v: v.tensor_scalar(
                mg_q[0:1, 13:14], mg_q[0:1, 12:13], -15.0, 0.0,
                ALU.add, ALU.max), waits=nw, dr=True)
            p.step("vector", lambda v: v.tensor_scalar(
                lam_s[0:1, 2:3], mg_q[0:1, 13:14], -1.0, None, ALU.mult),
                inc="v_mg3", dr=True)
            p.step("tensor", lambda t: t.matmul(
                PRJ[0:128, 0:1], ones_any[0:1, :], lam_s[0:1, 2:3],
                start=True, stop=True, skip_group_check=True),
                waits=[W("v_mg3", p.n("v_mg3")), W("v_prj", p.n("v_prj")),
                       W("c_or", 1)],
                inc="t_mgb")
            p.step("vector", lambda v, gi_=gi: v.tensor_copy(
                mneg[:, gi_:gi_ + 1], PRJ[0:128, 0:1]),
                waits=[W("t_mgb", p.n("t_mgb"))], inc=[("c_mg", 1), ("v_prj", 1)])
        items.append(fin3)
        return items

    for j in range(12):
        plan_proj_chunk(0, j)
    for it in make_mg(0):
        it()
    mg_pending = []

    finst = Bbc.bitcast(BF16)[0:64, 512:1024]
    Bbc_bf = Bbc.bitcast(BF16)[0:128, 0:512]

    def plan_fin(u):
        pr, qt = u // 4, u % 4
        cols = slice(2048 * (pr // 2) + 512 * qt, 2048 * (pr // 2) + 512 * (qt + 1))
        if pr % 2 == 0:
            p.step("vector", lambda v, c_=cols: v.tensor_tensor(
                afin[0:64, c_], Abc[0:64, :], EPI[0:64, 0:512], ALU.mult),
                waits=[W("t_C", u + 1)], inc="v_fin")
        else:
            p.step("vector", lambda v: v.tensor_tensor(
                finst, Abc[0:64, :], EPI[0:64, 0:512], ALU.mult),
                waits=[W("t_C", u + 1)], inc="v_fin")
            p.dma("gpsimd", lambda gp, c_=cols: gp.dma_start(
                out=afin[64:128, c_], in_=finst),
                waits=[W("v_fin", p.n("v_fin"))], inc="d_fin")

    SQ64E = 64.0e-5 ** 0.5

    def plan_sq(pu):
        p.step("scalar", lambda s: s.activation(
            Bbc_bf[64:65, :], Abc[64:65, :], AF.Square, scale=SQ64E),
            waits=[W("v_w64", 2 * pu + 1), W("t_ms", pu)], inc="a_e")
        p.step("scalar", lambda s: s.activation(
            Bbc_bf[0:64, :], Abc[0:64, :], AF.Square),
            waits=[W("v_ab", pu + 1), W("t_ms", pu)], inc="a_e")

    def plan_AB(pu):
        p.step("tensor", lambda t: t.matmul(
            EPI[0:64, 0:512], ones_any[64:65, 0:64], o1s[64:65, :],
            start=True, stop=True, tile_position=(64, 0),
            skip_group_check=True),
            waits=[W("v_Oe", 2 * (pu + 1)), W("v_fin", p.n("v_fin")),
                   W("d_fin", 16 * (p.n("d_fin") // 16)), W("c_or", 1)])
        p.step("tensor", lambda t: t.matmul(
            EPI[64:128, 0:512], ones_any[64:65, 0:64], o0s[64:65, :],
            start=True, stop=True, tile_position=(64, 64),
            skip_group_check=True), inc="t_AB")

    def plan_ms(pu):
        p.step("tensor", lambda t: t.matmul(
            EPI[0:1, 0:512], onesb[0:65, 0:1], Bbc_bf[0:65, :],
            start=True, stop=True, skip_group_check=True),
            waits=[W("a_e", 2 * (pu + 1))], inc="t_ms")

    def plan_lnexp(pu):
        p.step("scalar", lambda s: s.activation(
            o0s[0:1, :], EPI[0:1, 0:512], AF.Ln,
            scale=1.0 / 64.0),
            waits=[W("t_ms", pu + 1)], inc="a_c")
        p.step("scalar", lambda s: s.activation(
            o1s[0:1, :], o0s[0:1, :], AF.Exp, scale=-0.5),
            waits=[W("v_w64", 2 * pu + 2)], inc="a_c")

    def plan_lnbounce(pu):
        p.step("vector", lambda v: v.memset(lam_s[0:1, 6:7], 0.0),
               waits=[W("a_c", 2 * pu + 1)], inc="v_w64")

    def plan_C(pu):
        p.step("tensor", lambda t: t.matmul(
            EPI[0:64, 0:512], sub_r[0:1, 0:64], o1s[0:1, :],
            start=True, stop=True, skip_group_check=True),
            waits=[W("a_c", 2 * (pu + 1)), W("v_ab", pu + 1), W("c_lam", 2)],
            inc="t_C")

    def plan_wo_stage(oc):
        nonlocal n_wc
        for e in range(8):
            wts = [W("v_wc", n_wc)] if e == 0 else []
            p.dma("sync", lambda sy, e_=e, oc_=oc: sy.dma_start(
                out=wstF[:, 128 * e_:128 * (e_ + 1)],
                in_=dwoT[128 * e_:128 * (e_ + 1), 128 * oc_:128 * (oc_ + 1)]),
                waits=wts, inc="d_w")
        wts = [W("d_w", 16 * (p.n("d_w") // 16))]
        if oc == 0:
            wts += [W("t_kq", 96), W("t_vp", 256)]
        p.step("vector", lambda v, oc_=oc: v.tensor_copy(
            AP(tensor=wobf, offset=128 * oc_, ap=[[8192, 128], [1024, 8], [1, 128]]),
            wstF[:, :]), waits=wts, inc="v_wc")
        n_wc += 1

    for gi in range(NG):
        for pi in range(2):
            pr = gi * 2 + pi
            for qt in range(NQT):
                u = pr * 4 + qt
                ur = u - 8 * gi
                kqb = kq0 if gi % 2 == 0 else kq1
                qgb = qg0 if gi % 2 == 0 else qg1
                h0, h1 = 2 * pi, 2 * pi + 1
                def plan_S(r):
                    R = u * NKC + r
                    buf = R % 2
                    Sbuf = SA if buf == 0 else SB
                    wts = []
                    if R >= 2:
                        wts.append(W("a_P", R - 1))
                    if r == 0 and ur == 0:
                        wts.append(W("v_kq", 12 * (gi + 1)))
                    p.step("tensor", lambda t, S_=Sbuf, k_=kqb, q_=qgb,
                           h_=h0, r_=r, qt_=qt: t.matmul(
                        S_[:, 0:512],
                        k_[32 * h_:32 * (h_ + 1), 128 * r_:128 * (r_ + 1)],
                        q_[32 * h_:32 * (h_ + 1), 512 * qt_:512 * (qt_ + 1)],
                        start=True, stop=True, tile_position=(32 * h_, 0),
                        skip_group_check=True), waits=wts)
                    p.step("tensor", lambda t, S_=Sbuf, k_=kqb, q_=qgb,
                           h_=h1, r_=r, qt_=qt: t.matmul(
                        S_[:, 512:1024],
                        k_[32 * h_:32 * (h_ + 1), 128 * r_:128 * (r_ + 1)],
                        q_[32 * h_:32 * (h_ + 1), 512 * qt_:512 * (qt_ + 1)],
                        start=True, stop=True, tile_position=(32 * h_, 0),
                        skip_group_check=True), inc="t_S")
                    ew = [W("t_S", R + 1)]
                    if r == 0:
                        ew.append(W("c_mg", gi + 1))
                    pdst = (P3B[R % 3] if u <= 55
                            else P_AB[:, 1024 * buf:1024 * (buf + 1)])
                    p.step("scalar", lambda s, S_=Sbuf, d_=pdst, g_=gi: s.activation(
                        d_, S_[:, :], AF.Exp,
                        bias=mneg[:, g_:g_ + 1]), waits=ew, inc="a_P")

                def plan_V(r):
                    R = u * NKC + r
                    psrc = (P3B[R % 3] if u <= 55
                            else P_AB[:, 1024 * (R % 2):1024 * (R % 2 + 1)])
                    vw = [W("a_P", R + 1)]
                    if r == 0:
                        vw.append(W("v_Oe", 2 * u))
                        if u == 0:
                            vw.append(W("t_lc", 2))
                    p.step("tensor", lambda t, r_=r, pr_=pr, p_=psrc: t.matmul(
                        O0[0:65, :],
                        v_all[:, 1040 * r_ + 65 * pr_:1040 * r_ + 65 * pr_ + 65],
                        p_[:, 0:512],
                        start=(r_ == 0), stop=(r_ == NKC - 1),
                        skip_group_check=True), waits=vw)
                    p.step("tensor", lambda t, r_=r, pr_=pr, p_=psrc: t.matmul(
                        O1[0:65, :],
                        v_all[:, 1040 * r_ + 65 * pr_:1040 * r_ + 65 * pr_ + 65],
                        p_[:, 512:1024],
                        start=(r_ == 0), stop=(r_ == NKC - 1),
                        skip_group_check=True),
                        inc="t_V" if r == NKC - 1 else None)

                vlag = 2 if u <= 55 else 1
                for r in range(NKC):
                    plan_S(r)
                    if gi + 1 < NG and ur <= 5:
                        if r == 1 and ur in (0, 4):
                            plan_proj_stage(gi + 1, 2 * ur)
                        if 8 <= r <= 15:
                            plan_proj_mm(gi + 1, 2 * ur, r - 8)
                        if 20 <= r <= 27:
                            plan_proj_mm(gi + 1, 2 * ur + 1, r - 20)
                    if r >= vlag:
                        plan_V(r - vlag)
                    if r == NKC - 1:
                        for rv in range(NKC - vlag, NKC):
                            plan_V(rv)
                    if gi + 1 < NG:
                        if ur <= 5 and r == 16:
                            plan_proj_evac(gi + 1, 2 * ur)
                        if ur <= 5 and r == 28:
                            plan_proj_evac(gi + 1, 2 * ur + 1)
                        if ur == 6 and r == 4:
                            mg_pending.extend(make_mg(gi + 1))
                        if ur == 6 and r >= 16 and mg_pending:
                            mg_pending.pop(0)()
                    if u >= 1:
                        pu = u - 1
                        if r == 2:
                            plan_AB(pu)
                        if r == 4:
                            plan_sq(pu)
                        if r == 6:
                            plan_ms(pu)
                        if r == 8:
                            plan_lnexp(pu)
                        if r == 9:
                            plan_lnbounce(pu)
                        if r == 12:
                            plan_C(pu)
                p.step("vector", lambda v: v.tensor_copy(o0s[0:65, :], O0[0:65, :]),
                       waits=[W("t_V", u + 1), W("a_c", 2 * u), W("t_C", u),
                              W("t_AB", u)], inc="v_Oe", dr=True)
                p.step("scalar", lambda s: s.activation(
                    o1s[0:65, :], O1[0:65, :], AF.Identity),
                    waits=[W("t_V", u + 1), W("t_C", u), W("t_AB", u)],
                    inc="v_Oe")
                p.step("vector", lambda v: v.tensor_tensor(
                    Abc[64:65, :], o0s[64:65, :], o1s[64:65, :], ALU.mult),
                    waits=[W("v_Oe", 2 * (u + 1))], inc="v_w64", dr=True)
                if u >= 1:
                    plan_fin(u - 1)
                p.step("vector", lambda v: v.tensor_tensor(
                    Abc[0:64, :], o0s[0:64, :], EPI[0:64, 0:512], ALU.mult),
                    waits=[W("t_AB", u + 1)], dr=True)
                p.step("vector", lambda v: v.scalar_tensor_tensor(
                    Bbc[0:64, :], o1s[0:64, :], lamcol[0:64, 0:1],
                    EPI[64:128, 0:512], ALU.mult, ALU.mult),
                    waits=[W("c_or", 2), W("t_ms", u),
                           W("d_fin", 16 * (p.n("d_fin") // 16))])
                p.step("vector", lambda v: v.tensor_tensor(
                    Abc[0:64, :], Abc[0:64, :], Bbc[0:64, :], ALU.add),
                    inc="v_ab", dr=True)
                if u >= 56:
                    plan_wo_stage(u - 56)
    plan_AB(NU - 1)
    plan_sq(NU - 1)
    plan_ms(NU - 1)
    plan_lnexp(NU - 1)
    plan_lnbounce(NU - 1)
    plan_C(NU - 1)
    plan_fin(NU - 1)

    n_wo = 0
    n_t5 = 0
    for qt in range(NQT):
        for oc in range(8):
            bank = PB[n_wo % 2]
            wts = [W("v_wc", 24 + oc + 1)]
            if n_wo == 0:
                wts += [W("a_P", NU * NKC),
                        W("v_fin", NU), W("d_fin", 16 * 32)]
            if n_wo >= 2:
                wts.append(W("v_woe", n_wo - 1))
            for cc in range(8):
                p.step("tensor", lambda t, b_=bank, cc_=cc, oc_=oc, qt_=qt: t.matmul(
                    b_[0:128, 0:512],
                    wobf[:, 1024 * cc_ + 128 * oc_:1024 * cc_ + 128 * (oc_ + 1)],
                    afin[:, 2048 * cc_ + 512 * qt_:2048 * cc_ + 512 * (qt_ + 1)],
                    start=(cc_ == 0), stop=(cc_ == 7), skip_group_check=True),
                    waits=wts if cc == 0 else [],
                    inc="t_wo" if cc == 7 else None)
            n_wo += 1
            p.step("vector", lambda v, b_=bank, oc_=oc: v.tensor_copy(
                woq[:, 512 * oc_:512 * (oc_ + 1)], b_[0:128, 0:512]),
                waits=[W("t_wo", n_wo)], inc="v_woe")
        for j in range(4):
            jg = qt * 4 + j
            xw = []
            if jg == 0:
                xw += [W("t_kq", 96), W("t_vp", 256)]
            if jg >= 2:
                xw.append(W("v_hp", 8 * (jg - 1)))
            p.dma("sync", lambda sy, jg_=jg: sy.dma_start(
                out=xs5[:, 1024 * (jg_ % 2):1024 * (jg_ % 2 + 1)],
                in_=dx[128 * jg_:128 * (jg_ + 1), :]), waits=xw,
                inc=f"d_x5{jg % 2}")
            for oc in range(8):
                bank = PB[4 + (n_t5 % 2)]
                wts = [W("v_woe", 8 * (qt + 1))]
                if n_t5 == 0:
                    wts.append(W("v_Oe", NU))
                if n_t5 >= 2:
                    wts.append(W("v_hp", p.n("v_hp") - 1))
                p.step("tensor", lambda t, b_=bank, oc_=oc, j_=j: t.transpose(
                    b_[0:128, 0:128],
                    woq[:, 512 * oc_ + 128 * j_:512 * oc_ + 128 * (j_ + 1)],
                    ident_f[:, :]), waits=wts, inc="t_tr5")
                n_t5 += 1
                hw = [W("t_tr5", n_t5), W(f"d_x5{jg % 2}", 16 * (jg // 2 + 1))]
                if oc == 0:
                    if jg == 0:
                        hw.append(W("t_V", NU))
                    if jg >= 2:
                        hw += [W(f"d_hs{jg % 2}", 16 * (jg // 2)),
                               W("v_ln2", 2 * (jg // 2))]
                p.step("vector", lambda v, b_=bank, oc_=oc, jg_=jg: v.tensor_tensor(
                    htile[:, 1024 * (jg_ % 2) + 128 * oc_:
                          1024 * (jg_ % 2) + 128 * (oc_ + 1)],
                    xs5[:, 1024 * (jg_ % 2) + 128 * oc_:
                        1024 * (jg_ % 2) + 128 * (oc_ + 1)],
                    b_[0:128, 0:128], ALU.add), waits=hw,
                    inc=[("v_hp", 1), ("v_h", 1)] if oc == 7 else "v_hp")
            p.dma("gpsimd", lambda gp, jg_=jg: gp.dma_start(
                out=d_h[128 * jg_:128 * (jg_ + 1), :],
                in_=htile[:, 1024 * (jg_ % 2):1024 * (jg_ % 2 + 1)]),
                waits=[W("v_h", jg + 1)], inc=f"d_hs{jg % 2}")
            ht = htile[:, 1024 * (jg % 2):1024 * (jg % 2 + 1)]
            pr2 = jg // 2
            b2 = 2 * (pr2 % 2)
            c2 = b2 + jg % 2
            sq_w = [W("v_h", jg + 1), W("v_wc", 32)]
            if jg % 2 == 0 and pr2 >= 2:
                sq_w.append(W("v_q85", pr2 - 1))
            p.step("scalar", lambda s, h_=ht, c_=c2: (s.drain(), s.activation(
                scr5[:, :], h_, AF.Square, accum_out=stat_q8[:, c_:c_ + 1]))[1],
                waits=sq_w, inc="a_sq5")
            p.step("vector", lambda v, h_=ht, c_=c2: v.reduce_sum(
                stat_s8[:, c_:c_ + 1], h_, AX.X),
                waits=[W("v_h", jg + 1)], dr=(jg % 2 == 0))
            if jg % 2 == 0:
                continue
            p.step("vector", lambda v, b_=b2: v.tensor_scalar(
                stat_m8[:, b_:b_ + 2], stat_s8[:, b_:b_ + 2],
                1.0 / 1024.0, None, ALU.mult), dr=True)
            p.step("vector", lambda v, b_=b2: v.tensor_scalar(
                stat_q8[:, b_:b_ + 2], stat_q8[:, b_:b_ + 2],
                1.0 / 1024.0, None, ALU.mult),
                waits=[W("a_sq5", jg + 1)], inc="v_q85")
            p.step("vector", lambda v, b_=b2: v.tensor_tensor(
                stat_r8[:, b_:b_ + 2], stat_m8[:, b_:b_ + 2],
                stat_m8[:, b_:b_ + 2], ALU.mult), dr=True)
            p.step("vector", lambda v, b_=b2: v.tensor_tensor(
                stat_r8[:, b_:b_ + 2], stat_q8[:, b_:b_ + 2],
                stat_r8[:, b_:b_ + 2], ALU.subtract), inc="v_var5", dr=True)
            sqw5 = [W("v_var5", pr2 + 1)]
            if pr2 >= 2:
                sqw5.append(W("v_ln2", 2 * (pr2 - 1)))
            p.step("scalar", lambda s, b_=b2: s.activation(
                stat_r8[:, b_:b_ + 2], stat_r8[:, b_:b_ + 2],
                AF.Sqrt, bias=epsb[:, 0:1]), waits=sqw5, inc="a_sd5")
            p.step("vector", lambda v, b_=b2: v.reciprocal(
                stat_r8[:, b_:b_ + 2], stat_r8[:, b_:b_ + 2]),
                waits=[W("a_sd5", pr2 + 1)], dr=True)
            for jj in (jg - 1, jg):
                hs = htile[:, 1024 * (jj % 2):1024 * (jj % 2 + 1)]
                p.step("vector", lambda v, h_=hs, jj_=jj, c_=b2 + jj % 2:
                       v.tensor_scalar(
                    ln2[:, 1024 * jj_:1024 * (jj_ + 1)], h_,
                    stat_m8[:, c_:c_ + 1], stat_r8[:, c_:c_ + 1],
                    ALU.subtract, ALU.mult), dr=(jj == jg - 1))
            for jj in (jg - 1, jg):
                p.step("vector", lambda v, jj_=jj: v.reduce_max(
                    amx16[:, jj_:jj_ + 1], ln2[:, 1024 * jj_:1024 * (jj_ + 1)],
                    AX.X, apply_absolute_value=True), inc="v_ln2",
                    dr=(jj == jg - 1))

    p.step("vector", lambda v: v.reduce_max(amax1[:, 0:1], amx16[:, :], AX.X),
           waits=[W("v_ln2", 16)], inc="v_am0", dr=True)
    p.step("tensor", lambda t: t.transpose(EPI[0:1, 0:128], amax1[:, 0:1],
                                           ident_f[:, :]),
           waits=[W("v_am0", 1)], inc="t_am1")
    p.step("vector", lambda v: v.tensor_copy(sc_row[0:1, :], EPI[0:1, 0:128]),
           waits=[W("t_am1", 1)])
    p.step("vector", lambda v: v.reduce_max(ccs[0:1, 0:1], sc_row[0:1, :], AX.X),
           inc="v_am1", dr=True)
    p.dma("gpsimd", lambda gp: gp.dma_start(out=d_cc[0], in_=ccs[:, :]),
          waits=[W("v_am1", 1)], inc="d_cs")
    p.step("gpsimd", lambda gp: gp.collective_compute(
        "AllReduce", ALU.max, replica_groups=[list(range(8))],
        ins=[d_cc[0]], outs=[d_cc[1]]),
        waits=[W("d_cs", 16)], inc="cc")
    p.dma("gpsimd", lambda gp: gp.dma_start(out=ccs[:, :], in_=d_cc[1]),
          waits=[W("cc", 1)], inc="d_ccr")
    p.step("vector", lambda v: v.reciprocal(sc_row[0:1, 0:1], ccs[0:1, 0:1]),
           waits=[W("d_ccr", 16)], dr=True)
    p.step("vector", lambda v: v.tensor_scalar(sc_row[0:1, 0:1], sc_row[0:1, 0:1],
                                               32767.0, None, ALU.mult), inc="v_s2",
           dr=True)
    p.step("tensor", lambda t: t.matmul(
        EPI[0:128, 0:1], ones_any[0:1, :], sc_row[0:1, 0:1],
        start=True, stop=True, skip_group_check=True),
        waits=[W("v_s2", 1), W("v_fin", NU)], inc="t_am1")
    p.step("vector", lambda v: v.tensor_copy(s2col[:, 0:1], EPI[0:128, 0:1]),
           waits=[W("t_am1", 2)], inc="g_s2")

    n_xt = 0
    for jg in range(16):
        wts = [W("g_s2", 1)]
        if jg == 0:
            wts += [W("d_hs0", 16 * 8), W("d_hs1", 16 * 8)]
        p.step("vector", lambda v, jg_=jg: v.tensor_scalar(
            xsq_i[:, :], ln2[:, 1024 * jg_:1024 * (jg_ + 1)], s2col[:, 0:1],
            None, ALU.mult), waits=wts, dr=True)
        p.step("vector", lambda v: v.tensor_copy(xsq_b[:, :], xsq_i[:, :]),
               inc="v_xq", dr=True)
        for ec in range(8):
            bank = PB[n_xt % 4].bitcast(BF16)
            wts = [W("v_xq", jg + 1)]
            if n_xt >= 4:
                wts.append(W("v_xqe", n_xt - 3))
            if n_xt == 0:
                wts.append(W("v_woe", 32))
            p.step("tensor", lambda t, b_=bank, e_=ec: t.transpose(
                b_[0:128, 0:128], xsq_b[:, 128 * e_:128 * (e_ + 1)], ident_b[:, :]),
                waits=wts, inc="t_xqt")
            p.step("vector", lambda v, b_=bank, e_=ec, jg_=jg: v.tensor_copy(
                xsqT[:, 2048 * e_ + 128 * jg_:2048 * e_ + 128 * (jg_ + 1)],
                b_[0:128, 0:128]), waits=[W("t_xqt", n_xt + 1)], inc="v_xqe")
            n_xt += 1

    for cb in range(32):
        b0 = 1024 * (cb % 2)
        for e in range(8):
            wts = []
            if e == 0:
                if cb == 0:
                    wts.append(W("v_h", 16))
                if cb >= 2:
                    wts.append(W("v_wq5d", 8 * (cb - 1)))
            p.dma("sync", lambda sy, e_=e, cb_=cb, b_=b0: sy.dma_start(
                out=wst5[:, b_ + 128 * e_:b_ + 128 * (e_ + 1)],
                in_=dwin[128 * e_:128 * (e_ + 1), 128 * cb_:128 * (cb_ + 1)]),
                waits=wts, inc=f"d_wi{cb % 2}")
        p.step("vector", lambda v, b_=b0: v.tensor_reduce(
            wsc5[:, 0:128],
            AP(tensor=wst5, offset=b_, ap=[[2048, 128], [1, 128], [128, 8]]),
            AX.X, ALU.max, apply_absolute_value=True),
            waits=[W(f"d_wi{cb % 2}", 128 * (cb // 2 + 1))],
            inc="v_wq5", dr=True)
        tw = [W("v_wq5", p.n("v_wq5"))]
        if cb == 0:
            tw.append(W("v_hp", 128))
        if cb >= 1:
            tw.append(W("v_wq5b", cb))
        p.step("tensor", lambda t: t.transpose(PB[4][0:128, 0:128], wsc5[:, 0:128],
                                               ident_f[:, :]), waits=tw, inc="t_wq5")
        p.step("vector", lambda v: v.reduce_max(sc_t[:, 0:1], PB[4][0:128, 0:128],
                                                AX.X),
               waits=[W("t_wq5", p.n("t_wq5"))], dr=True)
        p.step("vector", lambda v, cb_=cb: v.tensor_scalar(
            acolw[:, cb_:cb_ + 1], sc_t[:, 0:1], 1e-8, 1.0 / 127.0,
            ALU.max, ALU.mult),
            inc="v_wq5b" if cb % 2 == 0 else None, dr=True)
        if cb % 2 == 0:
            continue
        p.step("vector", lambda v, cb_=cb: v.reciprocal(
            sc_t2[:, 0:2], acolw[:, cb_ - 1:cb_ + 1]),
            inc="v_wq5b", dr=True)
        for tx, cbx in ((0, cb - 1), (1, cb)):
            bx0 = 1024 * (cbx % 2)
            t2w = [W("v_wq5b", cb + 1)]
            if cbx >= 1:
                t2w.append(W("v_wq5c", cbx))
            p.step("tensor", lambda t, t_=tx: t.transpose(
                PB[5][0:1, 0:128], sc_t2[:, t_:t_ + 1],
                ident_f[:, :]), waits=t2w, inc="t_wq5b")
            p.step("vector", lambda v: v.tensor_copy(sc_row[0:1, :],
                                                     PB[5][0:1, 0:128]),
                   waits=[W("t_wq5b", p.n("t_wq5b"))], inc="v_wq5c")
            gw = [W("v_wq5c", p.n("v_wq5c"))]
            if cbx >= 1:
                gw.append(W("v_wq5d", 8 * cbx))
            if cbx == 0:
                gw += [W("g_s2", 1), W("c_or", 2)]
            p.step("tensor", lambda t: t.matmul(
                EPI[0:128, 0:128], ones_any[0:1, :], sc_row[0:1, 0:128],
                start=True, stop=True, skip_group_check=True),
                waits=gw, inc="g_wq5")
            p.step("vector", lambda v: v.tensor_copy(scbb[:, :],
                                                     EPI[0:128, 0:128]),
                   waits=[W("g_wq5", cbx + 1)], dr=True)
            p.step("vector", lambda v, b_=bx0: v.tensor_tensor(
                wi32b[:, :], wst5[:, b_:b_ + 1024],
                AP(tensor=scbb, offset=0, ap=[[128, 128], [0, 8], [1, 128]]),
                ALU.mult), dr=True)
            p.step("vector", lambda v, cb_=cbx: v.tensor_copy(
                AP(tensor=win_i, offset=128 * cb_,
                   ap=[[32768, 128], [4096, 8], [1, 128]]),
                wi32b[:, :]), inc=[("v_wq5d", 8)], dr=True)

    p.step("vector", lambda v: v.reciprocal(sc_t[:, 0:1], s2col[:, 0:1]),
           waits=[W("g_s2", 1), W("v_wq5d", 256)], dr=True)
    p.step("vector", lambda v: v.tensor_scalar(gsc[:, :], acolw[:, :], sc_t[:, 0:1],
                                               None, ALU.mult), inc="v_gsc",
           dr=True)
    n_f1 = 0
    for fc in range(32):
        for qt in range(NQT):
            bank = PB[n_f1 % 4]
            wts = [W("v_wq5d", 256), W("v_xqe", 128)]
            if n_f1 >= 4:
                wts.append(W("a_gel", n_f1 - 3))
            for e in range(8):
                p.step("tensor", lambda t, b_=bank, e_=e, fc_=fc, qt_=qt: t.matmul(
                    b_[0:128, 0:512],
                    win_i[:, 4096 * e_ + 128 * fc_:4096 * e_ + 128 * (fc_ + 1)],
                    xsqT[:, 2048 * e_ + 512 * qt_:2048 * e_ + 512 * (qt_ + 1)],
                    start=(e_ == 0), stop=(e_ == 7), skip_group_check=True),
                    waits=wts if e == 0 else [],
                    inc="t_f1" if e == 7 else None)
            gw = [W("t_f1", n_f1 + 1), W("v_gsc", 1)]
            if fc >= 2 and qt == 0:
                gw.append(W(f"d_hd{fc % 2}", 16 * (fc // 2)))
            p.step("scalar", lambda s, b_=bank, fc_=fc, qt_=qt: s.activation(
                hidb[:, 2048 * (fc_ % 2) + 512 * qt_:
                     2048 * (fc_ % 2) + 512 * (qt_ + 1)],
                b_[0:128, 0:512], AF.Gelu, scale=gsc[:, fc_:fc_ + 1],
                bias=b_in[:, fc_:fc_ + 1]), waits=gw, inc="a_gel")
            n_f1 += 1
        p.step("vector", lambda v, fc_=fc: v.reduce_max(
            amx32[:, fc_:fc_ + 1], hidb[:, 2048 * (fc_ % 2):2048 * (fc_ % 2 + 1)],
            AX.X, apply_absolute_value=True), waits=[W("a_gel", 4 * (fc + 1))],
            inc="v_ha")
        p.dma("gpsimd", lambda gp, fc_=fc: gp.dma_start(
            out=d_hid[128 * fc_:128 * (fc_ + 1), :],
            in_=hidb[:, 2048 * (fc_ % 2):2048 * (fc_ % 2 + 1)]),
            waits=[W("v_ha", fc + 1)], inc=f"d_hd{fc % 2}")
        buf = wstF if fc % 2 == 0 else wstF2
        dw_w = []
        if fc == 0:
            dw_w += [W("a_sq5", 16), W("v_wc", 32)]
        if fc >= 2:
            dw_w.append(W("v_wo6", fc - 1))
        p.dma("sync", lambda sy, fc_=fc, b_=buf: sy.dma_start(
            out=b_[:, :], in_=dwout[128 * fc_:128 * (fc_ + 1), :]),
            waits=dw_w, inc=f"d_wo{fc % 2}")
        p.step("vector", lambda v, b_=buf: v.reduce_max(
            sc_t[:, 0:1], b_[:, :], AX.X, apply_absolute_value=True),
            waits=[W(f"d_wo{fc % 2}", 16 * (fc // 2 + 1))], dr=True)
        p.step("vector", lambda v, fc_=fc: v.tensor_scalar(
            rocol[:, fc_:fc_ + 1], sc_t[:, 0:1], 1e-8, 1.0 / 127.0,
            ALU.max, ALU.mult), dr=True)
        if fc % 2 == 0:
            continue
        p.step("vector", lambda v, fc_=fc: v.reciprocal(
            sc_t2[:, 0:2], rocol[:, fc_ - 1:fc_ + 1]), dr=True)
        for tx, fcx in ((0, fc - 1), (1, fc)):
            bufx = wstF if fcx % 2 == 0 else wstF2
            wq_w = []
            if fcx == 0:
                wq_w += [W("v_ln2", 16), W("d_hs0", 16 * 8), W("d_hs1", 16 * 8)]
            p.step("vector", lambda v, b_=bufx, t_=tx: v.tensor_scalar(
                wo_i32[:, :], b_[:, :], sc_t2[:, t_:t_ + 1], None, ALU.mult),
                waits=wq_w, dr=True)
            p.step("vector", lambda v, fc_=fcx: v.tensor_copy(
                wout_i[:, 1024 * fc_:1024 * (fc_ + 1)], wo_i32[:, :]),
                inc="v_wo6", dr=True)

    p.step("vector", lambda v: v.reduce_max(amax2[:, 0:1], amx32[:, :], AX.X),
           waits=[W("v_ha", 32)], inc="v_am0", dr=True)
    p.step("tensor", lambda t: t.transpose(EPI[0:1, 0:128], amax2[:, 0:1],
                                           ident_f[:, :]),
           waits=[W("v_am0", 2), W("v_am1", 1)], inc="t_am2")
    p.step("vector", lambda v: v.tensor_copy(sc_row[0:1, :], EPI[0:1, 0:128]),
           waits=[W("t_am2", 1)])
    p.step("vector", lambda v: v.reduce_max(ccs[0:1, 0:1], sc_row[0:1, :], AX.X),
           inc="v_am2", dr=True)
    p.dma("gpsimd", lambda gp: gp.dma_start(out=d_cc[2], in_=ccs[:, :]),
          waits=[W("v_am2", 1)], inc="d_cs")
    p.step("gpsimd", lambda gp: gp.collective_compute(
        "AllReduce", ALU.max, replica_groups=[list(range(8))],
        ins=[d_cc[2]], outs=[d_cc[3]]),
        waits=[W("d_cs", 32)], inc="cc")
    p.dma("gpsimd", lambda gp: gp.dma_start(out=ccs[:, :], in_=d_cc[3]),
          waits=[W("cc", 2)], inc="d_ccr")
    p.step("vector", lambda v: v.reciprocal(sc_row[0:1, 1:2], ccs[0:1, 0:1]),
           waits=[W("d_ccr", 32)], dr=True)
    p.step("vector", lambda v: v.tensor_scalar(sc_row[0:1, 1:2], sc_row[0:1, 1:2],
                                               32767.0, None, ALU.mult), inc="v_s3",
           dr=True)
    p.step("tensor", lambda t: t.matmul(
        EPI[0:128, 0:1], ones_any[0:1, :], sc_row[0:1, 1:2],
        start=True, stop=True, skip_group_check=True),
        waits=[W("v_s3", 1), W("v_am2", 1)], inc="t_am2")
    p.step("vector", lambda v: v.tensor_copy(s3col[:, 0:1], EPI[0:128, 0:1]),
           waits=[W("t_am2", 2)], inc="g_s3")
    p.step("vector", lambda v: v.reciprocal(sc_t[:, 0:1], s3col[:, 0:1]),
           waits=[W("g_s3", 1), W("v_wo6", 32)], dr=True)
    p.step("vector", lambda v: v.tensor_scalar(invh[:, :], rocol[:, :], sc_t[:, 0:1],
                                               None, ALU.mult), inc="v_gsc",
           dr=True)

    n_t7 = 0
    for qt in range(NQT):
        for fc in range(32):
            k = qt * 32 + fc
            wts = []
            if k == 0:
                wts += [W("d_hd0", 16 * 16), W("d_hd1", 16 * 16),
                        W("d_hs0", 16 * 8), W("d_hs1", 16 * 8),
                        W("g_s3", 1)]
            if k >= 2:
                wts.append(W("v_f2e", k - 1))
            p.dma("sync", lambda sy, fc_=fc, qt_=qt, k_=k: sy.dma_start(
                out=hraw7[:, 512 * (k_ % 2):512 * (k_ % 2 + 1)],
                in_=d_hid[128 * fc_:128 * (fc_ + 1), 512 * qt_:512 * (qt_ + 1)]),
                waits=wts, inc=f"d_h7{k % 2}")
            qw = [W(f"d_h7{k % 2}", 16 * (k // 2 + 1)), W("v_gsc", 2)]
            if k >= 2:
                qw.append(W("t_f2", 8 * (k - 1)))
            hqb = hq_i32 if k % 2 == 0 else hq7b
            p.step("vector", lambda v, k_=k, h_=hqb: v.tensor_scalar(
                h_[:, :], hraw7[:, 512 * (k_ % 2):512 * (k_ % 2 + 1)],
                s3col[:, 0:1], None, ALU.mult), waits=qw, dr=(k == 0))
            p.step("vector", lambda v, k_=k, fc_=fc, h_=hqb: v.tensor_scalar(
                rhs7[:, 512 * (k_ % 2):512 * (k_ % 2 + 1)], h_[:, :],
                invh[:, fc_:fc_ + 1], None, ALU.mult), inc="v_f2e", dr=True)
            mw = [W("v_f2e", k + 1)]
            if fc == 0:
                if qt == 0:
                    mw += [W("a_gel", 128), W("v_am2", 1), W("v_wq5c", 32)]
                else:
                    mw.append(W("v_f2o", 8 * qt))
            for oc in range(8):
                p.step("tensor", lambda t, oc_=oc, fc_=fc, k_=k: t.matmul(
                    PB[oc_][0:128, 0:512],
                    wout_i[:, 1024 * fc_ + 128 * oc_:1024 * fc_ + 128 * (oc_ + 1)],
                    rhs7[:, 512 * (k_ % 2):512 * (k_ % 2 + 1)],
                    start=(fc_ == 0), stop=(fc_ == 31), skip_group_check=True),
                    waits=mw if oc == 0 else [], inc="t_f2")
        for oc in range(8):
            ew = [W("t_f2", 8 * 32 * (qt + 1))]
            if qt == 0 and oc == 0:
                ew.append(W("t_f1", 128))
            p.step("vector", lambda v, oc_=oc: v.tensor_scalar(
                outTs[:, 512 * oc_:512 * (oc_ + 1)], PB[oc_][0:128, 0:512],
                b_out[:, oc_:oc_ + 1], None, ALU.add), waits=ew, inc="v_f2o")
        for j in range(4):
            jg = qt * 4 + j
            hw = []
            if jg >= 2:
                hw.append(W("v_out", 8 * (jg - 1)))
            p.dma("sync", lambda sy, jg_=jg: sy.dma_start(
                out=hin7[:, 1024 * (jg_ % 2):1024 * (jg_ % 2 + 1)],
                in_=d_h[128 * jg_:128 * (jg_ + 1), :]), waits=hw,
                inc=f"d_hn{jg % 2}")
            for oc in range(8):
                bank = PB[n_t7 % 2]
                wts = [W("v_f2o", 8 * (qt + 1))]
                if n_t7 >= 2:
                    wts.append(W("v_out", n_t7 - 1))
                p.step("tensor", lambda t, b_=bank, oc_=oc, j_=j: t.transpose(
                    b_[0:128, 0:128],
                    outTs[:, 512 * oc_ + 128 * j_:512 * oc_ + 128 * (j_ + 1)],
                    ident_f[:, :]), waits=wts, inc="t_tr7")
                n_t7 += 1
                ow = [W("t_tr7", n_t7), W(f"d_hn{jg % 2}", 16 * (jg // 2 + 1))]
                if oc == 0 and jg >= 2:
                    ow.append(W(f"d_o{jg % 2}", 16 * (jg // 2)))
                p.step("vector", lambda v, b_=bank, oc_=oc, jg_=jg: v.tensor_tensor(
                    outs7[:, 1024 * (jg_ % 2) + 128 * oc_:
                          1024 * (jg_ % 2) + 128 * (oc_ + 1)],
                    hin7[:, 1024 * (jg_ % 2) + 128 * oc_:
                         1024 * (jg_ % 2) + 128 * (oc_ + 1)],
                    b_[0:128, 0:128], ALU.add), waits=ow, inc="v_out")
            p.dma("gpsimd", lambda gp, jg_=jg: gp.dma_start(
                out=dout[128 * jg_:128 * (jg_ + 1), :],
                in_=outs7[:, 1024 * (jg_ % 2):1024 * (jg_ % 2 + 1)]),
                waits=[W("v_out", 8 * (jg + 1))], inc=f"d_o{jg % 2}")
    p.step("gpsimd", lambda gp: (gp.wait_ge(p.sem("d_o0"), 16 * 8),
                                 gp.wait_ge(p.sem("d_o1"), 16 * 8)) and None)

    with nc.Block() as block:
        @block.sync
        def _(sy):
            p.emit("sync", sy)

        @block.gpsimd
        def _(gp):
            p.emit("gpsimd", gp)

        @block.scalar
        def _(sc):
            p.emit("scalar", sc)

        @block.tensor
        def _(te):
            p.emit("tensor", te)

        @block.vector
        def _(ve):
            p.emit("vector", ve)

    return nc


_NC_CACHE = None


def kernel(L0_attn_query, L0_attn_Wq, L0_attn_Wk, L0_attn_Wv, L0_attn_Wo,
           L0_attn_lambda_q1, L0_attn_lambda_k1, L0_attn_lambda_q2,
           L0_attn_lambda_k2, L0_attn_subln_w, L0_ffn_W_in, L0_ffn_b_in,
           L0_ffn_W_out, L0_ffn_b_out):
    global _NC_CACHE
    f32 = np.float32
    x = np.asarray(L0_attn_query, f32)
    wqT = np.ascontiguousarray(np.asarray(L0_attn_Wq, f32).T)
    wkT = np.ascontiguousarray(np.asarray(L0_attn_Wk, f32).T)
    wvT = np.ascontiguousarray(np.asarray(L0_attn_Wv, f32).T)
    woT = np.ascontiguousarray(np.asarray(L0_attn_Wo, f32).T)
    win = np.ascontiguousarray(np.asarray(L0_ffn_W_in, f32))
    wout = np.ascontiguousarray(np.asarray(L0_ffn_W_out, f32))
    b_in_pp = np.ascontiguousarray(np.asarray(L0_ffn_b_in, f32).reshape(32, 128).T)
    b_out_pp = np.ascontiguousarray(np.asarray(L0_ffn_b_out, f32).reshape(8, 128).T)
    lam_r = np.ascontiguousarray(np.concatenate(
        [np.asarray(a, f32).ravel() for a in
         (L0_attn_lambda_q1, L0_attn_lambda_k1,
          L0_attn_lambda_q2, L0_attn_lambda_k2)])[None, :])
    subln_r = np.ascontiguousarray(np.asarray(L0_attn_subln_w, f32)[None, :])
    ident_bf = np.eye(128, dtype=ml_dtypes.bfloat16)
    ident_f = np.eye(128, dtype=f32)

    if _NC_CACHE is None:
        _NC_CACHE = build_nc()
    nc = _NC_CACHE

    in_maps = []
    for c in range(8):
        b, half = c // 2, c % 2
        xb = x[b]
        xp = np.ascontiguousarray(np.concatenate(
            [xb[TQ * half:TQ * (half + 1)], xb[TQ * (1 - half):TQ * (2 - half)]],
            axis=0))
        in_maps.append({
            "x": xp, "wqT": wqT, "wkT": wkT, "wvT": wvT, "woT": woT,
            "win": win, "wout": wout, "b_in_pp": b_in_pp, "b_out_pp": b_out_pp,
            "lam_r": lam_r, "subln_r": subln_r,
            "ident_bf": ident_bf, "ident_f": ident_f,
        })
    res = run_bass_kernel_spmd(nc, in_maps, core_ids=list(range(8)))
    out = np.empty((B, T, D), f32)
    for c in range(8):
        b, half = c // 2, c % 2
        out[b, TQ * half:TQ * (half + 1)] = res.results[c]["out"]
    return out


# version 20
# speedup vs baseline: 1.0207x; 1.0171x over previous
import math
import numpy as np
import ml_dtypes
import concourse.bass as bass
import concourse.dve_ops as _dve_ops
import concourse.mybir as mybir
from concourse.bass_types import AP
from concourse.bass_utils import run_bass_kernel_spmd

F32 = mybir.dt.float32
BF16 = mybir.dt.bfloat16
I32 = mybir.dt.int32
AF = mybir.ActivationFunctionType
ALU = mybir.AluOpType
AX = mybir.AxisListType

B, T, D = 4, 4096, 1024
TQ = 2048
DFF = 4096
NG = 8
NKC = 32
NQT = 4
NU = 64
LAMBDA_INIT = float(0.8 - 0.6 * math.exp(-0.3 * 12))
SCALING = float(32 ** -0.5)

BASE = 16512
OFF_A = BASE
OFF_B = OFF_A + 66560
OFF_C = OFF_B + 65536
OFF_D = OFF_C + 32768
OFF_E = OFF_D + 24576
OFF_F = OFF_E + 4096
OFF_G = OFF_F + 8192


class Plan:
    def __init__(self, nc):
        self.nc = nc
        self.steps = {k: [] for k in ("sync", "gpsimd", "scalar", "tensor", "vector")}
        self.cnt = {}
        self.sems = {}

    def sem(self, name):
        if name not in self.sems:
            self.sems[name] = self.nc.alloc_semaphore(name)
            self.cnt[name] = 0
        return self.sems[name]

    def n(self, name):
        self.sem(name)
        return self.cnt[name]

    def step(self, eng, fn, waits=(), inc=None, inc_n=1, dr=False):
        ws = [(self.sem(s), v) for (s, v) in waits if v > 0]
        if inc is None:
            incs = []
        elif isinstance(inc, str):
            incs = [(inc, inc_n)]
        else:
            incs = list(inc)
        self.steps[eng].append((fn, ws, [(self.sem(s), n) for s, n in incs], dr))
        for s, n in incs:
            self.cnt[s] += n

    def dma(self, eng, fn, waits=(), inc=None):
        self.step(eng, fn, waits, inc, 16)

    def emit(self, name, eng):
        for fn, ws, incs, dr in self.steps[name]:
            for s, v in ws:
                eng.wait_ge(s, v)
            if dr:
                eng.drain()
            ins = fn(eng)
            if incs:
                ins.then_inc(*incs[0])
                for s, n in incs[1:]:
                    eng.drain().then_inc(s, n)


def build_nc():
    nc = bass.Bass("TRN2", target_bir_lowering=False, monotonic_sem_count=0)

    dx = nc.dram_tensor("x", [T, D], F32, kind="ExternalInput").ap()
    dwqT = nc.dram_tensor("wqT", [D, D], F32, kind="ExternalInput").ap()
    dwkT = nc.dram_tensor("wkT", [D, D], F32, kind="ExternalInput").ap()
    dwvT = nc.dram_tensor("wvT", [D, D], F32, kind="ExternalInput").ap()
    dwoT = nc.dram_tensor("woT", [D, D], F32, kind="ExternalInput").ap()
    dwin = nc.dram_tensor("win", [D, DFF], F32, kind="ExternalInput").ap()
    dwout = nc.dram_tensor("wout", [DFF, D], F32, kind="ExternalInput").ap()
    dbin = nc.dram_tensor("b_in_pp", [128, 32], F32, kind="ExternalInput").ap()
    dbout = nc.dram_tensor("b_out_pp", [128, 8], F32, kind="ExternalInput").ap()
    dlam = nc.dram_tensor("lam_r", [1, 128], F32, kind="ExternalInput").ap()
    dsub = nc.dram_tensor("subln_r", [1, 64], F32, kind="ExternalInput").ap()
    dib = nc.dram_tensor("ident_bf", [128, 128], BF16, kind="ExternalInput").ap()
    dif = nc.dram_tensor("ident_f", [128, 128], F32, kind="ExternalInput").ap()
    dout = nc.dram_tensor("out", [TQ, D], F32, kind="ExternalOutput").ap()
    d_h = nc.dram_tensor("sp_h", [TQ, D], F32).ap()
    d_hid = nc.dram_tensor("sp_hid", [DFF, TQ], BF16).ap()
    d_cc = [nc.dram_tensor(f"cc{i}", [1, 8], F32).ap() for i in range(4)]

    def at(name, shape, dtype, off):
        return nc.alloc_sbuf_tensor_at(name, shape, dtype, offset=off)

    ln1 = at("ln1", [128, 32768], BF16, OFF_A)
    v_all = at("v_all", [128, 33280], BF16, OFF_A)
    ln2 = at("ln2", [128, 16384], F32, OFF_A)
    win_i = at("win_i", [128, 32768], BF16, OFF_A)
    ln1T = at("ln1T", [128, 32768], BF16, OFF_B)
    wobf = at("wobf", [128, 8192], BF16, OFF_B)
    wout_i = at("wout_i", [128, 32768], BF16, OFF_B)
    xs5 = at("xs5", [128, 4096], F32, OFF_B + 16384)
    xC = at("xC", [128, 4096], F32, OFF_C)
    afin = at("afin", [128, 16384], BF16, OFF_C)
    xsqT = at("xsqT", [128, 16384], BF16, OFF_C)
    outTs = at("outTs", [128, 4096], F32, OFF_C)
    kq0 = at("kq0", [128, 4096], BF16, OFF_D)
    kq1 = at("kq1", [128, 4096], BF16, OFF_D + 8192)
    qg0 = at("qg0", [128, 2048], BF16, OFF_D + 16384)
    qg1 = at("qg1", [128, 2048], BF16, OFF_D + 20480)
    woq = at("woq", [128, 4096], F32, OFF_D)
    htile = at("htile", [128, 2048], F32, OFF_D + 16384)
    wst5 = at("wst5", [128, 2048], F32, OFF_D)
    wsc5 = at("wsc5", [128, 256], F32, OFF_D + 8192)
    wi32 = at("wi32", [128, 128], I32, OFF_D + 9216)
    hidb = at("hidb", [128, 8192], BF16, OFF_D)
    wo_i32 = at("wo_i32", [128, 1024], I32, OFF_D + 16384)
    hin7 = at("hin7", [128, 4096], F32, OFF_D)
    outs7 = at("outs7", [128, 4096], F32, OFF_D + 8192)
    rhs7 = at("rhs7", [128, 1024], BF16, OFF_D + 16384)
    xsq_b = at("xsq_b", [128, 1024], BF16, OFF_D + 18432)
    xsq_i = at("xsq_i", [128, 1024], I32, OFF_D + 20480)
    P_AB = at("P_AB", [128, 2048], BF16, OFF_E)
    sc_row = at("sc_row", [1, 128], F32, OFF_E)
    hraw7 = at("hraw7", [128, 1024], BF16, OFF_E)
    hq_i32 = at("hq_i32", [128, 512], I32, OFF_E + 2048)
    wstF = at("wstF", [128, 1024], F32, OFF_F)
    wbf = at("wbf", [128, 1024], BF16, OFF_F + 4096)
    lamst = at("lamst", [1, 128], F32, OFF_F + 6144)
    sqs = at("sqs", [128, 512], BF16, OFF_F + 6144)
    scSC = at("scSC", [128, 128], F32, OFF_D + 9728)
    scr5 = at("scr5", [128, 1024], F32, OFF_F)
    wstF2 = at("wstF2", [128, 1024], F32, OFF_F + 4096)
    g = OFF_G
    def G(name, shape, dtype):
        nonlocal g
        t = at(name, shape, dtype, g)
        g += ((shape[1] * mybir.dt.size(dtype) + 31) // 32) * 32
        return t
    ident_b = G("ident_b", [128, 128], BF16)
    ident_f = G("ident_f", [128, 128], F32)
    ones64 = G("ones64", [64, 1], F32)
    lam_s = G("lam_s", [1, 8], F32)
    sub_r = G("sub_r", [1, 64], F32)
    ones_any = G("ones_any", [128, 128], F32)
    mneg = G("mneg", [128, 8], F32)

    IND = G("IND", [128, 4], BF16)
    mgq4a = G("mgq4a", [4, 1], F32)
    mgq4b = G("mgq4b", [4, 1], F32)
    mgq4t = G("mgq4t", [4, 1], F32)
    o0s = G("o0s", [128, 512], F32)
    o1s = G("o1s", [128, 512], F32)
    Abc = G("Abc", [128, 512], F32)
    Bbc = G("Bbc", [128, 512], F32)
    lamcol = G("lamcol", [128, 1], F32)
    epsb = G("epsb", [128, 1], F32)
    b_in = G("b_in", [128, 32], F32)
    b_out = G("b_out", [128, 8], F32)
    amax1 = G("amax1", [128, 1], F32)
    amax2 = amax1
    amx32 = G("amx32", [128, 32], F32)
    amx16 = amx32[0:128, 0:16]
    mg_q = amx32[0:1, 0:16]
    ccs = G("ccs", [1, 8], F32)
    s2col = G("s2col", [128, 1], F32)
    s3col = G("s3col", [128, 1], F32)
    gsc = G("gsc", [128, 32], F32)
    invh = G("invh", [128, 32], F32)
    acolw = G("acolw", [128, 32], F32)
    rocol = G("rocol", [128, 32], F32)
    sc_t = G("sc_t", [128, 1], F32)
    sc_t2 = G("sc_t2", [128, 2], F32)
    onesb = G("onesb", [128, 1], BF16)
    wi32b = at("wi32b", [128, 1024], I32, OFF_D + 10240)
    scbb = at("scbb", [128, 128], F32, OFF_D + 14336)
    hq7b = at("hq7b", [128, 512], I32, OFF_F)
    stat_s8 = amx32[0:128, 16:24]

    stat_m8 = invh[0:128, 0:8]
    stat_q8 = invh[0:128, 8:16]
    stat_r8 = invh[0:128, 16:24]
    assert g <= 229344, f"SBUF overflow {g}"

    SA = nc.alloc_psum_tensor("SA", [128, 1024], F32)
    SB = nc.alloc_psum_tensor("SB", [128, 1024], F32)
    O0 = nc.alloc_psum_tensor("O0", [128, 512], F32)
    O1 = nc.alloc_psum_tensor("O1", [128, 512], F32)
    EPI = nc.alloc_psum_tensor("EPI", [128, 512], F32)
    PRJ = nc.alloc_psum_tensor("PRJ", [128, 512], F32)
    PB = [SA[:, 0:512], SA[:, 512:1024], SB[:, 0:512], SB[:, 512:1024],
          O0[:, :], O1[:, :], EPI[:, :], PRJ[:, :]]

    P3B = [P_AB[:, 0:1024], P_AB[:, 1024:2048], afin[:, 15360:16384]]

    p = Plan(nc)
    W = lambda s, v: (s, v)

    def lnT_rhs(e, tt):
        return AP(tensor=ln1T, offset=1024 * (4 * tt) + 128 * e,
                  ap=[[32768, 128], [1024, 4], [1, 128]])

    for dst, src in ((ident_b[:, :], dib), (ident_f[:, :], dif),
                     (lamst[:, :], dlam), (sub_r[:, :], dsub),
                     (b_in[:, :], dbin), (b_out[:, :], dbout)):
        p.dma("sync", lambda sy, d=dst, s=src: sy.dma_start(out=d, in_=s), inc="d_c")
    p.step("vector", lambda v: v.memset(ones64[:, :], 1.0))
    p.step("vector", lambda v: v.memset(epsb[:, :], 1e-5))
    p.step("vector", lambda v: v.memset(amax1[:, :], 0.0))
    p.step("vector", lambda v: v.memset(ccs[:, :], 0.0))
    p.step("vector", lambda v: v.memset(IND[:, :], 0.0))
    for h in range(4):
        p.step("vector", lambda v, h_=h: v.memset(IND[32 * h_:32 * (h_ + 1),
                                                      h_:h_ + 1], 1.0), dr=True)
    p.step("vector", lambda v: v.tensor_tensor(mg_q[0:1, 0:8], lamst[0:1, 0:8],
                                               lamst[0:1, 0:8], ALU.mult),
           waits=[W("d_c", 16 * 6)])
    p.step("vector", lambda v: v.tensor_tensor(mg_q[0:1, 0:8], lamst[0:1, 0:32],
                                               lamst[0:1, 32:64], ALU.mult) if False
           else v.tensor_tensor(sc_row[0:1, 0:32], lamst[0:1, 0:32],
                                lamst[0:1, 32:64], ALU.mult))
    p.step("vector", lambda v: v.reduce_sum(lam_s[0:1, 4:5], sc_row[0:1, 0:32], AX.X), dr=True)
    p.step("vector", lambda v: v.tensor_tensor(sc_row[0:1, 0:32], lamst[0:1, 64:96],
                                               lamst[0:1, 96:128], ALU.mult), dr=True)
    p.step("vector", lambda v: v.reduce_sum(lam_s[0:1, 5:6], sc_row[0:1, 0:32], AX.X),
           inc="c_dots", dr=True)
    p.step("scalar", lambda s: s.activation(lam_s[0:1, 4:5], lam_s[0:1, 4:5], AF.Exp),
           waits=[W("c_dots", 1)])
    p.step("scalar", lambda s: (s.drain(), s.activation(
        lam_s[0:1, 5:6], lam_s[0:1, 5:6], AF.Exp))[1], inc="c_dots")
    p.step("vector", lambda v: v.tensor_tensor(lam_s[0:1, 0:1], lam_s[0:1, 4:5],
                                               lam_s[0:1, 5:6], ALU.subtract),
           waits=[W("c_dots", 2)])
    p.step("vector", lambda v: v.tensor_scalar(lam_s[0:1, 0:1], lam_s[0:1, 0:1],
                                               LAMBDA_INIT, None, ALU.add), dr=True)
    p.step("vector", lambda v: v.tensor_scalar(lam_s[0:1, 1:2], lam_s[0:1, 0:1],
                                               -1.0, None, ALU.mult), inc="c_lam", dr=True)
    p.step("vector", lambda v: v.tensor_scalar(sub_r[0:1, :], sub_r[0:1, :],
                                               1.0 - LAMBDA_INIT, None, ALU.mult),
           inc="c_lam")
    p.step("vector", lambda v: v.memset(ones_any[:, :], 1.0), inc="c_or")
    p.step("vector", lambda v: v.memset(onesb[:, :], 1.0))
    p.step("tensor", lambda t: t.matmul(
        PRJ[0:128, 0:1], ones_any[0:1, :], lam_s[0:1, 1:2],
        start=True, stop=True, skip_group_check=True),
        waits=[W("c_lam", 1), W("c_or", 1)], inc="t_lc")
    p.step("vector", lambda v: v.tensor_copy(lamcol[:, 0:1], PRJ[0:128, 0:1]),
           waits=[W("t_lc", 1)], inc=[("c_or", 1), ("v_prj", 1)])

    for i in range(32):
        g_, j_ = i // 4, i % 4
        b4 = 4 * (g_ % 2)
        xt = xC[:, 1024 * j_:1024 * (j_ + 1)]
        lt = ln1[:, 1024 * i:1024 * (i + 1)]
        wts = [W("v_ln", i - 3), W("a_sq1", i - 3)] if i >= 4 else []
        xsem = f"d_x{i % 4}"
        p.dma("sync", lambda sy, d=xt, r0=128 * i: sy.dma_start(
            out=d, in_=dx[r0:r0 + 128, :]), waits=wts, inc=xsem)
        sq_w = [W(xsem, 16 * (g_ + 1))]
        if j_ == 0 and g_ >= 2:
            sq_w.append(W("v_q8", g_ - 1))
        p.step("scalar", lambda s, x_=xt, c_=b4 + j_: (s.drain(), s.activation(
            scr5[:, :], x_, AF.Square, accum_out=stat_q8[:, c_:c_ + 1]))[1],
            waits=sq_w, inc="a_sq1")
        p.step("vector", lambda v, x_=xt, c_=b4 + j_: v.reduce_sum(
            stat_s8[:, c_:c_ + 1], x_, AX.X),
            waits=[W(xsem, 16 * (g_ + 1))], dr=(j_ == 0))
        if j_ < 3:
            continue
        p.step("vector", lambda v, b_=b4: v.tensor_scalar(
            stat_m8[:, b_:b_ + 4], stat_s8[:, b_:b_ + 4],
            1.0 / 1024.0, None, ALU.mult), dr=True)
        p.step("vector", lambda v, b_=b4: v.tensor_scalar(
            stat_q8[:, b_:b_ + 4], stat_q8[:, b_:b_ + 4],
            1.0 / 1024.0, None, ALU.mult),
            waits=[W("a_sq1", 4 * (g_ + 1))], inc="v_q8")
        p.step("vector", lambda v, b_=b4: v.tensor_tensor(
            stat_r8[:, b_:b_ + 4], stat_m8[:, b_:b_ + 4],
            stat_m8[:, b_:b_ + 4], ALU.mult), dr=True)
        p.step("vector", lambda v, b_=b4: v.tensor_tensor(
            stat_r8[:, b_:b_ + 4], stat_q8[:, b_:b_ + 4],
            stat_r8[:, b_:b_ + 4], ALU.subtract), inc="v_var", dr=True)
        sqw = [W("v_var", g_ + 1)]
        if g_ >= 2:
            sqw.append(W("v_ln", 4 * (g_ - 1)))
        p.step("scalar", lambda s, b_=b4: s.activation(
            stat_r8[:, b_:b_ + 4], stat_r8[:, b_:b_ + 4],
            AF.Sqrt, bias=epsb[:, 0:1]), waits=sqw, inc="a_sd")
        p.step("vector", lambda v, b_=b4: v.reciprocal(
            stat_r8[:, b_:b_ + 4], stat_r8[:, b_:b_ + 4]),
            waits=[W("a_sd", g_ + 1)], dr=True)
        for jj in range(4):
            ii = 4 * g_ + jj
            p.step("vector", lambda v, jj_=jj, ii_=ii, c_=b4 + jj: v.tensor_scalar(
                ln1[:, 1024 * ii_:1024 * (ii_ + 1)],
                xC[:, 1024 * jj_:1024 * (jj_ + 1)],
                stat_m8[:, c_:c_ + 1], stat_r8[:, c_:c_ + 1],
                ALU.subtract, ALU.mult), inc="v_ln", dr=(jj == 0))
        for it in range(4 * g_, 4 * g_ + 4):
            for e in range(8):
                t_id = it * 8 + e
                bank = PB[t_id % 8].bitcast(BF16)
                wts = [W("v_ln", it + 1)]
                if t_id >= 8:
                    prev = t_id - 8
                    wts.append(W("v_e2" if prev % 2 == 0 else "a_e2",
                                 prev // 2 + 1))
                p.step("tensor", lambda t, b_=bank, i_=it, e_=e: t.transpose(
                    b_[0:128, 0:128],
                    ln1[:, 1024 * i_ + 128 * e_:1024 * i_ + 128 * (e_ + 1)],
                    ident_b[:, :]), waits=wts, inc="t_tr2")
                dst = ln1T[:, 1024 * it + 128 * e:1024 * it + 128 * (e + 1)]
                if t_id % 2 == 0:
                    p.step("vector", lambda v, d_=dst, b_=bank: v.tensor_copy(
                        d_, b_[0:128, 0:128]), waits=[W("t_tr2", t_id + 1)],
                        inc="v_e2")
                else:
                    p.step("scalar", lambda s, d_=dst, b_=bank: s.activation(
                        d_, b_[0:128, 0:128], AF.Identity),
                        waits=[W("t_tr2", t_id + 1)], inc="a_e2")

    n_wc = 0
    n_vp = 0
    for sb in range(8):
        for e in range(8):
            wts = []
            if e == 0:
                wts.append(W("a_sq1", 32))
                if sb >= 1:
                    wts.append(W("v_wc", n_wc))
            p.dma("sync", lambda sy, e_=e, c0=sb * 128: sy.dma_start(
                out=wstF[:, 128 * e_:128 * (e_ + 1)],
                in_=dwvT[128 * e_:128 * (e_ + 1), c0:c0 + 128]),
                waits=wts, inc="d_w")
        cast_w = [W("d_w", 16 * p.n("d_w") // 16 if False else 16 * (p.n("d_w") // 16))]
        if sb >= 1:
            cast_w.append(W("t_vp", n_vp))
        p.step("vector", lambda v: v.tensor_copy(wbf[:, :], wstF[:, :]),
               waits=cast_w, inc="v_wc")
        n_wc += 1
        for ct in range(32):
            bank = PB[n_vp % 4]
            wts = [W("v_wc", n_wc), W("t_tr2", min(256, 8 * (ct + 1)))]
            if n_vp < 4:
                wts += [W("v_e2", 128), W("a_e2", 128)]
            if n_vp >= 4:
                wts.append(W("v_vp", n_vp - 3))
            for e in range(8):
                p.step("tensor", lambda t, b_=bank, e_=e, ct_=ct: t.matmul(
                    b_[0:128, 0:128],
                    ln1T[:, 1024 * ct_ + 128 * e_:1024 * ct_ + 128 * (e_ + 1)],
                    wbf[:, 128 * e_:128 * (e_ + 1)],
                    start=(e_ == 0), stop=(e_ == 7), skip_group_check=True),
                    waits=wts if e == 0 else [],
                    inc="t_vp" if e == 7 else None)
            dst = AP(tensor=v_all, offset=1040 * ct + 65 * (2 * sb),
                     ap=[[33280, 128], [65, 2], [1, 64]])
            p.step("vector", lambda v, d_=dst, b_=bank: v.tensor_copy(
                d_, b_[0:128, 0:128]),
                waits=[W("t_vp", n_vp + 1), W("t_tr2", min(256, 8 * (ct + 3)))],
                inc="v_vp")
            n_vp += 1
    vones = AP(tensor=v_all, offset=64, ap=[[33280, 128], [1040, 32], [65, 16]])
    p.step("vector", lambda v: v.memset(vones, 1.0),
           waits=[W("t_tr2", 256), W("v_vp", 256)], inc="t_lc")

    n_kq = 0
    proj_state = {}

    def plan_proj_chunk(gi, j):
        nonlocal n_wc, n_kq
        kqb = kq0 if gi % 2 == 0 else kq1
        qgb = qg0 if gi % 2 == 0 else qg1
        wsel = 0 if j < 8 else 1
        tt = j if j < 8 else j - 8
        if (wsel == 0 and tt == 0) or (wsel == 1 and tt == 0):
            dsrc = dwkT if wsel == 0 else dwqT
            for e in range(8):
                wts = []
                if e == 0:
                    wts.append(W("v_wc", n_wc))
                p.dma("sync", lambda sy, e_=e, c0=gi * 128, ds=dsrc: sy.dma_start(
                    out=wstF[:, 128 * e_:128 * (e_ + 1)],
                    in_=ds[128 * e_:128 * (e_ + 1), c0:c0 + 128]),
                    waits=wts, inc="d_w")
            cw = [W("d_w", 16 * (p.n("d_w") // 16))]
            if n_kq >= 1 or gi >= 1 or wsel == 1:
                cw.append(W("t_kq", n_kq))
            else:
                cw.append(W("t_vp", 256))
            p.step("vector", lambda v: v.tensor_copy(wbf[:, :], wstF[:, :]),
                   waits=cw, inc="v_wc")
            n_wc += 1
        wts = [W("v_wc", n_wc), W("v_prj", p.n("v_prj"))]
        for e in range(8):
            rhs = lnT_rhs(e, tt)
            p.step("tensor", lambda t, e_=e, r_=rhs: t.matmul(
                PRJ[:, :], wbf[:, 128 * e_:128 * (e_ + 1)], r_,
                start=(e_ == 0), stop=(e_ == 7), skip_group_check=True),
                waits=wts if e == 0 else [],
                inc="t_kq" if e == 7 else None)
        dst = (kqb[:, 512 * tt:512 * (tt + 1)] if wsel == 0
               else qgb[:, 512 * tt:512 * (tt + 1)])
        if wsel == 0:
            p.step("vector", lambda v, d_=dst: v.tensor_copy(d_, PRJ[:, :]),
                   waits=[W("t_kq", n_kq + 1)], inc=[("v_kq", 1), ("v_prj", 1)])
        else:
            p.step("vector", lambda v, d_=dst: v.tensor_scalar(
                d_, PRJ[:, :], SCALING, None, ALU.mult),
                waits=[W("t_kq", n_kq + 1)], inc=[("v_kq", 1), ("v_prj", 1)])
        n_kq += 1

    def plan_proj_stage(gi, j):
        nonlocal n_wc
        wsel = 0 if j < 8 else 1
        dsrc = dwkT if wsel == 0 else dwqT
        for e in range(8):
            wts = []
            if e == 0:
                wts.append(W("v_wc", n_wc))
            p.dma("sync", lambda sy, e_=e, c0=gi * 128, ds=dsrc: sy.dma_start(
                out=wstF[:, 128 * e_:128 * (e_ + 1)],
                in_=ds[128 * e_:128 * (e_ + 1), c0:c0 + 128]),
                waits=wts, inc="d_w")
        cw = [W("d_w", 16 * (p.n("d_w") // 16)), W("t_kq", n_kq)]
        p.step("vector", lambda v: v.tensor_copy(wbf[:, :], wstF[:, :]),
               waits=cw, inc="v_wc")
        n_wc += 1

    def plan_proj_mm(gi, j, e):
        tt = j if j < 8 else j - 8
        rhs = lnT_rhs(e, tt)
        wts = [W("v_wc", n_wc), W("v_prj", p.n("v_prj"))] if e == 0 else []
        p.step("tensor", lambda t, e_=e, r_=rhs: t.matmul(
            PRJ[:, :], wbf[:, 128 * e_:128 * (e_ + 1)], r_,
            start=(e_ == 0), stop=(e_ == 7), skip_group_check=True),
            waits=wts, inc="t_kq" if e == 7 else None)

    def plan_proj_evac(gi, j):
        nonlocal n_kq
        kqb = kq0 if gi % 2 == 0 else kq1
        qgb = qg0 if gi % 2 == 0 else qg1
        wsel = 0 if j < 8 else 1
        tt = j if j < 8 else j - 8
        dst = (kqb[:, 512 * tt:512 * (tt + 1)] if wsel == 0
               else qgb[:, 512 * tt:512 * (tt + 1)])
        if wsel == 0:
            p.step("vector", lambda v, d_=dst: v.tensor_copy(d_, PRJ[:, :]),
                   waits=[W("t_kq", n_kq + 1)], inc=[("v_kq", 1), ("v_prj", 1)])
        else:
            p.step("vector", lambda v, d_=dst: v.tensor_scalar(
                d_, PRJ[:, :], SCALING, None, ALU.mult),
                waits=[W("t_kq", n_kq + 1)], inc=[("v_kq", 1), ("v_prj", 1)])
        n_kq += 1

    def make_mg(gi):
        kqb = kq0 if gi % 2 == 0 else kq1
        qgb = qg0 if gi % 2 == 0 else qg1
        items = []

        def init():
            p.step("vector", lambda v: v.memset(mgq4a[:, :], 0.0),
                   waits=[W("v_kq", 12 * (gi + 1))])
            p.step("vector", lambda v: v.memset(mgq4b[:, :], 0.0))
        items.append(init)
        chunks = ([(qgb, c, mgq4a) for c in range(4)] +
                  [(kqb, c, mgq4b) for c in range(8)])
        for (srcb, c, acc) in chunks:
            def mk(srcb=srcb, c=c, acc=acc):
                p.step("vector", lambda v: v.tensor_tensor(
                    sqs[:, :], srcb[:, 512 * c:512 * (c + 1)],
                    srcb[:, 512 * c:512 * (c + 1)], ALU.mult),
                    waits=[W("t_mq", p.n("t_mq"))], inc="v_sq", dr=True)
                p.step("tensor", lambda t: t.matmul(
                    PRJ[0:4, 0:512], IND[:, :], sqs[:, :],
                    start=True, stop=True, skip_group_check=True),
                    waits=[W("v_sq", p.n("v_sq")), W("v_prj", p.n("v_prj"))],
                    inc="t_mq")
                p.step("vector", lambda v: v.reduce_max(
                    mgq4t[:, 0:1], PRJ[0:4, 0:512], AX.X),
                    waits=[W("t_mq", p.n("t_mq"))], inc=[("v_prj", 1)], dr=True)
                p.step("vector", lambda v, a_=acc: v.tensor_tensor(
                    a_[:, 0:1], a_[:, 0:1], mgq4t[:, 0:1], ALU.max), inc="v_mq",
                    dr=True)
            items.append(mk)

        def fin1():
            p.step("vector", lambda v: v.tensor_tensor(
                mgq4a[:, 0:1], mgq4a[:, 0:1], mgq4b[:, 0:1], ALU.mult),
                inc="v_mq", dr=True)
        items.append(fin1)

        def fin2():
            p.step("tensor", lambda t: t.transpose(
                PRJ[0:1, 0:4], mgq4a[0:4, 0:1], ident_f[0:4, 0:4]),
                waits=[W("v_mq", p.n("v_mq")), W("v_prj", p.n("v_prj"))],
                inc="t_mq")
            p.step("vector", lambda v: v.tensor_copy(mg_q[0:1, 0:4],
                                                     PRJ[0:1, 0:4]),
                   waits=[W("t_mq", p.n("t_mq"))], inc=[("v_prj", 1)])
            p.step("vector", lambda v: v.reduce_max(mg_q[0:1, 12:13],
                                                    mg_q[0:1, 0:4], AX.X),
                   inc="v_mg2", dr=True)
        items.append(fin2)

        def fin3():
            p.step("scalar", lambda s: s.activation(mg_q[0:1, 12:13],
                                                    mg_q[0:1, 12:13], AF.Ln),
                   waits=[W("v_mg2", p.n("v_mg2"))])
            p.step("scalar", lambda s: (s.drain(), s.activation(
                mg_q[0:1, 12:13], mg_q[0:1, 12:13], AF.Exp, scale=0.5))[1],
                inc="a_mg")
            nw = [W("a_mg", p.n("a_mg"))]
            if gi >= 1:
                nw.append(W("t_mgb", gi))
            p.step("vector", lambda v: v.tensor_scalar(
                mg_q[0:1, 13:14], mg_q[0:1, 12:13], -15.0, 0.0,
                ALU.add, ALU.max), waits=nw, dr=True)
            p.step("vector", lambda v: v.tensor_scalar(
                lam_s[0:1, 2:3], mg_q[0:1, 13:14], -1.0, None, ALU.mult),
                inc="v_mg3", dr=True)
            p.step("tensor", lambda t: t.matmul(
                PRJ[0:128, 0:1], ones_any[0:1, :], lam_s[0:1, 2:3],
                start=True, stop=True, skip_group_check=True),
                waits=[W("v_mg3", p.n("v_mg3")), W("v_prj", p.n("v_prj")),
                       W("c_or", 1)],
                inc="t_mgb")
            p.step("vector", lambda v, gi_=gi: v.tensor_copy(
                mneg[:, gi_:gi_ + 1], PRJ[0:128, 0:1]),
                waits=[W("t_mgb", p.n("t_mgb"))], inc=[("c_mg", 1), ("v_prj", 1)])
        items.append(fin3)
        return items

    for j in range(12):
        plan_proj_chunk(0, j)
    for it in make_mg(0):
        it()
    mg_pending = []

    finst = Bbc.bitcast(BF16)[0:64, 512:1024]
    Bbc_bf = Bbc.bitcast(BF16)[0:128, 0:512]

    def plan_fin(u):
        pr, qt = u // 4, u % 4
        cols = slice(2048 * (pr // 2) + 512 * qt, 2048 * (pr // 2) + 512 * (qt + 1))
        if pr % 2 == 0:
            p.step("vector", lambda v, c_=cols: v.tensor_tensor(
                afin[0:64, c_], Abc[0:64, :], EPI[0:64, 0:512], ALU.mult),
                waits=[W("t_C", u + 1)], inc="v_fin")
        else:
            p.step("vector", lambda v: v.tensor_tensor(
                finst, Abc[0:64, :], EPI[0:64, 0:512], ALU.mult),
                waits=[W("t_C", u + 1)], inc="v_fin")
            p.dma("gpsimd", lambda gp, c_=cols: gp.dma_start(
                out=afin[64:128, c_], in_=finst),
                waits=[W("v_fin", p.n("v_fin"))], inc="d_fin")

    SQ64E = 64.0e-5 ** 0.5

    def plan_sq(pu):
        p.step("scalar", lambda s: s.activation(
            Bbc_bf[64:65, :], Abc[64:65, :], AF.Square, scale=SQ64E),
            waits=[W("v_w64", 2 * pu + 1), W("t_ms", pu)], inc="a_e")
        p.step("scalar", lambda s: s.activation(
            Bbc_bf[0:64, :], Abc[0:64, :], AF.Square),
            waits=[W("v_ab", pu + 1), W("t_ms", pu)], inc="a_e")

    def plan_AB(pu):
        p.step("tensor", lambda t: t.matmul(
            EPI[0:64, 0:512], ones_any[64:65, 0:64], o1s[64:65, :],
            start=True, stop=True, tile_position=(64, 0),
            skip_group_check=True),
            waits=[W("v_Oe", 2 * (pu + 1)), W("v_fin", p.n("v_fin")),
                   W("d_fin", 16 * (p.n("d_fin") // 16)), W("c_or", 1)])
        p.step("tensor", lambda t: t.matmul(
            EPI[64:128, 0:512], ones_any[64:65, 0:64], o0s[64:65, :],
            start=True, stop=True, tile_position=(64, 64),
            skip_group_check=True), inc="t_AB")

    def plan_ms(pu):
        p.step("tensor", lambda t: t.matmul(
            EPI[0:1, 0:512], onesb[0:65, 0:1], Bbc_bf[0:65, :],
            start=True, stop=True, skip_group_check=True),
            waits=[W("a_e", 2 * (pu + 1))], inc="t_ms")

    def plan_lnexp(pu):
        p.step("scalar", lambda s: s.activation(
            o0s[0:1, :], EPI[0:1, 0:512], AF.Ln,
            scale=1.0 / 64.0),
            waits=[W("t_ms", pu + 1)], inc="a_c")
        p.step("scalar", lambda s: s.activation(
            o1s[0:1, :], o0s[0:1, :], AF.Exp, scale=-0.5),
            waits=[W("v_w64", 2 * pu + 2)], inc="a_c")

    def plan_lnbounce(pu):
        p.step("vector", lambda v: v.memset(lam_s[0:1, 6:7], 0.0),
               waits=[W("a_c", 2 * pu + 1)], inc="v_w64")

    def plan_C(pu):
        p.step("tensor", lambda t: t.matmul(
            EPI[0:64, 0:512], sub_r[0:1, 0:64], o1s[0:1, :],
            start=True, stop=True, skip_group_check=True),
            waits=[W("a_c", 2 * (pu + 1)), W("v_ab", pu + 1), W("c_lam", 2)],
            inc="t_C")

    def plan_wo_stage(oc):
        nonlocal n_wc
        for e in range(8):
            wts = [W("v_wc", n_wc)] if e == 0 else []
            p.dma("sync", lambda sy, e_=e, oc_=oc: sy.dma_start(
                out=wstF[:, 128 * e_:128 * (e_ + 1)],
                in_=dwoT[128 * e_:128 * (e_ + 1), 128 * oc_:128 * (oc_ + 1)]),
                waits=wts, inc="d_w")
        wts = [W("d_w", 16 * (p.n("d_w") // 16))]
        if oc == 0:
            wts += [W("t_kq", 96), W("t_vp", 256)]
        p.step("vector", lambda v, oc_=oc: v.tensor_copy(
            AP(tensor=wobf, offset=128 * oc_, ap=[[8192, 128], [1024, 8], [1, 128]]),
            wstF[:, :]), waits=wts, inc="v_wc")
        n_wc += 1

    for gi in range(NG):
        for pi in range(2):
            pr = gi * 2 + pi
            for qt in range(NQT):
                u = pr * 4 + qt
                ur = u - 8 * gi
                kqb = kq0 if gi % 2 == 0 else kq1
                qgb = qg0 if gi % 2 == 0 else qg1
                h0, h1 = 2 * pi, 2 * pi + 1
                def plan_S(r):
                    R = u * NKC + r
                    buf = R % 2
                    Sbuf = SA if buf == 0 else SB
                    wts = []
                    if R >= 2:
                        wts.append(W("a_P", R - 1))
                    if r == 0 and ur == 0:
                        wts.append(W("v_kq", 12 * (gi + 1)))
                    p.step("tensor", lambda t, S_=Sbuf, k_=kqb, q_=qgb,
                           h_=h0, r_=r, qt_=qt: t.matmul(
                        S_[:, 0:512],
                        k_[32 * h_:32 * (h_ + 1), 128 * r_:128 * (r_ + 1)],
                        q_[32 * h_:32 * (h_ + 1), 512 * qt_:512 * (qt_ + 1)],
                        start=True, stop=True, tile_position=(32 * h_, 0),
                        skip_group_check=True), waits=wts)
                    p.step("tensor", lambda t, S_=Sbuf, k_=kqb, q_=qgb,
                           h_=h1, r_=r, qt_=qt: t.matmul(
                        S_[:, 512:1024],
                        k_[32 * h_:32 * (h_ + 1), 128 * r_:128 * (r_ + 1)],
                        q_[32 * h_:32 * (h_ + 1), 512 * qt_:512 * (qt_ + 1)],
                        start=True, stop=True, tile_position=(32 * h_, 0),
                        skip_group_check=True), inc="t_S")
                    ew = [W("t_S", R + 1)]
                    if r == 0:
                        ew.append(W("c_mg", gi + 1))
                    pdst = (P3B[R % 3] if u <= 55
                            else P_AB[:, 1024 * buf:1024 * (buf + 1)])
                    p.step("scalar", lambda s, S_=Sbuf, d_=pdst, g_=gi: s.activation(
                        d_, S_[:, :], AF.Exp,
                        bias=mneg[:, g_:g_ + 1]), waits=ew, inc="a_P")

                def plan_V(r):
                    R = u * NKC + r
                    psrc = (P3B[R % 3] if u <= 55
                            else P_AB[:, 1024 * (R % 2):1024 * (R % 2 + 1)])
                    vw = [W("a_P", R + 1)]
                    if r == 0:
                        vw.append(W("v_Oe", 2 * u))
                        if u == 0:
                            vw.append(W("t_lc", 2))
                    p.step("tensor", lambda t, r_=r, pr_=pr, p_=psrc: t.matmul(
                        O0[0:65, :],
                        v_all[:, 1040 * r_ + 65 * pr_:1040 * r_ + 65 * pr_ + 65],
                        p_[:, 0:512],
                        start=(r_ == 0), stop=(r_ == NKC - 1),
                        skip_group_check=True), waits=vw)
                    p.step("tensor", lambda t, r_=r, pr_=pr, p_=psrc: t.matmul(
                        O1[0:65, :],
                        v_all[:, 1040 * r_ + 65 * pr_:1040 * r_ + 65 * pr_ + 65],
                        p_[:, 512:1024],
                        start=(r_ == 0), stop=(r_ == NKC - 1),
                        skip_group_check=True),
                        inc="t_V" if r == NKC - 1 else None)

                vlag = 2 if u <= 55 else 1
                for r in range(NKC):
                    plan_S(r)
                    if gi + 1 < NG and ur <= 5:
                        if r == 1 and ur in (0, 4):
                            plan_proj_stage(gi + 1, 2 * ur)
                        if 5 <= r <= 12:
                            plan_proj_mm(gi + 1, 2 * ur, r - 5)
                        if 19 <= r <= 26:
                            plan_proj_mm(gi + 1, 2 * ur + 1, r - 19)
                    if r >= vlag:
                        plan_V(r - vlag)
                    if r == NKC - 1:
                        for rv in range(NKC - vlag, NKC):
                            plan_V(rv)
                    if gi + 1 < NG:
                        if ur <= 5 and r == 13:
                            plan_proj_evac(gi + 1, 2 * ur)
                        if ur <= 5 and r == 27:
                            plan_proj_evac(gi + 1, 2 * ur + 1)
                        if ur == 6 and r == 4:
                            mg_pending.extend(make_mg(gi + 1))
                        if ur == 6 and r >= 16 and mg_pending:
                            mg_pending.pop(0)()
                    if u >= 1:
                        pu = u - 1
                        if r == 2:
                            plan_AB(pu)
                        if r == 4:
                            plan_sq(pu)
                        if r == 6:
                            plan_ms(pu)
                        if r == 8:
                            plan_lnexp(pu)
                        if r == 9:
                            plan_lnbounce(pu)
                        if r == 12:
                            plan_C(pu)
                p.step("vector", lambda v: v.tensor_copy(o0s[0:65, :], O0[0:65, :]),
                       waits=[W("t_V", u + 1), W("a_c", 2 * u), W("t_C", u),
                              W("t_AB", u)], inc="v_Oe", dr=True)
                p.step("scalar", lambda s: s.activation(
                    o1s[0:65, :], O1[0:65, :], AF.Identity),
                    waits=[W("t_V", u + 1), W("t_C", u), W("t_AB", u)],
                    inc="v_Oe")
                p.step("vector", lambda v: v.tensor_tensor(
                    Abc[64:65, :], o0s[64:65, :], o1s[64:65, :], ALU.mult),
                    waits=[W("v_Oe", 2 * (u + 1))], inc="v_w64", dr=True)
                if u >= 1:
                    plan_fin(u - 1)
                p.step("vector", lambda v: v.tensor_tensor(
                    Abc[0:64, :], o0s[0:64, :], EPI[0:64, 0:512], ALU.mult),
                    waits=[W("t_AB", u + 1)], dr=True)
                p.step("vector", lambda v: v.scalar_tensor_tensor(
                    Bbc[0:64, :], o1s[0:64, :], lamcol[0:64, 0:1],
                    EPI[64:128, 0:512], ALU.mult, ALU.mult),
                    waits=[W("c_or", 2), W("t_ms", u),
                           W("d_fin", 16 * (p.n("d_fin") // 16))])
                p.step("vector", lambda v: v.tensor_tensor(
                    Abc[0:64, :], Abc[0:64, :], Bbc[0:64, :], ALU.add),
                    inc="v_ab", dr=True)
                if u >= 56:
                    plan_wo_stage(u - 56)
    plan_AB(NU - 1)
    plan_sq(NU - 1)
    plan_ms(NU - 1)
    plan_lnexp(NU - 1)
    plan_lnbounce(NU - 1)
    plan_C(NU - 1)
    plan_fin(NU - 1)

    n_wo = 0
    n_t5 = 0
    for qt in range(NQT):
        for oc in range(8):
            bank = PB[n_wo % 2]
            wts = [W("v_wc", 24 + oc + 1)]
            if n_wo == 0:
                wts += [W("a_P", NU * NKC),
                        W("v_fin", NU), W("d_fin", 16 * 32)]
            if n_wo >= 2:
                wts.append(W("v_woe", n_wo - 1))
            for cc in range(8):
                p.step("tensor", lambda t, b_=bank, cc_=cc, oc_=oc, qt_=qt: t.matmul(
                    b_[0:128, 0:512],
                    wobf[:, 1024 * cc_ + 128 * oc_:1024 * cc_ + 128 * (oc_ + 1)],
                    afin[:, 2048 * cc_ + 512 * qt_:2048 * cc_ + 512 * (qt_ + 1)],
                    start=(cc_ == 0), stop=(cc_ == 7), skip_group_check=True),
                    waits=wts if cc == 0 else [],
                    inc="t_wo" if cc == 7 else None)
            n_wo += 1
            p.step("vector", lambda v, b_=bank, oc_=oc: v.tensor_copy(
                woq[:, 512 * oc_:512 * (oc_ + 1)], b_[0:128, 0:512]),
                waits=[W("t_wo", n_wo)], inc="v_woe")
        for j in range(4):
            jg = qt * 4 + j
            xw = []
            if jg == 0:
                xw += [W("t_kq", 96), W("t_vp", 256)]
            if jg >= 2:
                xw.append(W("v_hp", 8 * (jg - 1)))
            p.dma("sync", lambda sy, jg_=jg: sy.dma_start(
                out=xs5[:, 1024 * (jg_ % 2):1024 * (jg_ % 2 + 1)],
                in_=dx[128 * jg_:128 * (jg_ + 1), :]), waits=xw,
                inc=f"d_x5{jg % 2}")
            for oc in range(8):
                bank = PB[4 + (n_t5 % 2)]
                wts = [W("v_woe", 8 * (qt + 1))]
                if n_t5 == 0:
                    wts.append(W("v_Oe", NU))
                if n_t5 >= 2:
                    wts.append(W("v_hp", p.n("v_hp") - 1))
                p.step("tensor", lambda t, b_=bank, oc_=oc, j_=j: t.transpose(
                    b_[0:128, 0:128],
                    woq[:, 512 * oc_ + 128 * j_:512 * oc_ + 128 * (j_ + 1)],
                    ident_f[:, :]), waits=wts, inc="t_tr5")
                n_t5 += 1
                hw = [W("t_tr5", n_t5), W(f"d_x5{jg % 2}", 16 * (jg // 2 + 1))]
                if oc == 0:
                    if jg == 0:
                        hw.append(W("t_V", NU))
                    if jg >= 2:
                        hw += [W(f"d_hs{jg % 2}", 16 * (jg // 2)),
                               W("v_ln2", 2 * (jg // 2))]
                p.step("vector", lambda v, b_=bank, oc_=oc, jg_=jg: v.tensor_tensor(
                    htile[:, 1024 * (jg_ % 2) + 128 * oc_:
                          1024 * (jg_ % 2) + 128 * (oc_ + 1)],
                    xs5[:, 1024 * (jg_ % 2) + 128 * oc_:
                        1024 * (jg_ % 2) + 128 * (oc_ + 1)],
                    b_[0:128, 0:128], ALU.add), waits=hw,
                    inc=[("v_hp", 1), ("v_h", 1)] if oc == 7 else "v_hp")
            p.dma("gpsimd", lambda gp, jg_=jg: gp.dma_start(
                out=d_h[128 * jg_:128 * (jg_ + 1), :],
                in_=htile[:, 1024 * (jg_ % 2):1024 * (jg_ % 2 + 1)]),
                waits=[W("v_h", jg + 1)], inc=f"d_hs{jg % 2}")
            ht = htile[:, 1024 * (jg % 2):1024 * (jg % 2 + 1)]
            pr2 = jg // 2
            b2 = 2 * (pr2 % 2)
            c2 = b2 + jg % 2
            sq_w = [W("v_h", jg + 1), W("v_wc", 32)]
            if jg % 2 == 0 and pr2 >= 2:
                sq_w.append(W("v_q85", pr2 - 1))
            p.step("scalar", lambda s, h_=ht, c_=c2: (s.drain(), s.activation(
                scr5[:, :], h_, AF.Square, accum_out=stat_q8[:, c_:c_ + 1]))[1],
                waits=sq_w, inc="a_sq5")
            p.step("vector", lambda v, h_=ht, c_=c2: v.reduce_sum(
                stat_s8[:, c_:c_ + 1], h_, AX.X),
                waits=[W("v_h", jg + 1)], dr=(jg % 2 == 0))
            if jg % 2 == 0:
                continue
            p.step("vector", lambda v, b_=b2: v.tensor_scalar(
                stat_m8[:, b_:b_ + 2], stat_s8[:, b_:b_ + 2],
                1.0 / 1024.0, None, ALU.mult), dr=True)
            p.step("vector", lambda v, b_=b2: v.tensor_scalar(
                stat_q8[:, b_:b_ + 2], stat_q8[:, b_:b_ + 2],
                1.0 / 1024.0, None, ALU.mult),
                waits=[W("a_sq5", jg + 1)], inc="v_q85")
            p.step("vector", lambda v, b_=b2: v.tensor_tensor(
                stat_r8[:, b_:b_ + 2], stat_m8[:, b_:b_ + 2],
                stat_m8[:, b_:b_ + 2], ALU.mult), dr=True)
            p.step("vector", lambda v, b_=b2: v.tensor_tensor(
                stat_r8[:, b_:b_ + 2], stat_q8[:, b_:b_ + 2],
                stat_r8[:, b_:b_ + 2], ALU.subtract), inc="v_var5", dr=True)
            sqw5 = [W("v_var5", pr2 + 1)]
            if pr2 >= 2:
                sqw5.append(W("v_ln2", 2 * (pr2 - 1)))
            p.step("scalar", lambda s, b_=b2: s.activation(
                stat_r8[:, b_:b_ + 2], stat_r8[:, b_:b_ + 2],
                AF.Sqrt, bias=epsb[:, 0:1]), waits=sqw5, inc="a_sd5")
            p.step("vector", lambda v, b_=b2: v.reciprocal(
                stat_r8[:, b_:b_ + 2], stat_r8[:, b_:b_ + 2]),
                waits=[W("a_sd5", pr2 + 1)], dr=True)
            for jj in (jg - 1, jg):
                hs = htile[:, 1024 * (jj % 2):1024 * (jj % 2 + 1)]
                p.step("vector", lambda v, h_=hs, jj_=jj, c_=b2 + jj % 2:
                       v.tensor_scalar(
                    ln2[:, 1024 * jj_:1024 * (jj_ + 1)], h_,
                    stat_m8[:, c_:c_ + 1], stat_r8[:, c_:c_ + 1],
                    ALU.subtract, ALU.mult), dr=(jj == jg - 1))
            for jj in (jg - 1, jg):
                p.step("vector", lambda v, jj_=jj: v.reduce_max(
                    amx16[:, jj_:jj_ + 1], ln2[:, 1024 * jj_:1024 * (jj_ + 1)],
                    AX.X, apply_absolute_value=True), inc="v_ln2",
                    dr=(jj == jg - 1))

    p.step("vector", lambda v: v.reduce_max(amax1[:, 0:1], amx16[:, :], AX.X),
           waits=[W("v_ln2", 16)], inc="v_am0", dr=True)
    p.step("tensor", lambda t: t.transpose(EPI[0:1, 0:128], amax1[:, 0:1],
                                           ident_f[:, :]),
           waits=[W("v_am0", 1)], inc="t_am1")
    p.step("vector", lambda v: v.tensor_copy(sc_row[0:1, :], EPI[0:1, 0:128]),
           waits=[W("t_am1", 1)])
    p.step("vector", lambda v: v.reduce_max(ccs[0:1, 0:1], sc_row[0:1, :], AX.X),
           inc="v_am1", dr=True)
    p.dma("gpsimd", lambda gp: gp.dma_start(out=d_cc[0], in_=ccs[:, :]),
          waits=[W("v_am1", 1)], inc="d_cs")
    p.step("gpsimd", lambda gp: gp.collective_compute(
        "AllReduce", ALU.max, replica_groups=[list(range(8))],
        ins=[d_cc[0]], outs=[d_cc[1]]),
        waits=[W("d_cs", 16)], inc="cc")
    p.dma("gpsimd", lambda gp: gp.dma_start(out=ccs[:, :], in_=d_cc[1]),
          waits=[W("cc", 1)], inc="d_ccr")
    p.step("vector", lambda v: v.reciprocal(sc_row[0:1, 0:1], ccs[0:1, 0:1]),
           waits=[W("d_ccr", 16)], dr=True)
    p.step("vector", lambda v: v.tensor_scalar(sc_row[0:1, 0:1], sc_row[0:1, 0:1],
                                               32767.0, None, ALU.mult), inc="v_s2",
           dr=True)
    p.step("tensor", lambda t: t.matmul(
        EPI[0:128, 0:1], ones_any[0:1, :], sc_row[0:1, 0:1],
        start=True, stop=True, skip_group_check=True),
        waits=[W("v_s2", 1), W("v_fin", NU)], inc="t_am1")
    p.step("vector", lambda v: v.tensor_copy(s2col[:, 0:1], EPI[0:128, 0:1]),
           waits=[W("t_am1", 2)], inc="g_s2")

    n_xt = 0
    for jg in range(16):
        wts = [W("g_s2", 1)]
        if jg == 0:
            wts += [W("d_hs0", 16 * 8), W("d_hs1", 16 * 8)]
        p.step("vector", lambda v, jg_=jg: v.tensor_scalar(
            xsq_i[:, :], ln2[:, 1024 * jg_:1024 * (jg_ + 1)], s2col[:, 0:1],
            None, ALU.mult), waits=wts, dr=True)
        p.step("vector", lambda v: v.tensor_copy(xsq_b[:, :], xsq_i[:, :]),
               inc="v_xq", dr=True)
        for ec in range(8):
            bank = PB[n_xt % 4].bitcast(BF16)
            wts = [W("v_xq", jg + 1)]
            if n_xt >= 4:
                wts.append(W("v_xqe", n_xt - 3))
            if n_xt == 0:
                wts.append(W("v_woe", 32))
            p.step("tensor", lambda t, b_=bank, e_=ec: t.transpose(
                b_[0:128, 0:128], xsq_b[:, 128 * e_:128 * (e_ + 1)], ident_b[:, :]),
                waits=wts, inc="t_xqt")
            p.step("vector", lambda v, b_=bank, e_=ec, jg_=jg: v.tensor_copy(
                xsqT[:, 2048 * e_ + 128 * jg_:2048 * e_ + 128 * (jg_ + 1)],
                b_[0:128, 0:128]), waits=[W("t_xqt", n_xt + 1)], inc="v_xqe")
            n_xt += 1

    for cb in range(32):
        b0 = 1024 * (cb % 2)
        for e in range(8):
            wts = []
            if e == 0:
                if cb == 0:
                    wts.append(W("v_h", 16))
                if cb >= 2:
                    wts.append(W("v_wq5d", 8 * (cb - 1)))
            p.dma("sync", lambda sy, e_=e, cb_=cb, b_=b0: sy.dma_start(
                out=wst5[:, b_ + 128 * e_:b_ + 128 * (e_ + 1)],
                in_=dwin[128 * e_:128 * (e_ + 1), 128 * cb_:128 * (cb_ + 1)]),
                waits=wts, inc=f"d_wi{cb % 2}")
        p.step("vector", lambda v, b_=b0: v.tensor_reduce(
            wsc5[:, 0:128],
            AP(tensor=wst5, offset=b_, ap=[[2048, 128], [1, 128], [128, 8]]),
            AX.X, ALU.max, apply_absolute_value=True),
            waits=[W(f"d_wi{cb % 2}", 128 * (cb // 2 + 1))],
            inc="v_wq5", dr=True)
        tw = [W("v_wq5", p.n("v_wq5"))]
        if cb == 0:
            tw.append(W("v_hp", 128))
        if cb >= 1:
            tw.append(W("v_wq5b", cb))
        p.step("tensor", lambda t: t.transpose(PB[4][0:128, 0:128], wsc5[:, 0:128],
                                               ident_f[:, :]), waits=tw, inc="t_wq5")
        p.step("vector", lambda v: v.reduce_max(sc_t[:, 0:1], PB[4][0:128, 0:128],
                                                AX.X),
               waits=[W("t_wq5", p.n("t_wq5"))], dr=True)
        p.step("vector", lambda v, cb_=cb: v.tensor_scalar(
            acolw[:, cb_:cb_ + 1], sc_t[:, 0:1], 1e-8, 1.0 / 127.0,
            ALU.max, ALU.mult),
            inc="v_wq5b" if cb % 2 == 0 else None, dr=True)
        if cb % 2 == 0:
            continue
        p.step("vector", lambda v, cb_=cb: v.reciprocal(
            sc_t2[:, 0:2], acolw[:, cb_ - 1:cb_ + 1]),
            inc="v_wq5b", dr=True)
        for tx, cbx in ((0, cb - 1), (1, cb)):
            bx0 = 1024 * (cbx % 2)
            t2w = [W("v_wq5b", cb + 1)]
            if cbx >= 1:
                t2w.append(W("v_wq5c", cbx))
            p.step("tensor", lambda t, t_=tx: t.transpose(
                PB[5][0:1, 0:128], sc_t2[:, t_:t_ + 1],
                ident_f[:, :]), waits=t2w, inc="t_wq5b")
            p.step("vector", lambda v: v.tensor_copy(sc_row[0:1, :],
                                                     PB[5][0:1, 0:128]),
                   waits=[W("t_wq5b", p.n("t_wq5b"))], inc="v_wq5c")
            gw = [W("v_wq5c", p.n("v_wq5c"))]
            if cbx >= 1:
                gw.append(W("v_wq5d", 8 * cbx))
            if cbx == 0:
                gw += [W("g_s2", 1), W("c_or", 2)]
            p.step("tensor", lambda t: t.matmul(
                EPI[0:128, 0:128], ones_any[0:1, :], sc_row[0:1, 0:128],
                start=True, stop=True, skip_group_check=True),
                waits=gw, inc="g_wq5")
            p.step("vector", lambda v: v.tensor_copy(scbb[:, :],
                                                     EPI[0:128, 0:128]),
                   waits=[W("g_wq5", cbx + 1)], dr=True)
            p.step("vector", lambda v, b_=bx0: v.tensor_tensor(
                wi32b[:, :], wst5[:, b_:b_ + 1024],
                AP(tensor=scbb, offset=0, ap=[[128, 128], [0, 8], [1, 128]]),
                ALU.mult), dr=True)
            p.step("vector", lambda v, cb_=cbx: v.tensor_copy(
                AP(tensor=win_i, offset=128 * cb_,
                   ap=[[32768, 128], [4096, 8], [1, 128]]),
                wi32b[:, :]), inc=[("v_wq5d", 8)], dr=True)

    p.step("vector", lambda v: v.reciprocal(sc_t[:, 0:1], s2col[:, 0:1]),
           waits=[W("g_s2", 1), W("v_wq5d", 256)], dr=True)
    p.step("vector", lambda v: v.tensor_scalar(gsc[:, :], acolw[:, :], sc_t[:, 0:1],
                                               None, ALU.mult), inc="v_gsc",
           dr=True)
    n_f1 = 0
    for fc in range(32):
        for qt in range(NQT):
            bank = PB[n_f1 % 4]
            wts = [W("v_wq5d", 256), W("v_xqe", 128)]
            if n_f1 >= 4:
                wts.append(W("a_gel", n_f1 - 3))
            for e in range(8):
                p.step("tensor", lambda t, b_=bank, e_=e, fc_=fc, qt_=qt: t.matmul(
                    b_[0:128, 0:512],
                    win_i[:, 4096 * e_ + 128 * fc_:4096 * e_ + 128 * (fc_ + 1)],
                    xsqT[:, 2048 * e_ + 512 * qt_:2048 * e_ + 512 * (qt_ + 1)],
                    start=(e_ == 0), stop=(e_ == 7), skip_group_check=True),
                    waits=wts if e == 0 else [],
                    inc="t_f1" if e == 7 else None)
            gw = [W("t_f1", n_f1 + 1), W("v_gsc", 1)]
            if fc >= 2 and qt == 0:
                gw.append(W(f"d_hd{fc % 2}", 16 * (fc // 2)))
            p.step("scalar", lambda s, b_=bank, fc_=fc, qt_=qt: s.activation(
                hidb[:, 2048 * (fc_ % 2) + 512 * qt_:
                     2048 * (fc_ % 2) + 512 * (qt_ + 1)],
                b_[0:128, 0:512], AF.Gelu, scale=gsc[:, fc_:fc_ + 1],
                bias=b_in[:, fc_:fc_ + 1]), waits=gw, inc="a_gel")
            n_f1 += 1
        p.step("vector", lambda v, fc_=fc: v.reduce_max(
            amx32[:, fc_:fc_ + 1], hidb[:, 2048 * (fc_ % 2):2048 * (fc_ % 2 + 1)],
            AX.X, apply_absolute_value=True), waits=[W("a_gel", 4 * (fc + 1))],
            inc="v_ha")
        p.dma("gpsimd", lambda gp, fc_=fc: gp.dma_start(
            out=d_hid[128 * fc_:128 * (fc_ + 1), :],
            in_=hidb[:, 2048 * (fc_ % 2):2048 * (fc_ % 2 + 1)]),
            waits=[W("v_ha", fc + 1)], inc=f"d_hd{fc % 2}")
        buf = wstF if fc % 2 == 0 else wstF2
        dw_w = []
        if fc == 0:
            dw_w += [W("a_sq5", 16), W("v_wc", 32)]
        if fc >= 2:
            dw_w.append(W("v_wo6", fc - 1))
        p.dma("sync", lambda sy, fc_=fc, b_=buf: sy.dma_start(
            out=b_[:, :], in_=dwout[128 * fc_:128 * (fc_ + 1), :]),
            waits=dw_w, inc=f"d_wo{fc % 2}")
        p.step("vector", lambda v, b_=buf: v.reduce_max(
            sc_t[:, 0:1], b_[:, :], AX.X, apply_absolute_value=True),
            waits=[W(f"d_wo{fc % 2}", 16 * (fc // 2 + 1))], dr=True)
        p.step("vector", lambda v, fc_=fc: v.tensor_scalar(
            rocol[:, fc_:fc_ + 1], sc_t[:, 0:1], 1e-8, 1.0 / 127.0,
            ALU.max, ALU.mult), dr=True)
        if fc % 2 == 0:
            continue
        p.step("vector", lambda v, fc_=fc: v.reciprocal(
            sc_t2[:, 0:2], rocol[:, fc_ - 1:fc_ + 1]), dr=True)
        for tx, fcx in ((0, fc - 1), (1, fc)):
            bufx = wstF if fcx % 2 == 0 else wstF2
            wq_w = []
            if fcx == 0:
                wq_w += [W("v_ln2", 16), W("d_hs0", 16 * 8), W("d_hs1", 16 * 8)]
            p.step("vector", lambda v, b_=bufx, t_=tx: v.tensor_scalar(
                wo_i32[:, :], b_[:, :], sc_t2[:, t_:t_ + 1], None, ALU.mult),
                waits=wq_w, dr=True)
            p.step("vector", lambda v, fc_=fcx: v.tensor_copy(
                wout_i[:, 1024 * fc_:1024 * (fc_ + 1)], wo_i32[:, :]),
                inc="v_wo6", dr=True)

    p.step("vector", lambda v: v.reduce_max(amax2[:, 0:1], amx32[:, :], AX.X),
           waits=[W("v_ha", 32)], inc="v_am0", dr=True)
    p.step("tensor", lambda t: t.transpose(EPI[0:1, 0:128], amax2[:, 0:1],
                                           ident_f[:, :]),
           waits=[W("v_am0", 2), W("v_am1", 1)], inc="t_am2")
    p.step("vector", lambda v: v.tensor_copy(sc_row[0:1, :], EPI[0:1, 0:128]),
           waits=[W("t_am2", 1)])
    p.step("vector", lambda v: v.reduce_max(ccs[0:1, 0:1], sc_row[0:1, :], AX.X),
           inc="v_am2", dr=True)
    p.dma("gpsimd", lambda gp: gp.dma_start(out=d_cc[2], in_=ccs[:, :]),
          waits=[W("v_am2", 1)], inc="d_cs")
    p.step("gpsimd", lambda gp: gp.collective_compute(
        "AllReduce", ALU.max, replica_groups=[list(range(8))],
        ins=[d_cc[2]], outs=[d_cc[3]]),
        waits=[W("d_cs", 32)], inc="cc")
    p.dma("gpsimd", lambda gp: gp.dma_start(out=ccs[:, :], in_=d_cc[3]),
          waits=[W("cc", 2)], inc="d_ccr")
    p.step("vector", lambda v: v.reciprocal(sc_row[0:1, 1:2], ccs[0:1, 0:1]),
           waits=[W("d_ccr", 32)], dr=True)
    p.step("vector", lambda v: v.tensor_scalar(sc_row[0:1, 1:2], sc_row[0:1, 1:2],
                                               32767.0, None, ALU.mult), inc="v_s3",
           dr=True)
    p.step("tensor", lambda t: t.matmul(
        EPI[0:128, 0:1], ones_any[0:1, :], sc_row[0:1, 1:2],
        start=True, stop=True, skip_group_check=True),
        waits=[W("v_s3", 1), W("v_am2", 1)], inc="t_am2")
    p.step("vector", lambda v: v.tensor_copy(s3col[:, 0:1], EPI[0:128, 0:1]),
           waits=[W("t_am2", 2)], inc="g_s3")
    p.step("vector", lambda v: v.reciprocal(sc_t[:, 0:1], s3col[:, 0:1]),
           waits=[W("g_s3", 1), W("v_wo6", 32)], dr=True)
    p.step("vector", lambda v: v.tensor_scalar(invh[:, :], rocol[:, :], sc_t[:, 0:1],
                                               None, ALU.mult), inc="v_gsc",
           dr=True)

    n_t7 = 0
    for qt in range(NQT):
        for fc in range(32):
            k = qt * 32 + fc
            wts = []
            if k == 0:
                wts += [W("d_hd0", 16 * 16), W("d_hd1", 16 * 16),
                        W("d_hs0", 16 * 8), W("d_hs1", 16 * 8),
                        W("g_s3", 1)]
            if k >= 2:
                wts.append(W("v_f2e", k - 1))
            p.dma("sync", lambda sy, fc_=fc, qt_=qt, k_=k: sy.dma_start(
                out=hraw7[:, 512 * (k_ % 2):512 * (k_ % 2 + 1)],
                in_=d_hid[128 * fc_:128 * (fc_ + 1), 512 * qt_:512 * (qt_ + 1)]),
                waits=wts, inc=f"d_h7{k % 2}")
            qw = [W(f"d_h7{k % 2}", 16 * (k // 2 + 1)), W("v_gsc", 2)]
            if k >= 2:
                qw.append(W("t_f2", 8 * (k - 1)))
            hqb = hq_i32 if k % 2 == 0 else hq7b
            p.step("vector", lambda v, k_=k, h_=hqb: v.tensor_scalar(
                h_[:, :], hraw7[:, 512 * (k_ % 2):512 * (k_ % 2 + 1)],
                s3col[:, 0:1], None, ALU.mult), waits=qw, dr=(k == 0))
            p.step("vector", lambda v, k_=k, fc_=fc, h_=hqb: v.tensor_scalar(
                rhs7[:, 512 * (k_ % 2):512 * (k_ % 2 + 1)], h_[:, :],
                invh[:, fc_:fc_ + 1], None, ALU.mult), inc="v_f2e", dr=True)
            mw = [W("v_f2e", k + 1)]
            if fc == 0:
                if qt == 0:
                    mw += [W("a_gel", 128), W("v_am2", 1), W("v_wq5c", 32)]
                else:
                    mw.append(W("v_f2o", 8 * qt))
            for oc in range(8):
                p.step("tensor", lambda t, oc_=oc, fc_=fc, k_=k: t.matmul(
                    PB[oc_][0:128, 0:512],
                    wout_i[:, 1024 * fc_ + 128 * oc_:1024 * fc_ + 128 * (oc_ + 1)],
                    rhs7[:, 512 * (k_ % 2):512 * (k_ % 2 + 1)],
                    start=(fc_ == 0), stop=(fc_ == 31), skip_group_check=True),
                    waits=mw if oc == 0 else [], inc="t_f2")
        for oc in range(8):
            ew = [W("t_f2", 8 * 32 * (qt + 1))]
            if qt == 0 and oc == 0:
                ew.append(W("t_f1", 128))
            p.step("vector", lambda v, oc_=oc: v.tensor_scalar(
                outTs[:, 512 * oc_:512 * (oc_ + 1)], PB[oc_][0:128, 0:512],
                b_out[:, oc_:oc_ + 1], None, ALU.add), waits=ew, inc="v_f2o")
        for j in range(4):
            jg = qt * 4 + j
            hw = []
            if jg >= 2:
                hw.append(W("v_out", 8 * (jg - 1)))
            p.dma("sync", lambda sy, jg_=jg: sy.dma_start(
                out=hin7[:, 1024 * (jg_ % 2):1024 * (jg_ % 2 + 1)],
                in_=d_h[128 * jg_:128 * (jg_ + 1), :]), waits=hw,
                inc=f"d_hn{jg % 2}")
            for oc in range(8):
                bank = PB[n_t7 % 2]
                wts = [W("v_f2o", 8 * (qt + 1))]
                if n_t7 >= 2:
                    wts.append(W("v_out", n_t7 - 1))
                p.step("tensor", lambda t, b_=bank, oc_=oc, j_=j: t.transpose(
                    b_[0:128, 0:128],
                    outTs[:, 512 * oc_ + 128 * j_:512 * oc_ + 128 * (j_ + 1)],
                    ident_f[:, :]), waits=wts, inc="t_tr7")
                n_t7 += 1
                ow = [W("t_tr7", n_t7), W(f"d_hn{jg % 2}", 16 * (jg // 2 + 1))]
                if oc == 0 and jg >= 2:
                    ow.append(W(f"d_o{jg % 2}", 16 * (jg // 2)))
                p.step("vector", lambda v, b_=bank, oc_=oc, jg_=jg: v.tensor_tensor(
                    outs7[:, 1024 * (jg_ % 2) + 128 * oc_:
                          1024 * (jg_ % 2) + 128 * (oc_ + 1)],
                    hin7[:, 1024 * (jg_ % 2) + 128 * oc_:
                         1024 * (jg_ % 2) + 128 * (oc_ + 1)],
                    b_[0:128, 0:128], ALU.add), waits=ow, inc="v_out")
            p.dma("gpsimd", lambda gp, jg_=jg: gp.dma_start(
                out=dout[128 * jg_:128 * (jg_ + 1), :],
                in_=outs7[:, 1024 * (jg_ % 2):1024 * (jg_ % 2 + 1)]),
                waits=[W("v_out", 8 * (jg + 1))], inc=f"d_o{jg % 2}")
    p.step("gpsimd", lambda gp: (gp.wait_ge(p.sem("d_o0"), 16 * 8),
                                 gp.wait_ge(p.sem("d_o1"), 16 * 8)) and None)

    with nc.Block() as block:
        @block.sync
        def _(sy):
            p.emit("sync", sy)

        @block.gpsimd
        def _(gp):
            p.emit("gpsimd", gp)

        @block.scalar
        def _(sc):
            p.emit("scalar", sc)

        @block.tensor
        def _(te):
            p.emit("tensor", te)

        @block.vector
        def _(ve):
            p.emit("vector", ve)

    return nc


_NC_CACHE = None


def kernel(L0_attn_query, L0_attn_Wq, L0_attn_Wk, L0_attn_Wv, L0_attn_Wo,
           L0_attn_lambda_q1, L0_attn_lambda_k1, L0_attn_lambda_q2,
           L0_attn_lambda_k2, L0_attn_subln_w, L0_ffn_W_in, L0_ffn_b_in,
           L0_ffn_W_out, L0_ffn_b_out):
    global _NC_CACHE
    f32 = np.float32
    x = np.asarray(L0_attn_query, f32)
    wqT = np.ascontiguousarray(np.asarray(L0_attn_Wq, f32).T)
    wkT = np.ascontiguousarray(np.asarray(L0_attn_Wk, f32).T)
    wvT = np.ascontiguousarray(np.asarray(L0_attn_Wv, f32).T)
    woT = np.ascontiguousarray(np.asarray(L0_attn_Wo, f32).T)
    win = np.ascontiguousarray(np.asarray(L0_ffn_W_in, f32))
    wout = np.ascontiguousarray(np.asarray(L0_ffn_W_out, f32))
    b_in_pp = np.ascontiguousarray(np.asarray(L0_ffn_b_in, f32).reshape(32, 128).T)
    b_out_pp = np.ascontiguousarray(np.asarray(L0_ffn_b_out, f32).reshape(8, 128).T)
    lam_r = np.ascontiguousarray(np.concatenate(
        [np.asarray(a, f32).ravel() for a in
         (L0_attn_lambda_q1, L0_attn_lambda_k1,
          L0_attn_lambda_q2, L0_attn_lambda_k2)])[None, :])
    subln_r = np.ascontiguousarray(np.asarray(L0_attn_subln_w, f32)[None, :])
    ident_bf = np.eye(128, dtype=ml_dtypes.bfloat16)
    ident_f = np.eye(128, dtype=f32)

    if _NC_CACHE is None:
        _NC_CACHE = build_nc()
    nc = _NC_CACHE

    in_maps = []
    for c in range(8):
        b, half = c // 2, c % 2
        xb = x[b]
        xp = np.ascontiguousarray(np.concatenate(
            [xb[TQ * half:TQ * (half + 1)], xb[TQ * (1 - half):TQ * (2 - half)]],
            axis=0))
        in_maps.append({
            "x": xp, "wqT": wqT, "wkT": wkT, "wvT": wvT, "woT": woT,
            "win": win, "wout": wout, "b_in_pp": b_in_pp, "b_out_pp": b_out_pp,
            "lam_r": lam_r, "subln_r": subln_r,
            "ident_bf": ident_bf, "ident_f": ident_f,
        })
    res = run_bass_kernel_spmd(nc, in_maps, core_ids=list(range(8)))
    out = np.empty((B, T, D), f32)
    for c in range(8):
        b, half = c // 2, c % 2
        out[b, TQ * half:TQ * (half + 1)] = res.results[c]["out"]
    return out


# version 21 (final)
# speedup vs baseline: 1.0279x; 1.0070x over previous
import math
import numpy as np
import ml_dtypes
import concourse.bass as bass
import concourse.dve_ops as _dve_ops
import concourse.mybir as mybir
from concourse.bass_types import AP
from concourse.bass_utils import run_bass_kernel_spmd

F32 = mybir.dt.float32
BF16 = mybir.dt.bfloat16
I32 = mybir.dt.int32
AF = mybir.ActivationFunctionType
ALU = mybir.AluOpType
AX = mybir.AxisListType

B, T, D = 4, 4096, 1024
TQ = 2048
DFF = 4096
NG = 8
NKC = 32
NQT = 4
NU = 64
LAMBDA_INIT = float(0.8 - 0.6 * math.exp(-0.3 * 12))
SCALING = float(32 ** -0.5)

BASE = 16512
OFF_A = BASE
OFF_B = OFF_A + 66560
OFF_C = OFF_B + 65536
OFF_D = OFF_C + 32768
OFF_E = OFF_D + 24576
OFF_F = OFF_E + 4096
OFF_G = OFF_F + 8192


class Plan:
    def __init__(self, nc):
        self.nc = nc
        self.steps = {k: [] for k in ("sync", "gpsimd", "scalar", "tensor", "vector")}
        self.cnt = {}
        self.sems = {}

    def sem(self, name):
        if name not in self.sems:
            self.sems[name] = self.nc.alloc_semaphore(name)
            self.cnt[name] = 0
        return self.sems[name]

    def n(self, name):
        self.sem(name)
        return self.cnt[name]

    def step(self, eng, fn, waits=(), inc=None, inc_n=1, dr=False):
        ws = [(self.sem(s), v) for (s, v) in waits if v > 0]
        if inc is None:
            incs = []
        elif isinstance(inc, str):
            incs = [(inc, inc_n)]
        else:
            incs = list(inc)
        self.steps[eng].append((fn, ws, [(self.sem(s), n) for s, n in incs], dr))
        for s, n in incs:
            self.cnt[s] += n

    def dma(self, eng, fn, waits=(), inc=None):
        self.step(eng, fn, waits, inc, 16)

    def emit(self, name, eng):
        for fn, ws, incs, dr in self.steps[name]:
            for s, v in ws:
                eng.wait_ge(s, v)
            if dr:
                eng.drain()
            ins = fn(eng)
            if incs:
                ins.then_inc(*incs[0])
                for s, n in incs[1:]:
                    eng.drain().then_inc(s, n)


def build_nc():
    nc = bass.Bass("TRN2", target_bir_lowering=False, monotonic_sem_count=0)

    dx = nc.dram_tensor("x", [T, D], F32, kind="ExternalInput").ap()
    dwqT = nc.dram_tensor("wqT", [D, D], F32, kind="ExternalInput").ap()
    dwkT = nc.dram_tensor("wkT", [D, D], F32, kind="ExternalInput").ap()
    dwvT = nc.dram_tensor("wvT", [D, D], F32, kind="ExternalInput").ap()
    dwoT = nc.dram_tensor("woT", [D, D], F32, kind="ExternalInput").ap()
    dwin = nc.dram_tensor("win", [D, DFF], F32, kind="ExternalInput").ap()
    dwout = nc.dram_tensor("wout", [DFF, D], F32, kind="ExternalInput").ap()
    dbin = nc.dram_tensor("b_in_pp", [128, 32], F32, kind="ExternalInput").ap()
    dbout = nc.dram_tensor("b_out_pp", [128, 8], F32, kind="ExternalInput").ap()
    dlam = nc.dram_tensor("lam_r", [1, 128], F32, kind="ExternalInput").ap()
    dsub = nc.dram_tensor("subln_r", [1, 64], F32, kind="ExternalInput").ap()
    dib = nc.dram_tensor("ident_bf", [128, 128], BF16, kind="ExternalInput").ap()
    dif = nc.dram_tensor("ident_f", [128, 128], F32, kind="ExternalInput").ap()
    dout = nc.dram_tensor("out", [TQ, D], F32, kind="ExternalOutput").ap()
    d_h = nc.dram_tensor("sp_h", [TQ, D], F32).ap()
    d_hid = nc.dram_tensor("sp_hid", [DFF, TQ], BF16).ap()
    d_cc = [nc.dram_tensor(f"cc{i}", [1, 8], F32).ap() for i in range(4)]

    def at(name, shape, dtype, off):
        return nc.alloc_sbuf_tensor_at(name, shape, dtype, offset=off)

    ln1 = at("ln1", [128, 32768], BF16, OFF_A)
    v_all = at("v_all", [128, 33280], BF16, OFF_A)
    ln2 = at("ln2", [128, 16384], F32, OFF_A)
    win_i = at("win_i", [128, 32768], BF16, OFF_A)
    ln1T = at("ln1T", [128, 32768], BF16, OFF_B)
    wobf = at("wobf", [128, 8192], BF16, OFF_B)
    wout_i = at("wout_i", [128, 32768], BF16, OFF_B)
    xs5 = at("xs5", [128, 4096], F32, OFF_B + 16384)
    xC = at("xC", [128, 4096], F32, OFF_C)
    afin = at("afin", [128, 16384], BF16, OFF_C)
    xsqT = at("xsqT", [128, 16384], BF16, OFF_C)
    outTs = at("outTs", [128, 4096], F32, OFF_C)
    kq0 = at("kq0", [128, 4096], BF16, OFF_D)
    kq1 = at("kq1", [128, 4096], BF16, OFF_D + 8192)
    qg0 = at("qg0", [128, 2048], BF16, OFF_D + 16384)
    qg1 = at("qg1", [128, 2048], BF16, OFF_D + 20480)
    woq = at("woq", [128, 4096], F32, OFF_D)
    htile = at("htile", [128, 2048], F32, OFF_D + 16384)
    wst5 = at("wst5", [128, 2048], F32, OFF_D)
    wsc5 = at("wsc5", [128, 256], F32, OFF_D + 8192)
    wi32 = at("wi32", [128, 128], I32, OFF_D + 9216)
    hidb = at("hidb", [128, 8192], BF16, OFF_D)
    wo_i32 = at("wo_i32", [128, 1024], I32, OFF_D + 16384)
    hin7 = at("hin7", [128, 4096], F32, OFF_D)
    outs7 = at("outs7", [128, 4096], F32, OFF_D + 8192)
    rhs7 = at("rhs7", [128, 1024], BF16, OFF_D + 16384)
    xsq_b = at("xsq_b", [128, 1024], BF16, OFF_D + 18432)
    xsq_i = at("xsq_i", [128, 1024], I32, OFF_D + 20480)
    P_AB = at("P_AB", [128, 2048], BF16, OFF_E)
    sc_row = at("sc_row", [1, 128], F32, OFF_E)
    hraw7 = at("hraw7", [128, 1024], BF16, OFF_E)
    hq_i32 = at("hq_i32", [128, 512], I32, OFF_E + 2048)
    wstF = at("wstF", [128, 1024], F32, OFF_F)
    wbf = at("wbf", [128, 1024], BF16, OFF_F + 4096)
    lamst = at("lamst", [1, 128], F32, OFF_F + 6144)
    sqs = at("sqs", [128, 512], BF16, OFF_F + 6144)
    scSC = at("scSC", [128, 128], F32, OFF_D + 9728)
    scr5 = at("scr5", [128, 1024], F32, OFF_F)
    wstF2 = at("wstF2", [128, 1024], F32, OFF_F + 4096)
    g = OFF_G
    def G(name, shape, dtype):
        nonlocal g
        t = at(name, shape, dtype, g)
        g += ((shape[1] * mybir.dt.size(dtype) + 31) // 32) * 32
        return t
    ident_b = G("ident_b", [128, 128], BF16)
    ident_f = G("ident_f", [128, 128], F32)
    ones64 = G("ones64", [64, 1], F32)
    lam_s = G("lam_s", [1, 8], F32)
    sub_r = G("sub_r", [1, 64], F32)
    ones_any = G("ones_any", [128, 128], F32)
    mneg = G("mneg", [128, 8], F32)

    IND = G("IND", [128, 4], BF16)
    mgq4a = G("mgq4a", [4, 1], F32)
    mgq4b = G("mgq4b", [4, 1], F32)
    mgq4t = G("mgq4t", [4, 1], F32)
    o0s = G("o0s", [128, 512], F32)
    o1s = G("o1s", [128, 512], F32)
    Abc = G("Abc", [128, 512], F32)
    Bbc = G("Bbc", [128, 512], F32)
    lamcol = G("lamcol", [128, 1], F32)
    epsb = G("epsb", [128, 1], F32)
    b_in = G("b_in", [128, 32], F32)
    b_out = G("b_out", [128, 8], F32)
    amax1 = G("amax1", [128, 1], F32)
    amax2 = amax1
    amx32 = G("amx32", [128, 32], F32)
    amx16 = amx32[0:128, 0:16]
    mg_q = amx32[0:1, 0:16]
    ccs = G("ccs", [1, 8], F32)
    s2col = G("s2col", [128, 1], F32)
    s3col = G("s3col", [128, 1], F32)
    gsc = G("gsc", [128, 32], F32)
    invh = G("invh", [128, 32], F32)
    acolw = G("acolw", [128, 32], F32)
    rocol = G("rocol", [128, 32], F32)
    sc_t = G("sc_t", [128, 1], F32)
    sc_t2 = G("sc_t2", [128, 2], F32)
    onesb = G("onesb", [128, 1], BF16)
    wi32b = at("wi32b", [128, 1024], I32, OFF_D + 10240)
    scbb = at("scbb", [128, 128], F32, OFF_D + 14336)
    hq7b = at("hq7b", [128, 512], I32, OFF_F)
    stat_s8 = amx32[0:128, 16:24]

    stat_m8 = invh[0:128, 0:8]
    stat_q8 = invh[0:128, 8:16]
    stat_r8 = invh[0:128, 16:24]
    assert g <= 229344, f"SBUF overflow {g}"

    SA = nc.alloc_psum_tensor("SA", [128, 1024], F32)
    SB = nc.alloc_psum_tensor("SB", [128, 1024], F32)
    O0 = nc.alloc_psum_tensor("O0", [128, 512], F32)
    O1 = nc.alloc_psum_tensor("O1", [128, 512], F32)
    EPI = nc.alloc_psum_tensor("EPI", [128, 512], F32)
    PRJ = nc.alloc_psum_tensor("PRJ", [128, 512], F32)
    PB = [SA[:, 0:512], SA[:, 512:1024], SB[:, 0:512], SB[:, 512:1024],
          O0[:, :], O1[:, :], EPI[:, :], PRJ[:, :]]

    P3B = [P_AB[:, 0:1024], P_AB[:, 1024:2048], afin[:, 15360:16384]]

    p = Plan(nc)
    W = lambda s, v: (s, v)

    def lnT_rhs(e, tt):
        return AP(tensor=ln1T, offset=1024 * (4 * tt) + 128 * e,
                  ap=[[32768, 128], [1024, 4], [1, 128]])

    for dst, src in ((ident_b[:, :], dib), (ident_f[:, :], dif),
                     (lamst[:, :], dlam), (sub_r[:, :], dsub),
                     (b_in[:, :], dbin), (b_out[:, :], dbout)):
        p.dma("sync", lambda sy, d=dst, s=src: sy.dma_start(out=d, in_=s), inc="d_c")
    p.step("vector", lambda v: v.memset(ones64[:, :], 1.0))
    p.step("vector", lambda v: v.memset(epsb[:, :], 1e-5))
    p.step("vector", lambda v: v.memset(amax1[:, :], 0.0))
    p.step("vector", lambda v: v.memset(ccs[:, :], 0.0))
    p.step("vector", lambda v: v.memset(IND[:, :], 0.0))
    for h in range(4):
        p.step("vector", lambda v, h_=h: v.memset(IND[32 * h_:32 * (h_ + 1),
                                                      h_:h_ + 1], 1.0), dr=True)
    p.step("vector", lambda v: v.tensor_tensor(mg_q[0:1, 0:8], lamst[0:1, 0:8],
                                               lamst[0:1, 0:8], ALU.mult),
           waits=[W("d_c", 16 * 6)])
    p.step("vector", lambda v: v.tensor_tensor(mg_q[0:1, 0:8], lamst[0:1, 0:32],
                                               lamst[0:1, 32:64], ALU.mult) if False
           else v.tensor_tensor(sc_row[0:1, 0:32], lamst[0:1, 0:32],
                                lamst[0:1, 32:64], ALU.mult))
    p.step("vector", lambda v: v.reduce_sum(lam_s[0:1, 4:5], sc_row[0:1, 0:32], AX.X), dr=True)
    p.step("vector", lambda v: v.tensor_tensor(sc_row[0:1, 0:32], lamst[0:1, 64:96],
                                               lamst[0:1, 96:128], ALU.mult), dr=True)
    p.step("vector", lambda v: v.reduce_sum(lam_s[0:1, 5:6], sc_row[0:1, 0:32], AX.X),
           inc="c_dots", dr=True)
    p.step("scalar", lambda s: s.activation(lam_s[0:1, 4:5], lam_s[0:1, 4:5], AF.Exp),
           waits=[W("c_dots", 1)])
    p.step("scalar", lambda s: (s.drain(), s.activation(
        lam_s[0:1, 5:6], lam_s[0:1, 5:6], AF.Exp))[1], inc="c_dots")
    p.step("vector", lambda v: v.tensor_tensor(lam_s[0:1, 0:1], lam_s[0:1, 4:5],
                                               lam_s[0:1, 5:6], ALU.subtract),
           waits=[W("c_dots", 2)])
    p.step("vector", lambda v: v.tensor_scalar(lam_s[0:1, 0:1], lam_s[0:1, 0:1],
                                               LAMBDA_INIT, None, ALU.add), dr=True)
    p.step("vector", lambda v: v.tensor_scalar(lam_s[0:1, 1:2], lam_s[0:1, 0:1],
                                               -1.0, None, ALU.mult), inc="c_lam", dr=True)
    p.step("vector", lambda v: v.tensor_scalar(sub_r[0:1, :], sub_r[0:1, :],
                                               1.0 - LAMBDA_INIT, None, ALU.mult),
           inc="c_lam")
    p.step("vector", lambda v: v.memset(ones_any[:, :], 1.0), inc="c_or")
    p.step("vector", lambda v: v.memset(onesb[:, :], 1.0))
    p.step("tensor", lambda t: t.matmul(
        PRJ[0:128, 0:1], ones_any[0:1, :], lam_s[0:1, 1:2],
        start=True, stop=True, skip_group_check=True),
        waits=[W("c_lam", 1), W("c_or", 1)], inc="t_lc")
    p.step("vector", lambda v: v.tensor_copy(lamcol[:, 0:1], PRJ[0:128, 0:1]),
           waits=[W("t_lc", 1)], inc=[("c_or", 1), ("v_prj", 1)])

    for i in range(32):
        g_, j_ = i // 4, i % 4
        b4 = 4 * (g_ % 2)
        xt = xC[:, 1024 * j_:1024 * (j_ + 1)]
        lt = ln1[:, 1024 * i:1024 * (i + 1)]
        wts = [W("v_ln", i - 3), W("a_sq1", i - 3)] if i >= 4 else []
        xsem = f"d_x{i % 4}"
        p.dma("sync", lambda sy, d=xt, r0=128 * i: sy.dma_start(
            out=d, in_=dx[r0:r0 + 128, :]), waits=wts, inc=xsem)
        sq_w = [W(xsem, 16 * (g_ + 1))]
        if j_ == 0 and g_ >= 2:
            sq_w.append(W("v_q8", g_ - 1))
        p.step("scalar", lambda s, x_=xt, c_=b4 + j_: (s.drain(), s.activation(
            scr5[:, :], x_, AF.Square, accum_out=stat_q8[:, c_:c_ + 1]))[1],
            waits=sq_w, inc="a_sq1")
        p.step("vector", lambda v, x_=xt, c_=b4 + j_: v.reduce_sum(
            stat_s8[:, c_:c_ + 1], x_, AX.X),
            waits=[W(xsem, 16 * (g_ + 1))], dr=(j_ == 0))
        if j_ < 3:
            continue
        p.step("vector", lambda v, b_=b4: v.tensor_scalar(
            stat_m8[:, b_:b_ + 4], stat_s8[:, b_:b_ + 4],
            1.0 / 1024.0, None, ALU.mult), dr=True)
        p.step("vector", lambda v, b_=b4: v.tensor_scalar(
            stat_q8[:, b_:b_ + 4], stat_q8[:, b_:b_ + 4],
            1.0 / 1024.0, None, ALU.mult),
            waits=[W("a_sq1", 4 * (g_ + 1))], inc="v_q8")
        p.step("vector", lambda v, b_=b4: v.tensor_tensor(
            stat_r8[:, b_:b_ + 4], stat_m8[:, b_:b_ + 4],
            stat_m8[:, b_:b_ + 4], ALU.mult), dr=True)
        p.step("vector", lambda v, b_=b4: v.tensor_tensor(
            stat_r8[:, b_:b_ + 4], stat_q8[:, b_:b_ + 4],
            stat_r8[:, b_:b_ + 4], ALU.subtract), inc="v_var", dr=True)
        sqw = [W("v_var", g_ + 1)]
        if g_ >= 2:
            sqw.append(W("v_ln", 4 * (g_ - 1)))
        p.step("scalar", lambda s, b_=b4: s.activation(
            stat_r8[:, b_:b_ + 4], stat_r8[:, b_:b_ + 4],
            AF.Sqrt, bias=epsb[:, 0:1]), waits=sqw, inc="a_sd")
        p.step("vector", lambda v, b_=b4: v.reciprocal(
            stat_r8[:, b_:b_ + 4], stat_r8[:, b_:b_ + 4]),
            waits=[W("a_sd", g_ + 1)], dr=True)
        for jj in range(4):
            ii = 4 * g_ + jj
            p.step("vector", lambda v, jj_=jj, ii_=ii, c_=b4 + jj: v.tensor_scalar(
                ln1[:, 1024 * ii_:1024 * (ii_ + 1)],
                xC[:, 1024 * jj_:1024 * (jj_ + 1)],
                stat_m8[:, c_:c_ + 1], stat_r8[:, c_:c_ + 1],
                ALU.subtract, ALU.mult), inc="v_ln", dr=(jj == 0))
        for it in range(4 * g_, 4 * g_ + 4):
            for e in range(8):
                t_id = it * 8 + e
                bank = PB[t_id % 8].bitcast(BF16)
                wts = [W("v_ln", it + 1)]
                if t_id >= 8:
                    prev = t_id - 8
                    wts.append(W("v_e2" if prev % 2 == 0 else "a_e2",
                                 prev // 2 + 1))
                p.step("tensor", lambda t, b_=bank, i_=it, e_=e: t.transpose(
                    b_[0:128, 0:128],
                    ln1[:, 1024 * i_ + 128 * e_:1024 * i_ + 128 * (e_ + 1)],
                    ident_b[:, :]), waits=wts, inc="t_tr2")
                dst = ln1T[:, 1024 * it + 128 * e:1024 * it + 128 * (e + 1)]
                if t_id % 2 == 0:
                    p.step("vector", lambda v, d_=dst, b_=bank: v.tensor_copy(
                        d_, b_[0:128, 0:128]), waits=[W("t_tr2", t_id + 1)],
                        inc="v_e2")
                else:
                    p.step("scalar", lambda s, d_=dst, b_=bank: s.activation(
                        d_, b_[0:128, 0:128], AF.Identity),
                        waits=[W("t_tr2", t_id + 1)], inc="a_e2")

    n_wc = 0
    n_vp = 0
    for sb in range(8):
        for e in range(8):
            wts = []
            if e == 0:
                wts.append(W("a_sq1", 32))
                if sb >= 1:
                    wts.append(W("v_wc", n_wc))
            p.dma("sync", lambda sy, e_=e, c0=sb * 128: sy.dma_start(
                out=wstF[:, 128 * e_:128 * (e_ + 1)],
                in_=dwvT[128 * e_:128 * (e_ + 1), c0:c0 + 128]),
                waits=wts, inc="d_w")
        cast_w = [W("d_w", 16 * p.n("d_w") // 16 if False else 16 * (p.n("d_w") // 16))]
        if sb >= 1:
            cast_w.append(W("t_vp", n_vp))
        p.step("vector", lambda v: v.tensor_copy(wbf[:, :], wstF[:, :]),
               waits=cast_w, inc="v_wc")
        n_wc += 1
        for ct in range(32):
            bank = PB[n_vp % 4]
            wts = [W("v_wc", n_wc), W("t_tr2", min(256, 8 * (ct + 1)))]
            if n_vp < 4:
                wts += [W("v_e2", 128), W("a_e2", 128)]
            if n_vp >= 4:
                wts.append(W("v_vp", n_vp - 3))
            for e in range(8):
                p.step("tensor", lambda t, b_=bank, e_=e, ct_=ct: t.matmul(
                    b_[0:128, 0:128],
                    ln1T[:, 1024 * ct_ + 128 * e_:1024 * ct_ + 128 * (e_ + 1)],
                    wbf[:, 128 * e_:128 * (e_ + 1)],
                    start=(e_ == 0), stop=(e_ == 7), skip_group_check=True),
                    waits=wts if e == 0 else [],
                    inc="t_vp" if e == 7 else None)
            dst = AP(tensor=v_all, offset=1040 * ct + 65 * (2 * sb),
                     ap=[[33280, 128], [65, 2], [1, 64]])
            p.step("vector", lambda v, d_=dst, b_=bank: v.tensor_copy(
                d_, b_[0:128, 0:128]),
                waits=[W("t_vp", n_vp + 1), W("t_tr2", min(256, 8 * (ct + 3)))],
                inc="v_vp")
            n_vp += 1
    vones = AP(tensor=v_all, offset=64, ap=[[33280, 128], [1040, 32], [65, 16]])
    p.step("vector", lambda v: v.memset(vones, 1.0),
           waits=[W("t_tr2", 256), W("v_vp", 256)], inc="t_lc")

    n_kq = 0
    proj_state = {}

    def plan_proj_chunk(gi, j):
        nonlocal n_wc, n_kq
        kqb = kq0 if gi % 2 == 0 else kq1
        qgb = qg0 if gi % 2 == 0 else qg1
        wsel = 0 if j < 8 else 1
        tt = j if j < 8 else j - 8
        if (wsel == 0 and tt == 0) or (wsel == 1 and tt == 0):
            dsrc = dwkT if wsel == 0 else dwqT
            for e in range(8):
                wts = []
                if e == 0:
                    wts.append(W("v_wc", n_wc))
                p.dma("sync", lambda sy, e_=e, c0=gi * 128, ds=dsrc: sy.dma_start(
                    out=wstF[:, 128 * e_:128 * (e_ + 1)],
                    in_=ds[128 * e_:128 * (e_ + 1), c0:c0 + 128]),
                    waits=wts, inc="d_w")
            cw = [W("d_w", 16 * (p.n("d_w") // 16))]
            if n_kq >= 1 or gi >= 1 or wsel == 1:
                cw.append(W("t_kq", n_kq))
            else:
                cw.append(W("t_vp", 256))
            p.step("vector", lambda v: v.tensor_copy(wbf[:, :], wstF[:, :]),
                   waits=cw, inc="v_wc")
            n_wc += 1
        wts = [W("v_wc", n_wc), W("v_prj", p.n("v_prj"))]
        for e in range(8):
            rhs = lnT_rhs(e, tt)
            p.step("tensor", lambda t, e_=e, r_=rhs: t.matmul(
                PRJ[:, :], wbf[:, 128 * e_:128 * (e_ + 1)], r_,
                start=(e_ == 0), stop=(e_ == 7), skip_group_check=True),
                waits=wts if e == 0 else [],
                inc="t_kq" if e == 7 else None)
        dst = (kqb[:, 512 * tt:512 * (tt + 1)] if wsel == 0
               else qgb[:, 512 * tt:512 * (tt + 1)])
        if wsel == 0:
            p.step("vector", lambda v, d_=dst: v.tensor_copy(d_, PRJ[:, :]),
                   waits=[W("t_kq", n_kq + 1)], inc=[("v_kq", 1), ("v_prj", 1)])
        else:
            p.step("vector", lambda v, d_=dst: v.tensor_scalar(
                d_, PRJ[:, :], SCALING, None, ALU.mult),
                waits=[W("t_kq", n_kq + 1)], inc=[("v_kq", 1), ("v_prj", 1)])
        n_kq += 1

    def plan_proj_stage(gi, j):
        nonlocal n_wc
        wsel = 0 if j < 8 else 1
        dsrc = dwkT if wsel == 0 else dwqT
        for e in range(8):
            wts = []
            if e == 0:
                wts.append(W("v_wc", n_wc))
            p.dma("sync", lambda sy, e_=e, c0=gi * 128, ds=dsrc: sy.dma_start(
                out=wstF[:, 128 * e_:128 * (e_ + 1)],
                in_=ds[128 * e_:128 * (e_ + 1), c0:c0 + 128]),
                waits=wts, inc="d_w")
        cw = [W("d_w", 16 * (p.n("d_w") // 16)), W("t_kq", n_kq)]
        p.step("vector", lambda v: v.tensor_copy(wbf[:, :], wstF[:, :]),
               waits=cw, inc="v_wc")
        n_wc += 1

    def plan_proj_mm(gi, j, e):
        tt = j if j < 8 else j - 8
        rhs = lnT_rhs(e, tt)
        wts = [W("v_wc", n_wc), W("v_prj", p.n("v_prj"))] if e == 0 else []
        p.step("tensor", lambda t, e_=e, r_=rhs: t.matmul(
            PRJ[:, :], wbf[:, 128 * e_:128 * (e_ + 1)], r_,
            start=(e_ == 0), stop=(e_ == 7), skip_group_check=True),
            waits=wts, inc="t_kq" if e == 7 else None)

    def plan_proj_evac(gi, j):
        nonlocal n_kq
        kqb = kq0 if gi % 2 == 0 else kq1
        qgb = qg0 if gi % 2 == 0 else qg1
        wsel = 0 if j < 8 else 1
        tt = j if j < 8 else j - 8
        dst = (kqb[:, 512 * tt:512 * (tt + 1)] if wsel == 0
               else qgb[:, 512 * tt:512 * (tt + 1)])
        if wsel == 0:
            p.step("vector", lambda v, d_=dst: v.tensor_copy(d_, PRJ[:, :]),
                   waits=[W("t_kq", n_kq + 1)], inc=[("v_kq", 1), ("v_prj", 1)])
        else:
            p.step("vector", lambda v, d_=dst: v.tensor_scalar(
                d_, PRJ[:, :], SCALING, None, ALU.mult),
                waits=[W("t_kq", n_kq + 1)], inc=[("v_kq", 1), ("v_prj", 1)])
        n_kq += 1

    def make_mg(gi):
        kqb = kq0 if gi % 2 == 0 else kq1
        qgb = qg0 if gi % 2 == 0 else qg1
        items = []

        def init():
            p.step("vector", lambda v: v.memset(mgq4a[:, :], 0.0),
                   waits=[W("v_kq", 12 * (gi + 1))])
            p.step("vector", lambda v: v.memset(mgq4b[:, :], 0.0))
        items.append(init)
        chunks = ([(qgb, c, mgq4a) for c in range(4)] +
                  [(kqb, c, mgq4b) for c in range(8)])
        for (srcb, c, acc) in chunks:
            def mk(srcb=srcb, c=c, acc=acc):
                p.step("vector", lambda v: v.tensor_tensor(
                    sqs[:, :], srcb[:, 512 * c:512 * (c + 1)],
                    srcb[:, 512 * c:512 * (c + 1)], ALU.mult),
                    waits=[W("t_mq", p.n("t_mq"))], inc="v_sq", dr=True)
                p.step("tensor", lambda t: t.matmul(
                    PRJ[0:4, 0:512], IND[:, :], sqs[:, :],
                    start=True, stop=True, skip_group_check=True),
                    waits=[W("v_sq", p.n("v_sq")), W("v_prj", p.n("v_prj"))],
                    inc="t_mq")
                p.step("vector", lambda v: v.reduce_max(
                    mgq4t[:, 0:1], PRJ[0:4, 0:512], AX.X),
                    waits=[W("t_mq", p.n("t_mq"))], inc=[("v_prj", 1)], dr=True)
                p.step("vector", lambda v, a_=acc: v.tensor_tensor(
                    a_[:, 0:1], a_[:, 0:1], mgq4t[:, 0:1], ALU.max), inc="v_mq",
                    dr=True)
            items.append(mk)

        def fin1():
            p.step("vector", lambda v: v.tensor_tensor(
                mgq4a[:, 0:1], mgq4a[:, 0:1], mgq4b[:, 0:1], ALU.mult),
                inc="v_mq", dr=True)
        items.append(fin1)

        def fin2():
            p.step("tensor", lambda t: t.transpose(
                PRJ[0:1, 0:4], mgq4a[0:4, 0:1], ident_f[0:4, 0:4]),
                waits=[W("v_mq", p.n("v_mq")), W("v_prj", p.n("v_prj"))],
                inc="t_mq")
            p.step("vector", lambda v: v.tensor_copy(mg_q[0:1, 0:4],
                                                     PRJ[0:1, 0:4]),
                   waits=[W("t_mq", p.n("t_mq"))], inc=[("v_prj", 1)])
            p.step("vector", lambda v: v.reduce_max(mg_q[0:1, 12:13],
                                                    mg_q[0:1, 0:4], AX.X),
                   inc="v_mg2", dr=True)
        items.append(fin2)

        def fin3():
            p.step("scalar", lambda s: s.activation(mg_q[0:1, 12:13],
                                                    mg_q[0:1, 12:13], AF.Ln),
                   waits=[W("v_mg2", p.n("v_mg2"))])
            p.step("scalar", lambda s: (s.drain(), s.activation(
                mg_q[0:1, 12:13], mg_q[0:1, 12:13], AF.Exp, scale=0.5))[1],
                inc="a_mg")
            nw = [W("a_mg", p.n("a_mg"))]
            if gi >= 1:
                nw.append(W("t_mgb", gi))
            p.step("vector", lambda v: v.tensor_scalar(
                mg_q[0:1, 13:14], mg_q[0:1, 12:13], -15.0, 0.0,
                ALU.add, ALU.max), waits=nw, dr=True)
            p.step("vector", lambda v: v.tensor_scalar(
                lam_s[0:1, 2:3], mg_q[0:1, 13:14], -1.0, None, ALU.mult),
                inc="v_mg3", dr=True)
            p.step("tensor", lambda t: t.matmul(
                PRJ[0:128, 0:1], ones_any[0:1, :], lam_s[0:1, 2:3],
                start=True, stop=True, skip_group_check=True),
                waits=[W("v_mg3", p.n("v_mg3")), W("v_prj", p.n("v_prj")),
                       W("c_or", 1)],
                inc="t_mgb")
            p.step("vector", lambda v, gi_=gi: v.tensor_copy(
                mneg[:, gi_:gi_ + 1], PRJ[0:128, 0:1]),
                waits=[W("t_mgb", p.n("t_mgb"))], inc=[("c_mg", 1), ("v_prj", 1)])
        items.append(fin3)
        return items

    for j in range(12):
        plan_proj_chunk(0, j)
    for it in make_mg(0):
        it()
    mg_pending = []

    finst = Bbc.bitcast(BF16)[0:64, 512:1024]
    Bbc_bf = Bbc.bitcast(BF16)[0:128, 0:512]

    def plan_fin(u):
        pr, qt = u // 4, u % 4
        cols = slice(2048 * (pr // 2) + 512 * qt, 2048 * (pr // 2) + 512 * (qt + 1))
        if pr % 2 == 0:
            p.step("vector", lambda v, c_=cols: v.tensor_tensor(
                afin[0:64, c_], Abc[0:64, :], EPI[0:64, 0:512], ALU.mult),
                waits=[W("t_C", u + 1)], inc="v_fin")
        else:
            p.step("vector", lambda v: v.tensor_tensor(
                finst, Abc[0:64, :], EPI[0:64, 0:512], ALU.mult),
                waits=[W("t_C", u + 1)], inc="v_fin")
            p.dma("gpsimd", lambda gp, c_=cols: gp.dma_start(
                out=afin[64:128, c_], in_=finst),
                waits=[W("v_fin", p.n("v_fin"))], inc="d_fin")

    SQ64E = 64.0e-5 ** 0.5

    def plan_sq(pu):
        p.step("scalar", lambda s: s.activation(
            Bbc_bf[64:65, :], Abc[64:65, :], AF.Square, scale=SQ64E),
            waits=[W("v_w64", 2 * pu + 1), W("t_ms", pu)], inc="a_e")
        p.step("scalar", lambda s: s.activation(
            Bbc_bf[0:64, :], Abc[0:64, :], AF.Square),
            waits=[W("v_ab", pu + 1), W("t_ms", pu)], inc="a_e")

    def plan_AB(pu):
        p.step("tensor", lambda t: t.matmul(
            EPI[0:64, 0:512], ones_any[64:65, 0:64], o1s[64:65, :],
            start=True, stop=True, tile_position=(64, 0),
            skip_group_check=True),
            waits=[W("v_Oe", 2 * (pu + 1)), W("v_fin", p.n("v_fin")),
                   W("d_fin", 16 * (p.n("d_fin") // 16)), W("c_or", 1)])
        p.step("tensor", lambda t: t.matmul(
            EPI[64:128, 0:512], ones_any[64:65, 0:64], o0s[64:65, :],
            start=True, stop=True, tile_position=(64, 64),
            skip_group_check=True), inc="t_AB")

    def plan_ms(pu):
        p.step("tensor", lambda t: t.matmul(
            EPI[0:1, 0:512], onesb[0:65, 0:1], Bbc_bf[0:65, :],
            start=True, stop=True, skip_group_check=True),
            waits=[W("a_e", 2 * (pu + 1))], inc="t_ms")

    def plan_lnexp(pu):
        p.step("scalar", lambda s: s.activation(
            o0s[0:1, :], EPI[0:1, 0:512], AF.Ln,
            scale=1.0 / 64.0),
            waits=[W("t_ms", pu + 1)], inc="a_c")
        p.step("scalar", lambda s: s.activation(
            o1s[0:1, :], o0s[0:1, :], AF.Exp, scale=-0.5),
            waits=[W("v_w64", 2 * pu + 2)], inc="a_c")

    def plan_lnbounce(pu):
        p.step("vector", lambda v: v.memset(lam_s[0:1, 6:7], 0.0),
               waits=[W("a_c", 2 * pu + 1)], inc="v_w64")

    def plan_C(pu):
        p.step("tensor", lambda t: t.matmul(
            EPI[0:64, 0:512], sub_r[0:1, 0:64], o1s[0:1, :],
            start=True, stop=True, skip_group_check=True),
            waits=[W("a_c", 2 * (pu + 1)), W("v_ab", pu + 1), W("c_lam", 2)],
            inc="t_C")

    def plan_wo_stage(oc):
        nonlocal n_wc
        for e in range(8):
            wts = [W("v_wc", n_wc)] if e == 0 else []
            p.dma("sync", lambda sy, e_=e, oc_=oc: sy.dma_start(
                out=wstF[:, 128 * e_:128 * (e_ + 1)],
                in_=dwoT[128 * e_:128 * (e_ + 1), 128 * oc_:128 * (oc_ + 1)]),
                waits=wts, inc="d_w")
        wts = [W("d_w", 16 * (p.n("d_w") // 16))]
        if oc == 0:
            wts += [W("t_kq", 96), W("t_vp", 256)]
        p.step("vector", lambda v, oc_=oc: v.tensor_copy(
            AP(tensor=wobf, offset=128 * oc_, ap=[[8192, 128], [1024, 8], [1, 128]]),
            wstF[:, :]), waits=wts, inc="v_wc")
        n_wc += 1

    for gi in range(NG):
        for pi in range(2):
            pr = gi * 2 + pi
            for qt in range(NQT):
                u = pr * 4 + qt
                ur = u - 8 * gi
                kqb = kq0 if gi % 2 == 0 else kq1
                qgb = qg0 if gi % 2 == 0 else qg1
                h0, h1 = 2 * pi, 2 * pi + 1
                def plan_S(r):
                    R = u * NKC + r
                    buf = R % 2
                    Sbuf = SA if buf == 0 else SB
                    wts = []
                    if R >= 2:
                        wts.append(W("a_P", R - 1))
                    if r == 0 and ur == 0:
                        wts.append(W("v_kq", 12 * (gi + 1)))
                    p.step("tensor", lambda t, S_=Sbuf, k_=kqb, q_=qgb,
                           h_=h0, r_=r, qt_=qt: t.matmul(
                        S_[:, 0:512],
                        k_[32 * h_:32 * (h_ + 1), 128 * r_:128 * (r_ + 1)],
                        q_[32 * h_:32 * (h_ + 1), 512 * qt_:512 * (qt_ + 1)],
                        start=True, stop=True, tile_position=(32 * h_, 0),
                        skip_group_check=True), waits=wts)
                    p.step("tensor", lambda t, S_=Sbuf, k_=kqb, q_=qgb,
                           h_=h1, r_=r, qt_=qt: t.matmul(
                        S_[:, 512:1024],
                        k_[32 * h_:32 * (h_ + 1), 128 * r_:128 * (r_ + 1)],
                        q_[32 * h_:32 * (h_ + 1), 512 * qt_:512 * (qt_ + 1)],
                        start=True, stop=True, tile_position=(32 * h_, 0),
                        skip_group_check=True), inc="t_S")
                    ew = [W("t_S", R + 1)]
                    if r == 0:
                        ew.append(W("c_mg", gi + 1))
                    pdst = (P3B[R % 3] if u <= 55
                            else P_AB[:, 1024 * buf:1024 * (buf + 1)])
                    p.step("scalar", lambda s, S_=Sbuf, d_=pdst, g_=gi: s.activation(
                        d_, S_[:, :], AF.Exp,
                        bias=mneg[:, g_:g_ + 1]), waits=ew, inc="a_P")

                def plan_V(r):
                    R = u * NKC + r
                    psrc = (P3B[R % 3] if u <= 55
                            else P_AB[:, 1024 * (R % 2):1024 * (R % 2 + 1)])
                    vw = [W("a_P", R + 1)]
                    if r == 0:
                        vw.append(W("v_Oe", 2 * u))
                        if u == 0:
                            vw.append(W("t_lc", 2))
                    p.step("tensor", lambda t, r_=r, pr_=pr, p_=psrc: t.matmul(
                        O0[0:65, :],
                        v_all[:, 1040 * r_ + 65 * pr_:1040 * r_ + 65 * pr_ + 65],
                        p_[:, 0:512],
                        start=(r_ == 0), stop=(r_ == NKC - 1),
                        skip_group_check=True), waits=vw)
                    p.step("tensor", lambda t, r_=r, pr_=pr, p_=psrc: t.matmul(
                        O1[0:65, :],
                        v_all[:, 1040 * r_ + 65 * pr_:1040 * r_ + 65 * pr_ + 65],
                        p_[:, 512:1024],
                        start=(r_ == 0), stop=(r_ == NKC - 1),
                        skip_group_check=True),
                        inc="t_V" if r == NKC - 1 else None)

                vlag = 2 if u <= 55 else 1
                for r in range(NKC):
                    plan_S(r)
                    if gi + 1 < NG and (ur <= 5 or ur == 7):
                        if r == 1 and ur in (0, 4):
                            plan_proj_stage(gi + 1, 2 * ur)
                        if 5 <= r <= 12:
                            plan_proj_mm(gi + 1, 10 if ur == 5 else
                                         (11 if ur == 7 else 2 * ur), r - 5)
                        if 19 <= r <= 26 and ur <= 4:
                            plan_proj_mm(gi + 1, 2 * ur + 1, r - 19)
                    if r >= vlag:
                        plan_V(r - vlag)
                    if r == NKC - 1:
                        for rv in range(NKC - vlag, NKC):
                            plan_V(rv)
                    if gi + 1 < NG:
                        if (ur <= 5 or ur == 7) and r == 13:
                            plan_proj_evac(gi + 1, 10 if ur == 5 else
                                           (11 if ur == 7 else 2 * ur))
                        if ur <= 4 and r == 27:
                            plan_proj_evac(gi + 1, 2 * ur + 1)
                        if ur == 7 and r == 4:
                            mg_pending.extend(make_mg(gi + 1))
                        if ur == 7 and r >= 16 and mg_pending:
                            mg_pending.pop(0)()
                    if u >= 1:
                        pu = u - 1
                        if r == 2:
                            plan_AB(pu)
                        if r == 4:
                            plan_sq(pu)
                        if r == 6:
                            plan_ms(pu)
                        if r == 8:
                            plan_lnexp(pu)
                        if r == 9:
                            plan_lnbounce(pu)
                        if r == 12:
                            plan_C(pu)
                p.step("vector", lambda v: v.tensor_copy(o0s[0:65, :], O0[0:65, :]),
                       waits=[W("t_V", u + 1), W("a_c", 2 * u), W("t_C", u),
                              W("t_AB", u)], inc="v_Oe", dr=True)
                p.step("scalar", lambda s: s.activation(
                    o1s[0:65, :], O1[0:65, :], AF.Identity),
                    waits=[W("t_V", u + 1), W("t_C", u), W("t_AB", u)],
                    inc="v_Oe")
                p.step("vector", lambda v: v.tensor_tensor(
                    Abc[64:65, :], o0s[64:65, :], o1s[64:65, :], ALU.mult),
                    waits=[W("v_Oe", 2 * (u + 1))], inc="v_w64", dr=True)
                if u >= 1:
                    plan_fin(u - 1)
                p.step("vector", lambda v: v.tensor_tensor(
                    Abc[0:64, :], o0s[0:64, :], EPI[0:64, 0:512], ALU.mult),
                    waits=[W("t_AB", u + 1)], dr=True)
                p.step("vector", lambda v: v.scalar_tensor_tensor(
                    Bbc[0:64, :], o1s[0:64, :], lamcol[0:64, 0:1],
                    EPI[64:128, 0:512], ALU.mult, ALU.mult),
                    waits=[W("c_or", 2), W("t_ms", u),
                           W("d_fin", 16 * (p.n("d_fin") // 16))])
                p.step("vector", lambda v: v.tensor_tensor(
                    Abc[0:64, :], Abc[0:64, :], Bbc[0:64, :], ALU.add),
                    inc="v_ab", dr=True)
                if u >= 56:
                    plan_wo_stage(u - 56)
    plan_AB(NU - 1)
    plan_sq(NU - 1)
    plan_ms(NU - 1)
    plan_lnexp(NU - 1)
    plan_lnbounce(NU - 1)
    plan_C(NU - 1)
    plan_fin(NU - 1)

    n_wo = 0
    n_t5 = 0
    for qt in range(NQT):
        for oc in range(8):
            bank = PB[n_wo % 2]
            wts = [W("v_wc", 24 + oc + 1)]
            if n_wo == 0:
                wts += [W("a_P", NU * NKC),
                        W("v_fin", NU), W("d_fin", 16 * 32)]
            if n_wo >= 2:
                wts.append(W("v_woe", n_wo - 1))
            for cc in range(8):
                p.step("tensor", lambda t, b_=bank, cc_=cc, oc_=oc, qt_=qt: t.matmul(
                    b_[0:128, 0:512],
                    wobf[:, 1024 * cc_ + 128 * oc_:1024 * cc_ + 128 * (oc_ + 1)],
                    afin[:, 2048 * cc_ + 512 * qt_:2048 * cc_ + 512 * (qt_ + 1)],
                    start=(cc_ == 0), stop=(cc_ == 7), skip_group_check=True),
                    waits=wts if cc == 0 else [],
                    inc="t_wo" if cc == 7 else None)
            n_wo += 1
            p.step("vector", lambda v, b_=bank, oc_=oc: v.tensor_copy(
                woq[:, 512 * oc_:512 * (oc_ + 1)], b_[0:128, 0:512]),
                waits=[W("t_wo", n_wo)], inc="v_woe")
        for j in range(4):
            jg = qt * 4 + j
            xw = []
            if jg == 0:
                xw += [W("t_kq", 96), W("t_vp", 256)]
            if jg >= 2:
                xw.append(W("v_hp", 8 * (jg - 1)))
            p.dma("sync", lambda sy, jg_=jg: sy.dma_start(
                out=xs5[:, 1024 * (jg_ % 2):1024 * (jg_ % 2 + 1)],
                in_=dx[128 * jg_:128 * (jg_ + 1), :]), waits=xw,
                inc=f"d_x5{jg % 2}")
            for oc in range(8):
                bank = PB[4 + (n_t5 % 2)]
                wts = [W("v_woe", 8 * (qt + 1))]
                if n_t5 == 0:
                    wts.append(W("v_Oe", NU))
                if n_t5 >= 2:
                    wts.append(W("v_hp", p.n("v_hp") - 1))
                p.step("tensor", lambda t, b_=bank, oc_=oc, j_=j: t.transpose(
                    b_[0:128, 0:128],
                    woq[:, 512 * oc_ + 128 * j_:512 * oc_ + 128 * (j_ + 1)],
                    ident_f[:, :]), waits=wts, inc="t_tr5")
                n_t5 += 1
                hw = [W("t_tr5", n_t5), W(f"d_x5{jg % 2}", 16 * (jg // 2 + 1))]
                if oc == 0:
                    if jg == 0:
                        hw.append(W("t_V", NU))
                    if jg >= 2:
                        hw += [W(f"d_hs{jg % 2}", 16 * (jg // 2)),
                               W("v_ln2", 2 * (jg // 2))]
                p.step("vector", lambda v, b_=bank, oc_=oc, jg_=jg: v.tensor_tensor(
                    htile[:, 1024 * (jg_ % 2) + 128 * oc_:
                          1024 * (jg_ % 2) + 128 * (oc_ + 1)],
                    xs5[:, 1024 * (jg_ % 2) + 128 * oc_:
                        1024 * (jg_ % 2) + 128 * (oc_ + 1)],
                    b_[0:128, 0:128], ALU.add), waits=hw,
                    inc=[("v_hp", 1), ("v_h", 1)] if oc == 7 else "v_hp")
            p.dma("gpsimd", lambda gp, jg_=jg: gp.dma_start(
                out=d_h[128 * jg_:128 * (jg_ + 1), :],
                in_=htile[:, 1024 * (jg_ % 2):1024 * (jg_ % 2 + 1)]),
                waits=[W("v_h", jg + 1)], inc=f"d_hs{jg % 2}")
            ht = htile[:, 1024 * (jg % 2):1024 * (jg % 2 + 1)]
            pr2 = jg // 2
            b2 = 2 * (pr2 % 2)
            c2 = b2 + jg % 2
            sq_w = [W("v_h", jg + 1), W("v_wc", 32)]
            if jg % 2 == 0 and pr2 >= 2:
                sq_w.append(W("v_q85", pr2 - 1))
            p.step("scalar", lambda s, h_=ht, c_=c2: (s.drain(), s.activation(
                scr5[:, :], h_, AF.Square, accum_out=stat_q8[:, c_:c_ + 1]))[1],
                waits=sq_w, inc="a_sq5")
            p.step("vector", lambda v, h_=ht, c_=c2: v.reduce_sum(
                stat_s8[:, c_:c_ + 1], h_, AX.X),
                waits=[W("v_h", jg + 1)], dr=(jg % 2 == 0))
            if jg % 2 == 0:
                continue
            p.step("vector", lambda v, b_=b2: v.tensor_scalar(
                stat_m8[:, b_:b_ + 2], stat_s8[:, b_:b_ + 2],
                1.0 / 1024.0, None, ALU.mult), dr=True)
            p.step("vector", lambda v, b_=b2: v.tensor_scalar(
                stat_q8[:, b_:b_ + 2], stat_q8[:, b_:b_ + 2],
                1.0 / 1024.0, None, ALU.mult),
                waits=[W("a_sq5", jg + 1)], inc="v_q85")
            p.step("vector", lambda v, b_=b2: v.tensor_tensor(
                stat_r8[:, b_:b_ + 2], stat_m8[:, b_:b_ + 2],
                stat_m8[:, b_:b_ + 2], ALU.mult), dr=True)
            p.step("vector", lambda v, b_=b2: v.tensor_tensor(
                stat_r8[:, b_:b_ + 2], stat_q8[:, b_:b_ + 2],
                stat_r8[:, b_:b_ + 2], ALU.subtract), inc="v_var5", dr=True)
            sqw5 = [W("v_var5", pr2 + 1)]
            if pr2 >= 2:
                sqw5.append(W("v_ln2", 2 * (pr2 - 1)))
            p.step("scalar", lambda s, b_=b2: s.activation(
                stat_r8[:, b_:b_ + 2], stat_r8[:, b_:b_ + 2],
                AF.Sqrt, bias=epsb[:, 0:1]), waits=sqw5, inc="a_sd5")
            p.step("vector", lambda v, b_=b2: v.reciprocal(
                stat_r8[:, b_:b_ + 2], stat_r8[:, b_:b_ + 2]),
                waits=[W("a_sd5", pr2 + 1)], dr=True)
            for jj in (jg - 1, jg):
                hs = htile[:, 1024 * (jj % 2):1024 * (jj % 2 + 1)]
                p.step("vector", lambda v, h_=hs, jj_=jj, c_=b2 + jj % 2:
                       v.tensor_scalar(
                    ln2[:, 1024 * jj_:1024 * (jj_ + 1)], h_,
                    stat_m8[:, c_:c_ + 1], stat_r8[:, c_:c_ + 1],
                    ALU.subtract, ALU.mult), dr=(jj == jg - 1))
            for jj in (jg - 1, jg):
                p.step("vector", lambda v, jj_=jj: v.reduce_max(
                    amx16[:, jj_:jj_ + 1], ln2[:, 1024 * jj_:1024 * (jj_ + 1)],
                    AX.X, apply_absolute_value=True), inc="v_ln2",
                    dr=(jj == jg - 1))

    p.step("vector", lambda v: v.reduce_max(amax1[:, 0:1], amx16[:, :], AX.X),
           waits=[W("v_ln2", 16)], inc="v_am0", dr=True)
    p.step("tensor", lambda t: t.transpose(EPI[0:1, 0:128], amax1[:, 0:1],
                                           ident_f[:, :]),
           waits=[W("v_am0", 1)], inc="t_am1")
    p.step("vector", lambda v: v.tensor_copy(sc_row[0:1, :], EPI[0:1, 0:128]),
           waits=[W("t_am1", 1)])
    p.step("vector", lambda v: v.reduce_max(ccs[0:1, 0:1], sc_row[0:1, :], AX.X),
           inc="v_am1", dr=True)
    p.dma("gpsimd", lambda gp: gp.dma_start(out=d_cc[0], in_=ccs[:, :]),
          waits=[W("v_am1", 1)], inc="d_cs")
    p.step("gpsimd", lambda gp: gp.collective_compute(
        "AllReduce", ALU.max, replica_groups=[list(range(8))],
        ins=[d_cc[0]], outs=[d_cc[1]]),
        waits=[W("d_cs", 16)], inc="cc")
    p.dma("gpsimd", lambda gp: gp.dma_start(out=ccs[:, :], in_=d_cc[1]),
          waits=[W("cc", 1)], inc="d_ccr")
    p.step("vector", lambda v: v.reciprocal(sc_row[0:1, 0:1], ccs[0:1, 0:1]),
           waits=[W("d_ccr", 16)], dr=True)
    p.step("vector", lambda v: v.tensor_scalar(sc_row[0:1, 0:1], sc_row[0:1, 0:1],
                                               32767.0, None, ALU.mult), inc="v_s2",
           dr=True)
    p.step("tensor", lambda t: t.matmul(
        EPI[0:128, 0:1], ones_any[0:1, :], sc_row[0:1, 0:1],
        start=True, stop=True, skip_group_check=True),
        waits=[W("v_s2", 1), W("v_fin", NU)], inc="t_am1")
    p.step("vector", lambda v: v.tensor_copy(s2col[:, 0:1], EPI[0:128, 0:1]),
           waits=[W("t_am1", 2)], inc="g_s2")

    n_xt = 0
    for jg in range(16):
        wts = [W("g_s2", 1)]
        if jg == 0:
            wts += [W("d_hs0", 16 * 8), W("d_hs1", 16 * 8)]
        p.step("vector", lambda v, jg_=jg: v.tensor_scalar(
            xsq_i[:, :], ln2[:, 1024 * jg_:1024 * (jg_ + 1)], s2col[:, 0:1],
            None, ALU.mult), waits=wts, dr=True)
        p.step("vector", lambda v: v.tensor_copy(xsq_b[:, :], xsq_i[:, :]),
               inc="v_xq", dr=True)
        for ec in range(8):
            bank = PB[n_xt % 4].bitcast(BF16)
            wts = [W("v_xq", jg + 1)]
            if n_xt >= 4:
                wts.append(W("v_xqe", n_xt - 3))
            if n_xt == 0:
                wts.append(W("v_woe", 32))
            p.step("tensor", lambda t, b_=bank, e_=ec: t.transpose(
                b_[0:128, 0:128], xsq_b[:, 128 * e_:128 * (e_ + 1)], ident_b[:, :]),
                waits=wts, inc="t_xqt")
            p.step("vector", lambda v, b_=bank, e_=ec, jg_=jg: v.tensor_copy(
                xsqT[:, 2048 * e_ + 128 * jg_:2048 * e_ + 128 * (jg_ + 1)],
                b_[0:128, 0:128]), waits=[W("t_xqt", n_xt + 1)], inc="v_xqe")
            n_xt += 1

    for cb in range(32):
        b0 = 1024 * (cb % 2)
        for e in range(8):
            wts = []
            if e == 0:
                if cb == 0:
                    wts.append(W("v_h", 16))
                if cb >= 2:
                    wts.append(W("v_wq5d", 8 * (cb - 1)))
            p.dma("sync", lambda sy, e_=e, cb_=cb, b_=b0: sy.dma_start(
                out=wst5[:, b_ + 128 * e_:b_ + 128 * (e_ + 1)],
                in_=dwin[128 * e_:128 * (e_ + 1), 128 * cb_:128 * (cb_ + 1)]),
                waits=wts, inc=f"d_wi{cb % 2}")
        p.step("vector", lambda v, b_=b0: v.tensor_reduce(
            wsc5[:, 0:128],
            AP(tensor=wst5, offset=b_, ap=[[2048, 128], [1, 128], [128, 8]]),
            AX.X, ALU.max, apply_absolute_value=True),
            waits=[W(f"d_wi{cb % 2}", 128 * (cb // 2 + 1))],
            inc="v_wq5", dr=True)
        tw = [W("v_wq5", p.n("v_wq5"))]
        if cb == 0:
            tw.append(W("v_hp", 128))
        if cb >= 1:
            tw.append(W("v_wq5b", cb))
        p.step("tensor", lambda t: t.transpose(PB[4][0:128, 0:128], wsc5[:, 0:128],
                                               ident_f[:, :]), waits=tw, inc="t_wq5")
        p.step("vector", lambda v: v.reduce_max(sc_t[:, 0:1], PB[4][0:128, 0:128],
                                                AX.X),
               waits=[W("t_wq5", p.n("t_wq5"))], dr=True)
        p.step("vector", lambda v, cb_=cb: v.tensor_scalar(
            acolw[:, cb_:cb_ + 1], sc_t[:, 0:1], 1e-8, 1.0 / 127.0,
            ALU.max, ALU.mult),
            inc="v_wq5b" if cb % 2 == 0 else None, dr=True)
        if cb % 2 == 0:
            continue
        p.step("vector", lambda v, cb_=cb: v.reciprocal(
            sc_t2[:, 0:2], acolw[:, cb_ - 1:cb_ + 1]),
            inc="v_wq5b", dr=True)
        for tx, cbx in ((0, cb - 1), (1, cb)):
            bx0 = 1024 * (cbx % 2)
            t2w = [W("v_wq5b", cb + 1)]
            if cbx >= 1:
                t2w.append(W("v_wq5c", cbx))
            p.step("tensor", lambda t, t_=tx: t.transpose(
                PB[5][0:1, 0:128], sc_t2[:, t_:t_ + 1],
                ident_f[:, :]), waits=t2w, inc="t_wq5b")
            p.step("vector", lambda v: v.tensor_copy(sc_row[0:1, :],
                                                     PB[5][0:1, 0:128]),
                   waits=[W("t_wq5b", p.n("t_wq5b"))], inc="v_wq5c")
            gw = [W("v_wq5c", p.n("v_wq5c"))]
            if cbx >= 1:
                gw.append(W("v_wq5d", 8 * cbx))
            if cbx == 0:
                gw += [W("g_s2", 1), W("c_or", 2)]
            p.step("tensor", lambda t: t.matmul(
                EPI[0:128, 0:128], ones_any[0:1, :], sc_row[0:1, 0:128],
                start=True, stop=True, skip_group_check=True),
                waits=gw, inc="g_wq5")
            p.step("vector", lambda v: v.tensor_copy(scbb[:, :],
                                                     EPI[0:128, 0:128]),
                   waits=[W("g_wq5", cbx + 1)], dr=True)
            p.step("vector", lambda v, b_=bx0: v.tensor_tensor(
                wi32b[:, :], wst5[:, b_:b_ + 1024],
                AP(tensor=scbb, offset=0, ap=[[128, 128], [0, 8], [1, 128]]),
                ALU.mult), dr=True)
            p.step("vector", lambda v, cb_=cbx: v.tensor_copy(
                AP(tensor=win_i, offset=128 * cb_,
                   ap=[[32768, 128], [4096, 8], [1, 128]]),
                wi32b[:, :]), inc=[("v_wq5d", 8)], dr=True)

    p.step("vector", lambda v: v.reciprocal(sc_t[:, 0:1], s2col[:, 0:1]),
           waits=[W("g_s2", 1), W("v_wq5d", 256)], dr=True)
    p.step("vector", lambda v: v.tensor_scalar(gsc[:, :], acolw[:, :], sc_t[:, 0:1],
                                               None, ALU.mult), inc="v_gsc",
           dr=True)
    n_f1 = 0
    for fc in range(32):
        for qt in range(NQT):
            bank = PB[n_f1 % 4]
            wts = [W("v_wq5d", 256), W("v_xqe", 128)]
            if n_f1 >= 4:
                wts.append(W("a_gel", n_f1 - 3))
            for e in range(8):
                p.step("tensor", lambda t, b_=bank, e_=e, fc_=fc, qt_=qt: t.matmul(
                    b_[0:128, 0:512],
                    win_i[:, 4096 * e_ + 128 * fc_:4096 * e_ + 128 * (fc_ + 1)],
                    xsqT[:, 2048 * e_ + 512 * qt_:2048 * e_ + 512 * (qt_ + 1)],
                    start=(e_ == 0), stop=(e_ == 7), skip_group_check=True),
                    waits=wts if e == 0 else [],
                    inc="t_f1" if e == 7 else None)
            gw = [W("t_f1", n_f1 + 1), W("v_gsc", 1)]
            if fc >= 2 and qt == 0:
                gw.append(W(f"d_hd{fc % 2}", 16 * (fc // 2)))
            p.step("scalar", lambda s, b_=bank, fc_=fc, qt_=qt: s.activation(
                hidb[:, 2048 * (fc_ % 2) + 512 * qt_:
                     2048 * (fc_ % 2) + 512 * (qt_ + 1)],
                b_[0:128, 0:512], AF.Gelu, scale=gsc[:, fc_:fc_ + 1],
                bias=b_in[:, fc_:fc_ + 1]), waits=gw, inc="a_gel")
            n_f1 += 1
        p.step("vector", lambda v, fc_=fc: v.reduce_max(
            amx32[:, fc_:fc_ + 1], hidb[:, 2048 * (fc_ % 2):2048 * (fc_ % 2 + 1)],
            AX.X, apply_absolute_value=True), waits=[W("a_gel", 4 * (fc + 1))],
            inc="v_ha")
        p.dma("gpsimd", lambda gp, fc_=fc: gp.dma_start(
            out=d_hid[128 * fc_:128 * (fc_ + 1), :],
            in_=hidb[:, 2048 * (fc_ % 2):2048 * (fc_ % 2 + 1)]),
            waits=[W("v_ha", fc + 1)], inc=f"d_hd{fc % 2}")
        buf = wstF if fc % 2 == 0 else wstF2
        dw_w = []
        if fc == 0:
            dw_w += [W("a_sq5", 16), W("v_wc", 32)]
        if fc >= 2:
            dw_w.append(W("v_wo6", fc - 1))
        p.dma("sync", lambda sy, fc_=fc, b_=buf: sy.dma_start(
            out=b_[:, :], in_=dwout[128 * fc_:128 * (fc_ + 1), :]),
            waits=dw_w, inc=f"d_wo{fc % 2}")
        p.step("vector", lambda v, b_=buf: v.reduce_max(
            sc_t[:, 0:1], b_[:, :], AX.X, apply_absolute_value=True),
            waits=[W(f"d_wo{fc % 2}", 16 * (fc // 2 + 1))], dr=True)
        p.step("vector", lambda v, fc_=fc: v.tensor_scalar(
            rocol[:, fc_:fc_ + 1], sc_t[:, 0:1], 1e-8, 1.0 / 127.0,
            ALU.max, ALU.mult), dr=True)
        if fc % 2 == 0:
            continue
        p.step("vector", lambda v, fc_=fc: v.reciprocal(
            sc_t2[:, 0:2], rocol[:, fc_ - 1:fc_ + 1]), dr=True)
        for tx, fcx in ((0, fc - 1), (1, fc)):
            bufx = wstF if fcx % 2 == 0 else wstF2
            wq_w = []
            if fcx == 0:
                wq_w += [W("v_ln2", 16), W("d_hs0", 16 * 8), W("d_hs1", 16 * 8)]
            p.step("vector", lambda v, b_=bufx, t_=tx: v.tensor_scalar(
                wo_i32[:, :], b_[:, :], sc_t2[:, t_:t_ + 1], None, ALU.mult),
                waits=wq_w, dr=True)
            p.step("vector", lambda v, fc_=fcx: v.tensor_copy(
                wout_i[:, 1024 * fc_:1024 * (fc_ + 1)], wo_i32[:, :]),
                inc="v_wo6", dr=True)

    p.step("vector", lambda v: v.reduce_max(amax2[:, 0:1], amx32[:, :], AX.X),
           waits=[W("v_ha", 32)], inc="v_am0", dr=True)
    p.step("tensor", lambda t: t.transpose(EPI[0:1, 0:128], amax2[:, 0:1],
                                           ident_f[:, :]),
           waits=[W("v_am0", 2), W("v_am1", 1)], inc="t_am2")
    p.step("vector", lambda v: v.tensor_copy(sc_row[0:1, :], EPI[0:1, 0:128]),
           waits=[W("t_am2", 1)])
    p.step("vector", lambda v: v.reduce_max(ccs[0:1, 0:1], sc_row[0:1, :], AX.X),
           inc="v_am2", dr=True)
    p.dma("gpsimd", lambda gp: gp.dma_start(out=d_cc[2], in_=ccs[:, :]),
          waits=[W("v_am2", 1)], inc="d_cs")
    p.step("gpsimd", lambda gp: gp.collective_compute(
        "AllReduce", ALU.max, replica_groups=[list(range(8))],
        ins=[d_cc[2]], outs=[d_cc[3]]),
        waits=[W("d_cs", 32)], inc="cc")
    p.dma("gpsimd", lambda gp: gp.dma_start(out=ccs[:, :], in_=d_cc[3]),
          waits=[W("cc", 2)], inc="d_ccr")
    p.step("vector", lambda v: v.reciprocal(sc_row[0:1, 1:2], ccs[0:1, 0:1]),
           waits=[W("d_ccr", 32)], dr=True)
    p.step("vector", lambda v: v.tensor_scalar(sc_row[0:1, 1:2], sc_row[0:1, 1:2],
                                               32767.0, None, ALU.mult), inc="v_s3",
           dr=True)
    p.step("tensor", lambda t: t.matmul(
        EPI[0:128, 0:1], ones_any[0:1, :], sc_row[0:1, 1:2],
        start=True, stop=True, skip_group_check=True),
        waits=[W("v_s3", 1), W("v_am2", 1)], inc="t_am2")
    p.step("vector", lambda v: v.tensor_copy(s3col[:, 0:1], EPI[0:128, 0:1]),
           waits=[W("t_am2", 2)], inc="g_s3")
    p.step("vector", lambda v: v.reciprocal(sc_t[:, 0:1], s3col[:, 0:1]),
           waits=[W("g_s3", 1), W("v_wo6", 32)], dr=True)
    p.step("vector", lambda v: v.tensor_scalar(invh[:, :], rocol[:, :], sc_t[:, 0:1],
                                               None, ALU.mult), inc="v_gsc",
           dr=True)

    n_t7 = 0
    for qt in range(NQT):
        for fc in range(32):
            k = qt * 32 + fc
            wts = []
            if k == 0:
                wts += [W("d_hd0", 16 * 16), W("d_hd1", 16 * 16),
                        W("d_hs0", 16 * 8), W("d_hs1", 16 * 8),
                        W("g_s3", 1)]
            if k >= 2:
                wts.append(W("v_f2e", k - 1))
            p.dma("sync", lambda sy, fc_=fc, qt_=qt, k_=k: sy.dma_start(
                out=hraw7[:, 512 * (k_ % 2):512 * (k_ % 2 + 1)],
                in_=d_hid[128 * fc_:128 * (fc_ + 1), 512 * qt_:512 * (qt_ + 1)]),
                waits=wts, inc=f"d_h7{k % 2}")
            qw = [W(f"d_h7{k % 2}", 16 * (k // 2 + 1)), W("v_gsc", 2)]
            if k >= 2:
                qw.append(W("t_f2", 8 * (k - 1)))
            hqb = hq_i32 if k % 2 == 0 else hq7b
            p.step("vector", lambda v, k_=k, h_=hqb: v.tensor_scalar(
                h_[:, :], hraw7[:, 512 * (k_ % 2):512 * (k_ % 2 + 1)],
                s3col[:, 0:1], None, ALU.mult), waits=qw, dr=(k == 0))
            p.step("vector", lambda v, k_=k, fc_=fc, h_=hqb: v.tensor_scalar(
                rhs7[:, 512 * (k_ % 2):512 * (k_ % 2 + 1)], h_[:, :],
                invh[:, fc_:fc_ + 1], None, ALU.mult), inc="v_f2e", dr=True)
            mw = [W("v_f2e", k + 1)]
            if fc == 0:
                if qt == 0:
                    mw += [W("a_gel", 128), W("v_am2", 1), W("v_wq5c", 32)]
                else:
                    mw.append(W("v_f2o", 8 * qt))
            for oc in range(8):
                p.step("tensor", lambda t, oc_=oc, fc_=fc, k_=k: t.matmul(
                    PB[oc_][0:128, 0:512],
                    wout_i[:, 1024 * fc_ + 128 * oc_:1024 * fc_ + 128 * (oc_ + 1)],
                    rhs7[:, 512 * (k_ % 2):512 * (k_ % 2 + 1)],
                    start=(fc_ == 0), stop=(fc_ == 31), skip_group_check=True),
                    waits=mw if oc == 0 else [], inc="t_f2")
        for oc in range(8):
            ew = [W("t_f2", 8 * 32 * (qt + 1))]
            if qt == 0 and oc == 0:
                ew.append(W("t_f1", 128))
            p.step("vector", lambda v, oc_=oc: v.tensor_scalar(
                outTs[:, 512 * oc_:512 * (oc_ + 1)], PB[oc_][0:128, 0:512],
                b_out[:, oc_:oc_ + 1], None, ALU.add), waits=ew, inc="v_f2o")
        for j in range(4):
            jg = qt * 4 + j
            hw = []
            if jg >= 2:
                hw.append(W("v_out", 8 * (jg - 1)))
            p.dma("sync", lambda sy, jg_=jg: sy.dma_start(
                out=hin7[:, 1024 * (jg_ % 2):1024 * (jg_ % 2 + 1)],
                in_=d_h[128 * jg_:128 * (jg_ + 1), :]), waits=hw,
                inc=f"d_hn{jg % 2}")
            for oc in range(8):
                bank = PB[n_t7 % 2]
                wts = [W("v_f2o", 8 * (qt + 1))]
                if n_t7 >= 2:
                    wts.append(W("v_out", n_t7 - 1))
                p.step("tensor", lambda t, b_=bank, oc_=oc, j_=j: t.transpose(
                    b_[0:128, 0:128],
                    outTs[:, 512 * oc_ + 128 * j_:512 * oc_ + 128 * (j_ + 1)],
                    ident_f[:, :]), waits=wts, inc="t_tr7")
                n_t7 += 1
                ow = [W("t_tr7", n_t7), W(f"d_hn{jg % 2}", 16 * (jg // 2 + 1))]
                if oc == 0 and jg >= 2:
                    ow.append(W(f"d_o{jg % 2}", 16 * (jg // 2)))
                p.step("vector", lambda v, b_=bank, oc_=oc, jg_=jg: v.tensor_tensor(
                    outs7[:, 1024 * (jg_ % 2) + 128 * oc_:
                          1024 * (jg_ % 2) + 128 * (oc_ + 1)],
                    hin7[:, 1024 * (jg_ % 2) + 128 * oc_:
                         1024 * (jg_ % 2) + 128 * (oc_ + 1)],
                    b_[0:128, 0:128], ALU.add), waits=ow, inc="v_out")
            p.dma("gpsimd", lambda gp, jg_=jg: gp.dma_start(
                out=dout[128 * jg_:128 * (jg_ + 1), :],
                in_=outs7[:, 1024 * (jg_ % 2):1024 * (jg_ % 2 + 1)]),
                waits=[W("v_out", 8 * (jg + 1))], inc=f"d_o{jg % 2}")
    p.step("gpsimd", lambda gp: (gp.wait_ge(p.sem("d_o0"), 16 * 8),
                                 gp.wait_ge(p.sem("d_o1"), 16 * 8)) and None)

    with nc.Block() as block:
        @block.sync
        def _(sy):
            p.emit("sync", sy)

        @block.gpsimd
        def _(gp):
            p.emit("gpsimd", gp)

        @block.scalar
        def _(sc):
            p.emit("scalar", sc)

        @block.tensor
        def _(te):
            p.emit("tensor", te)

        @block.vector
        def _(ve):
            p.emit("vector", ve)

    return nc


_NC_CACHE = None


def kernel(L0_attn_query, L0_attn_Wq, L0_attn_Wk, L0_attn_Wv, L0_attn_Wo,
           L0_attn_lambda_q1, L0_attn_lambda_k1, L0_attn_lambda_q2,
           L0_attn_lambda_k2, L0_attn_subln_w, L0_ffn_W_in, L0_ffn_b_in,
           L0_ffn_W_out, L0_ffn_b_out):
    global _NC_CACHE
    f32 = np.float32
    x = np.asarray(L0_attn_query, f32)
    wqT = np.ascontiguousarray(np.asarray(L0_attn_Wq, f32).T)
    wkT = np.ascontiguousarray(np.asarray(L0_attn_Wk, f32).T)
    wvT = np.ascontiguousarray(np.asarray(L0_attn_Wv, f32).T)
    woT = np.ascontiguousarray(np.asarray(L0_attn_Wo, f32).T)
    win = np.ascontiguousarray(np.asarray(L0_ffn_W_in, f32))
    wout = np.ascontiguousarray(np.asarray(L0_ffn_W_out, f32))
    b_in_pp = np.ascontiguousarray(np.asarray(L0_ffn_b_in, f32).reshape(32, 128).T)
    b_out_pp = np.ascontiguousarray(np.asarray(L0_ffn_b_out, f32).reshape(8, 128).T)
    lam_r = np.ascontiguousarray(np.concatenate(
        [np.asarray(a, f32).ravel() for a in
         (L0_attn_lambda_q1, L0_attn_lambda_k1,
          L0_attn_lambda_q2, L0_attn_lambda_k2)])[None, :])
    subln_r = np.ascontiguousarray(np.asarray(L0_attn_subln_w, f32)[None, :])
    ident_bf = np.eye(128, dtype=ml_dtypes.bfloat16)
    ident_f = np.eye(128, dtype=f32)

    if _NC_CACHE is None:
        _NC_CACHE = build_nc()
    nc = _NC_CACHE

    in_maps = []
    for c in range(8):
        b, half = c // 2, c % 2
        xb = x[b]
        xp = np.ascontiguousarray(np.concatenate(
            [xb[TQ * half:TQ * (half + 1)], xb[TQ * (1 - half):TQ * (2 - half)]],
            axis=0))
        in_maps.append({
            "x": xp, "wqT": wqT, "wkT": wkT, "wvT": wvT, "woT": woT,
            "win": win, "wout": wout, "b_in_pp": b_in_pp, "b_out_pp": b_out_pp,
            "lam_r": lam_r, "subln_r": subln_r,
            "ident_bf": ident_bf, "ident_f": ident_f,
        })
    res = run_bass_kernel_spmd(nc, in_maps, core_ids=list(range(8)))
    out = np.empty((B, T, D), f32)
    for c in range(8):
        b, half = c // 2, c % 2
        out[b, TQ * half:TQ * (half + 1)] = res.results[c]["out"]
    return out


# version 22
# speedup vs baseline: 1.0358x; 1.0077x over previous
import math
import numpy as np
import ml_dtypes
import concourse.bass as bass
import concourse.dve_ops as _dve_ops
import concourse.mybir as mybir
from concourse.bass_types import AP
from concourse.bass_utils import run_bass_kernel_spmd

F32 = mybir.dt.float32
BF16 = mybir.dt.bfloat16
I32 = mybir.dt.int32
AF = mybir.ActivationFunctionType
ALU = mybir.AluOpType
AX = mybir.AxisListType

B, T, D = 4, 4096, 1024
TQ = 2048
DFF = 4096
NG = 8
NKC = 32
NQT = 4
NU = 64
LAMBDA_INIT = float(0.8 - 0.6 * math.exp(-0.3 * 12))
SCALING = float(32 ** -0.5)

BASE = 16512
OFF_A = BASE
OFF_B = OFF_A + 66560
OFF_C = OFF_B + 65536
OFF_D = OFF_C + 32768
OFF_E = OFF_D + 24576
OFF_F = OFF_E + 4096
OFF_G = OFF_F + 8192


class Plan:
    def __init__(self, nc):
        self.nc = nc
        self.steps = {k: [] for k in ("sync", "gpsimd", "scalar", "tensor", "vector")}
        self.cnt = {}
        self.sems = {}

    def sem(self, name):
        if name not in self.sems:
            self.sems[name] = self.nc.alloc_semaphore(name)
            self.cnt[name] = 0
        return self.sems[name]

    def n(self, name):
        self.sem(name)
        return self.cnt[name]

    def step(self, eng, fn, waits=(), inc=None, inc_n=1, dr=False):
        ws = [(self.sem(s), v) for (s, v) in waits if v > 0]
        if inc is None:
            incs = []
        elif isinstance(inc, str):
            incs = [(inc, inc_n)]
        else:
            incs = list(inc)
        self.steps[eng].append((fn, ws, [(self.sem(s), n) for s, n in incs], dr))
        for s, n in incs:
            self.cnt[s] += n

    def dma(self, eng, fn, waits=(), inc=None):
        self.step(eng, fn, waits, inc, 16)

    def emit(self, name, eng):
        for fn, ws, incs, dr in self.steps[name]:
            for s, v in ws:
                eng.wait_ge(s, v)
            if dr:
                eng.drain()
            ins = fn(eng)
            if incs:
                ins.then_inc(*incs[0])
                for s, n in incs[1:]:
                    eng.drain().then_inc(s, n)


def build_nc():
    nc = bass.Bass("TRN2", target_bir_lowering=False, monotonic_sem_count=0)

    dx = nc.dram_tensor("x", [T, D], F32, kind="ExternalInput").ap()
    dwqT = nc.dram_tensor("wqT", [D, D], F32, kind="ExternalInput").ap()
    dwkT = nc.dram_tensor("wkT", [D, D], F32, kind="ExternalInput").ap()
    dwvT = nc.dram_tensor("wvT", [D, D], F32, kind="ExternalInput").ap()
    dwoT = nc.dram_tensor("woT", [D, D], F32, kind="ExternalInput").ap()
    dwin = nc.dram_tensor("win", [D, DFF], F32, kind="ExternalInput").ap()
    dwout = nc.dram_tensor("wout", [DFF, D], F32, kind="ExternalInput").ap()
    dbin = nc.dram_tensor("b_in_pp", [128, 32], F32, kind="ExternalInput").ap()
    dbout = nc.dram_tensor("b_out_pp", [128, 8], F32, kind="ExternalInput").ap()
    dlam = nc.dram_tensor("lam_r", [1, 128], F32, kind="ExternalInput").ap()
    dsub = nc.dram_tensor("subln_r", [1, 64], F32, kind="ExternalInput").ap()
    dib = nc.dram_tensor("ident_bf", [128, 128], BF16, kind="ExternalInput").ap()
    dif = nc.dram_tensor("ident_f", [128, 128], F32, kind="ExternalInput").ap()
    dout = nc.dram_tensor("out", [TQ, D], F32, kind="ExternalOutput").ap()
    d_h = nc.dram_tensor("sp_h", [TQ, D], F32).ap()
    d_hid = nc.dram_tensor("sp_hid", [DFF, TQ], BF16).ap()
    d_cc = [nc.dram_tensor(f"cc{i}", [1, 8], F32).ap() for i in range(4)]

    def at(name, shape, dtype, off):
        return nc.alloc_sbuf_tensor_at(name, shape, dtype, offset=off)

    ln1 = at("ln1", [128, 32768], BF16, OFF_A)
    v_all = at("v_all", [128, 33280], BF16, OFF_A)
    ln2 = at("ln2", [128, 16384], F32, OFF_A)
    win_i = at("win_i", [128, 32768], BF16, OFF_A)
    ln1T = at("ln1T", [128, 32768], BF16, OFF_B)
    wobf = at("wobf", [128, 8192], BF16, OFF_B)
    wout_i = at("wout_i", [128, 32768], BF16, OFF_B)
    xs5 = at("xs5", [128, 4096], F32, OFF_B + 16384)
    xC = at("xC", [128, 4096], F32, OFF_C)
    afin = at("afin", [128, 16384], BF16, OFF_C)
    xsqT = at("xsqT", [128, 16384], BF16, OFF_C)
    outTs = at("outTs", [128, 4096], F32, OFF_C)
    kq0 = at("kq0", [128, 4096], BF16, OFF_D)
    kq1 = at("kq1", [128, 4096], BF16, OFF_D + 8192)
    qg0 = at("qg0", [128, 2048], BF16, OFF_D + 16384)
    qg1 = at("qg1", [128, 2048], BF16, OFF_D + 20480)
    woq = at("woq", [128, 4096], F32, OFF_D)
    htile = at("htile", [128, 2048], F32, OFF_D + 16384)
    wst5 = at("wst5", [128, 2048], F32, OFF_D)
    wsc5 = at("wsc5", [128, 256], F32, OFF_D + 8192)
    wi32 = at("wi32", [128, 128], I32, OFF_D + 9216)
    hidb = at("hidb", [128, 8192], BF16, OFF_D)
    wo_i32 = at("wo_i32", [128, 1024], I32, OFF_D + 16384)
    hin7 = at("hin7", [128, 4096], F32, OFF_D)
    outs7 = at("outs7", [128, 4096], F32, OFF_D + 8192)
    rhs7 = at("rhs7", [128, 1024], BF16, OFF_D + 16384)
    xsq_b = at("xsq_b", [128, 1024], BF16, OFF_D + 18432)
    xsq_i = at("xsq_i", [128, 1024], I32, OFF_D + 20480)
    P_AB = at("P_AB", [128, 2048], BF16, OFF_E)
    sc_row = at("sc_row", [1, 128], F32, OFF_E)
    hraw7 = at("hraw7", [128, 1024], BF16, OFF_E)
    hq_i32 = at("hq_i32", [128, 512], I32, OFF_E + 2048)
    wstF = at("wstF", [128, 1024], F32, OFF_F)
    wbf = at("wbf", [128, 1024], BF16, OFF_F + 4096)
    lamst = at("lamst", [1, 128], F32, OFF_F + 6144)
    sqs = at("sqs", [128, 512], BF16, OFF_F + 6144)
    scSC = at("scSC", [128, 128], F32, OFF_D + 9728)
    scr5 = at("scr5", [128, 1024], F32, OFF_F)
    wstF2 = at("wstF2", [128, 1024], F32, OFF_F + 4096)
    g = OFF_G
    def G(name, shape, dtype):
        nonlocal g
        t = at(name, shape, dtype, g)
        g += ((shape[1] * mybir.dt.size(dtype) + 31) // 32) * 32
        return t
    ident_b = G("ident_b", [128, 128], BF16)
    ident_f = G("ident_f", [128, 128], F32)
    ones64 = G("ones64", [64, 1], F32)
    lam_s = G("lam_s", [1, 8], F32)
    sub_r = G("sub_r", [1, 64], F32)
    ones_any = G("ones_any", [128, 128], F32)
    mneg = G("mneg", [128, 8], F32)

    IND = G("IND", [128, 4], BF16)
    mgq4a = G("mgq4a", [4, 1], F32)
    mgq4b = G("mgq4b", [4, 1], F32)
    mgq4t = G("mgq4t", [4, 1], F32)
    o0s = G("o0s", [128, 512], F32)
    o1s = G("o1s", [128, 512], F32)
    Abc = G("Abc", [128, 512], F32)
    Bbc = G("Bbc", [128, 512], F32)
    lamcol = G("lamcol", [128, 1], F32)
    epsb = G("epsb", [128, 1], F32)
    b_in = G("b_in", [128, 32], F32)
    b_out = G("b_out", [128, 8], F32)
    amax1 = G("amax1", [128, 1], F32)
    amax2 = amax1
    amx32 = G("amx32", [128, 32], F32)
    amx16 = amx32[0:128, 0:16]
    mg_q = amx32[0:1, 0:16]
    ccs = G("ccs", [1, 8], F32)
    s2col = G("s2col", [128, 1], F32)
    s3col = G("s3col", [128, 1], F32)
    gsc = G("gsc", [128, 32], F32)
    invh = G("invh", [128, 32], F32)
    acolw = G("acolw", [128, 32], F32)
    rocol = G("rocol", [128, 32], F32)
    sc_t = G("sc_t", [128, 1], F32)
    sc_t2 = G("sc_t2", [128, 2], F32)
    onesb = G("onesb", [128, 1], BF16)
    wi32b = at("wi32b", [128, 1024], I32, OFF_D + 10240)
    scbb = at("scbb", [128, 128], F32, OFF_D + 14336)
    hq7b = at("hq7b", [128, 512], I32, OFF_F)
    stat_s8 = amx32[0:128, 16:24]

    stat_m8 = invh[0:128, 0:8]
    stat_q8 = invh[0:128, 8:16]
    stat_r8 = invh[0:128, 16:24]
    assert g <= 229344, f"SBUF overflow {g}"

    SA = nc.alloc_psum_tensor("SA", [128, 1024], F32)
    SB = nc.alloc_psum_tensor("SB", [128, 1024], F32)
    O0 = nc.alloc_psum_tensor("O0", [128, 512], F32)
    O1 = nc.alloc_psum_tensor("O1", [128, 512], F32)
    EPI = nc.alloc_psum_tensor("EPI", [128, 512], F32)
    PRJ = nc.alloc_psum_tensor("PRJ", [128, 512], F32)
    PB = [SA[:, 0:512], SA[:, 512:1024], SB[:, 0:512], SB[:, 512:1024],
          O0[:, :], O1[:, :], EPI[:, :], PRJ[:, :]]

    P3B = [P_AB[:, 0:1024], P_AB[:, 1024:2048], afin[:, 15360:16384]]

    p = Plan(nc)
    W = lambda s, v: (s, v)

    def lnT_rhs(e, tt):
        return AP(tensor=ln1T, offset=1024 * (4 * tt) + 128 * e,
                  ap=[[32768, 128], [1024, 4], [1, 128]])

    for dst, src in ((ident_b[:, :], dib), (ident_f[:, :], dif),
                     (lamst[:, :], dlam), (sub_r[:, :], dsub),
                     (b_in[:, :], dbin), (b_out[:, :], dbout)):
        p.dma("sync", lambda sy, d=dst, s=src: sy.dma_start(out=d, in_=s), inc="d_c")
    p.step("vector", lambda v: v.memset(ones64[:, :], 1.0))
    p.step("vector", lambda v: v.memset(epsb[:, :], 1e-5))
    p.step("vector", lambda v: v.memset(amax1[:, :], 0.0))
    p.step("vector", lambda v: v.memset(ccs[:, :], 0.0))
    p.step("vector", lambda v: v.memset(IND[:, :], 0.0))
    for h in range(4):
        p.step("vector", lambda v, h_=h: v.memset(IND[32 * h_:32 * (h_ + 1),
                                                      h_:h_ + 1], 1.0), dr=True)
    p.step("vector", lambda v: v.tensor_tensor(mg_q[0:1, 0:8], lamst[0:1, 0:8],
                                               lamst[0:1, 0:8], ALU.mult),
           waits=[W("d_c", 16 * 6)])
    p.step("vector", lambda v: v.tensor_tensor(mg_q[0:1, 0:8], lamst[0:1, 0:32],
                                               lamst[0:1, 32:64], ALU.mult) if False
           else v.tensor_tensor(sc_row[0:1, 0:32], lamst[0:1, 0:32],
                                lamst[0:1, 32:64], ALU.mult))
    p.step("vector", lambda v: v.reduce_sum(lam_s[0:1, 4:5], sc_row[0:1, 0:32], AX.X), dr=True)
    p.step("vector", lambda v: v.tensor_tensor(sc_row[0:1, 0:32], lamst[0:1, 64:96],
                                               lamst[0:1, 96:128], ALU.mult), dr=True)
    p.step("vector", lambda v: v.reduce_sum(lam_s[0:1, 5:6], sc_row[0:1, 0:32], AX.X),
           inc="c_dots", dr=True)
    p.step("scalar", lambda s: s.activation(lam_s[0:1, 4:5], lam_s[0:1, 4:5], AF.Exp),
           waits=[W("c_dots", 1)])
    p.step("scalar", lambda s: (s.drain(), s.activation(
        lam_s[0:1, 5:6], lam_s[0:1, 5:6], AF.Exp))[1], inc="c_dots")
    p.step("vector", lambda v: v.tensor_tensor(lam_s[0:1, 0:1], lam_s[0:1, 4:5],
                                               lam_s[0:1, 5:6], ALU.subtract),
           waits=[W("c_dots", 2)])
    p.step("vector", lambda v: v.tensor_scalar(lam_s[0:1, 0:1], lam_s[0:1, 0:1],
                                               LAMBDA_INIT, None, ALU.add), dr=True)
    p.step("vector", lambda v: v.tensor_scalar(lam_s[0:1, 1:2], lam_s[0:1, 0:1],
                                               -1.0, None, ALU.mult), inc="c_lam", dr=True)
    p.step("vector", lambda v: v.tensor_scalar(sub_r[0:1, :], sub_r[0:1, :],
                                               1.0 - LAMBDA_INIT, None, ALU.mult),
           inc="c_lam")
    p.step("vector", lambda v: v.memset(ones_any[:, :], 1.0), inc="c_or")
    p.step("vector", lambda v: v.memset(onesb[:, :], 1.0))
    p.step("tensor", lambda t: t.matmul(
        PRJ[0:128, 0:1], ones_any[0:1, :], lam_s[0:1, 1:2],
        start=True, stop=True, skip_group_check=True),
        waits=[W("c_lam", 1), W("c_or", 1)], inc="t_lc")
    p.step("vector", lambda v: v.tensor_copy(lamcol[:, 0:1], PRJ[0:128, 0:1]),
           waits=[W("t_lc", 1)], inc=[("c_or", 1), ("v_prj", 1)])

    for i in range(32):
        g_, j_ = i // 4, i % 4
        b4 = 4 * (g_ % 2)
        xt = xC[:, 1024 * j_:1024 * (j_ + 1)]
        lt = ln1[:, 1024 * i:1024 * (i + 1)]
        wts = [W("v_ln", i - 3), W("a_sq1", i - 3)] if i >= 4 else []
        xsem = f"d_x{i % 4}"
        p.dma("sync", lambda sy, d=xt, r0=128 * i: sy.dma_start(
            out=d, in_=dx[r0:r0 + 128, :]), waits=wts, inc=xsem)
        sq_w = [W(xsem, 16 * (g_ + 1))]
        if j_ == 0 and g_ >= 2:
            sq_w.append(W("v_q8", g_ - 1))
        p.step("scalar", lambda s, x_=xt, c_=b4 + j_: (s.drain(), s.activation(
            scr5[:, :], x_, AF.Square, accum_out=stat_q8[:, c_:c_ + 1]))[1],
            waits=sq_w, inc="a_sq1")
        p.step("vector", lambda v, x_=xt, c_=b4 + j_: v.reduce_sum(
            stat_s8[:, c_:c_ + 1], x_, AX.X),
            waits=[W(xsem, 16 * (g_ + 1))], dr=(j_ == 0))
        if j_ < 3:
            continue
        p.step("vector", lambda v, b_=b4: v.tensor_scalar(
            stat_m8[:, b_:b_ + 4], stat_s8[:, b_:b_ + 4],
            1.0 / 1024.0, None, ALU.mult), dr=True)
        p.step("vector", lambda v, b_=b4: v.tensor_scalar(
            stat_q8[:, b_:b_ + 4], stat_q8[:, b_:b_ + 4],
            1.0 / 1024.0, None, ALU.mult),
            waits=[W("a_sq1", 4 * (g_ + 1))], inc="v_q8")
        p.step("vector", lambda v, b_=b4: v.tensor_tensor(
            stat_r8[:, b_:b_ + 4], stat_m8[:, b_:b_ + 4],
            stat_m8[:, b_:b_ + 4], ALU.mult), dr=True)
        p.step("vector", lambda v, b_=b4: v.tensor_tensor(
            stat_r8[:, b_:b_ + 4], stat_q8[:, b_:b_ + 4],
            stat_r8[:, b_:b_ + 4], ALU.subtract), inc="v_var", dr=True)
        sqw = [W("v_var", g_ + 1)]
        if g_ >= 2:
            sqw.append(W("v_ln", 4 * (g_ - 1)))
        p.step("scalar", lambda s, b_=b4: s.activation(
            stat_r8[:, b_:b_ + 4], stat_r8[:, b_:b_ + 4],
            AF.Sqrt, bias=epsb[:, 0:1]), waits=sqw, inc="a_sd")
        p.step("vector", lambda v, b_=b4: v.reciprocal(
            stat_r8[:, b_:b_ + 4], stat_r8[:, b_:b_ + 4]),
            waits=[W("a_sd", g_ + 1)], dr=True)
        for jj in range(4):
            ii = 4 * g_ + jj
            p.step("vector", lambda v, jj_=jj, ii_=ii, c_=b4 + jj: v.tensor_scalar(
                ln1[:, 1024 * ii_:1024 * (ii_ + 1)],
                xC[:, 1024 * jj_:1024 * (jj_ + 1)],
                stat_m8[:, c_:c_ + 1], stat_r8[:, c_:c_ + 1],
                ALU.subtract, ALU.mult), inc="v_ln", dr=(jj == 0))
        for it in range(4 * g_, 4 * g_ + 4):
            for e in range(8):
                t_id = it * 8 + e
                bank = PB[t_id % 8].bitcast(BF16)
                wts = [W("v_ln", it + 1)]
                if t_id >= 8:
                    prev = t_id - 8
                    wts.append(W("v_e2" if prev % 2 == 0 else "a_e2",
                                 prev // 2 + 1))
                p.step("tensor", lambda t, b_=bank, i_=it, e_=e: t.transpose(
                    b_[0:128, 0:128],
                    ln1[:, 1024 * i_ + 128 * e_:1024 * i_ + 128 * (e_ + 1)],
                    ident_b[:, :]), waits=wts, inc="t_tr2")
                dst = ln1T[:, 1024 * it + 128 * e:1024 * it + 128 * (e + 1)]
                if t_id % 2 == 0:
                    p.step("vector", lambda v, d_=dst, b_=bank: v.tensor_copy(
                        d_, b_[0:128, 0:128]), waits=[W("t_tr2", t_id + 1)],
                        inc="v_e2")
                else:
                    p.step("scalar", lambda s, d_=dst, b_=bank: s.activation(
                        d_, b_[0:128, 0:128], AF.Identity),
                        waits=[W("t_tr2", t_id + 1)], inc="a_e2")

    n_wc = 0
    n_vp = 0
    for sb in range(8):
        for e in range(8):
            wts = []
            if e == 0:
                wts.append(W("a_sq1", 32))
                if sb >= 1:
                    wts.append(W("v_wc", n_wc))
            p.dma("sync", lambda sy, e_=e, c0=sb * 128: sy.dma_start(
                out=wstF[:, 128 * e_:128 * (e_ + 1)],
                in_=dwvT[128 * e_:128 * (e_ + 1), c0:c0 + 128]),
                waits=wts, inc="d_w")
        cast_w = [W("d_w", 16 * p.n("d_w") // 16 if False else 16 * (p.n("d_w") // 16))]
        if sb >= 1:
            cast_w.append(W("t_vp", n_vp))
        p.step("vector", lambda v: v.tensor_copy(wbf[:, :], wstF[:, :]),
               waits=cast_w, inc="v_wc")
        n_wc += 1
        for ct in range(32):
            bank = PB[n_vp % 4]
            wts = [W("v_wc", n_wc), W("t_tr2", min(256, 8 * (ct + 1)))]
            if n_vp < 4:
                wts += [W("v_e2", 128), W("a_e2", 128)]
            if n_vp >= 4:
                wts.append(W("v_vp", n_vp - 3))
            for e in range(8):
                p.step("tensor", lambda t, b_=bank, e_=e, ct_=ct: t.matmul(
                    b_[0:128, 0:128],
                    ln1T[:, 1024 * ct_ + 128 * e_:1024 * ct_ + 128 * (e_ + 1)],
                    wbf[:, 128 * e_:128 * (e_ + 1)],
                    start=(e_ == 0), stop=(e_ == 7), skip_group_check=True),
                    waits=wts if e == 0 else [],
                    inc="t_vp" if e == 7 else None)
            dst = AP(tensor=v_all, offset=1040 * ct + 65 * (2 * sb),
                     ap=[[33280, 128], [65, 2], [1, 64]])
            p.step("vector", lambda v, d_=dst, b_=bank: v.tensor_copy(
                d_, b_[0:128, 0:128]),
                waits=[W("t_vp", n_vp + 1), W("t_tr2", min(256, 8 * (ct + 3)))],
                inc="v_vp")
            n_vp += 1
    vones = AP(tensor=v_all, offset=64, ap=[[33280, 128], [1040, 32], [65, 16]])
    p.step("vector", lambda v: v.memset(vones, 1.0),
           waits=[W("t_tr2", 256), W("v_vp", 256)], inc="t_lc")

    n_kq = 0
    proj_state = {}

    def plan_proj_chunk(gi, j):
        nonlocal n_wc, n_kq
        kqb = kq0 if gi % 2 == 0 else kq1
        qgb = qg0 if gi % 2 == 0 else qg1
        wsel = 0 if j < 8 else 1
        tt = j if j < 8 else j - 8
        if (wsel == 0 and tt == 0) or (wsel == 1 and tt == 0):
            dsrc = dwkT if wsel == 0 else dwqT
            for e in range(8):
                wts = []
                if e == 0:
                    wts.append(W("v_wc", n_wc))
                p.dma("sync", lambda sy, e_=e, c0=gi * 128, ds=dsrc: sy.dma_start(
                    out=wstF[:, 128 * e_:128 * (e_ + 1)],
                    in_=ds[128 * e_:128 * (e_ + 1), c0:c0 + 128]),
                    waits=wts, inc="d_w")
            cw = [W("d_w", 16 * (p.n("d_w") // 16))]
            if n_kq >= 1 or gi >= 1 or wsel == 1:
                cw.append(W("t_kq", n_kq))
            else:
                cw.append(W("t_vp", 256))
            p.step("vector", lambda v: v.tensor_copy(wbf[:, :], wstF[:, :]),
                   waits=cw, inc="v_wc")
            n_wc += 1
        wts = [W("v_wc", n_wc), W("v_prj", p.n("v_prj"))]
        for e in range(8):
            rhs = lnT_rhs(e, tt)
            p.step("tensor", lambda t, e_=e, r_=rhs: t.matmul(
                PRJ[:, :], wbf[:, 128 * e_:128 * (e_ + 1)], r_,
                start=(e_ == 0), stop=(e_ == 7), skip_group_check=True),
                waits=wts if e == 0 else [],
                inc="t_kq" if e == 7 else None)
        dst = (kqb[:, 512 * tt:512 * (tt + 1)] if wsel == 0
               else qgb[:, 512 * tt:512 * (tt + 1)])
        if wsel == 0:
            p.step("vector", lambda v, d_=dst: v.tensor_copy(d_, PRJ[:, :]),
                   waits=[W("t_kq", n_kq + 1)], inc=[("v_kq", 1), ("v_prj", 1)])
        else:
            p.step("vector", lambda v, d_=dst: v.tensor_scalar(
                d_, PRJ[:, :], SCALING, None, ALU.mult),
                waits=[W("t_kq", n_kq + 1)], inc=[("v_kq", 1), ("v_prj", 1)])
        n_kq += 1

    def plan_proj_stage(gi, j):
        nonlocal n_wc
        wsel = 0 if j < 8 else 1
        dsrc = dwkT if wsel == 0 else dwqT
        for e in range(8):
            wts = []
            if e == 0:
                wts.append(W("v_wc", n_wc))
            p.dma("sync", lambda sy, e_=e, c0=gi * 128, ds=dsrc: sy.dma_start(
                out=wstF[:, 128 * e_:128 * (e_ + 1)],
                in_=ds[128 * e_:128 * (e_ + 1), c0:c0 + 128]),
                waits=wts, inc="d_w")
        cw = [W("d_w", 16 * (p.n("d_w") // 16)), W("t_kq", n_kq)]
        p.step("vector", lambda v: v.tensor_copy(wbf[:, :], wstF[:, :]),
               waits=cw, inc="v_wc")
        n_wc += 1

    def plan_proj_mm(gi, j, e):
        tt = j if j < 8 else j - 8
        rhs = lnT_rhs(e, tt)
        wts = [W("v_wc", n_wc), W("v_prj", p.n("v_prj"))] if e == 0 else []
        p.step("tensor", lambda t, e_=e, r_=rhs: t.matmul(
            PRJ[:, :], wbf[:, 128 * e_:128 * (e_ + 1)], r_,
            start=(e_ == 0), stop=(e_ == 7), skip_group_check=True),
            waits=wts, inc="t_kq" if e == 7 else None)

    def plan_proj_evac(gi, j):
        nonlocal n_kq
        kqb = kq0 if gi % 2 == 0 else kq1
        qgb = qg0 if gi % 2 == 0 else qg1
        wsel = 0 if j < 8 else 1
        tt = j if j < 8 else j - 8
        dst = (kqb[:, 512 * tt:512 * (tt + 1)] if wsel == 0
               else qgb[:, 512 * tt:512 * (tt + 1)])
        if wsel == 0:
            p.step("vector", lambda v, d_=dst: v.tensor_copy(d_, PRJ[:, :]),
                   waits=[W("t_kq", n_kq + 1)], inc=[("v_kq", 1), ("v_prj", 1)])
        else:
            p.step("vector", lambda v, d_=dst: v.tensor_scalar(
                d_, PRJ[:, :], SCALING, None, ALU.mult),
                waits=[W("t_kq", n_kq + 1)], inc=[("v_kq", 1), ("v_prj", 1)])
        n_kq += 1

    def make_mg(gi):
        kqb = kq0 if gi % 2 == 0 else kq1
        qgb = qg0 if gi % 2 == 0 else qg1
        items = []

        def init():
            p.step("vector", lambda v: v.memset(mgq4a[:, :], 0.0),
                   waits=[W("v_kq", 12 * (gi + 1))])
            p.step("vector", lambda v: v.memset(mgq4b[:, :], 0.0))
        items.append(init)
        chunks = ([(qgb, c, mgq4a) for c in range(4)] +
                  [(kqb, c, mgq4b) for c in range(8)])
        for (srcb, c, acc) in chunks:
            def mk(srcb=srcb, c=c, acc=acc):
                p.step("vector", lambda v: v.tensor_tensor(
                    sqs[:, :], srcb[:, 512 * c:512 * (c + 1)],
                    srcb[:, 512 * c:512 * (c + 1)], ALU.mult),
                    waits=[W("t_mq", p.n("t_mq"))], inc="v_sq", dr=True)
                p.step("tensor", lambda t: t.matmul(
                    PRJ[0:4, 0:512], IND[:, :], sqs[:, :],
                    start=True, stop=True, skip_group_check=True),
                    waits=[W("v_sq", p.n("v_sq")), W("v_prj", p.n("v_prj"))],
                    inc="t_mq")
                p.step("vector", lambda v: v.reduce_max(
                    mgq4t[:, 0:1], PRJ[0:4, 0:512], AX.X),
                    waits=[W("t_mq", p.n("t_mq"))], inc=[("v_prj", 1)], dr=True)
                p.step("vector", lambda v, a_=acc: v.tensor_tensor(
                    a_[:, 0:1], a_[:, 0:1], mgq4t[:, 0:1], ALU.max), inc="v_mq",
                    dr=True)
            items.append(mk)

        def fin1():
            p.step("vector", lambda v: v.tensor_tensor(
                mgq4a[:, 0:1], mgq4a[:, 0:1], mgq4b[:, 0:1], ALU.mult),
                inc="v_mq", dr=True)
        items.append(fin1)

        def fin2():
            p.step("tensor", lambda t: t.transpose(
                PRJ[0:1, 0:4], mgq4a[0:4, 0:1], ident_f[0:4, 0:4]),
                waits=[W("v_mq", p.n("v_mq")), W("v_prj", p.n("v_prj"))],
                inc="t_mq")
            p.step("vector", lambda v: v.tensor_copy(mg_q[0:1, 0:4],
                                                     PRJ[0:1, 0:4]),
                   waits=[W("t_mq", p.n("t_mq"))], inc=[("v_prj", 1)])
            p.step("vector", lambda v: v.reduce_max(mg_q[0:1, 12:13],
                                                    mg_q[0:1, 0:4], AX.X),
                   inc="v_mg2", dr=True)
        items.append(fin2)

        def fin3():
            p.step("scalar", lambda s: s.activation(mg_q[0:1, 12:13],
                                                    mg_q[0:1, 12:13], AF.Ln),
                   waits=[W("v_mg2", p.n("v_mg2"))])
            p.step("scalar", lambda s: (s.drain(), s.activation(
                mg_q[0:1, 12:13], mg_q[0:1, 12:13], AF.Exp, scale=0.5))[1],
                inc="a_mg")
            nw = [W("a_mg", p.n("a_mg"))]
            if gi >= 1:
                nw.append(W("t_mgb", gi))
            p.step("vector", lambda v: v.tensor_scalar(
                mg_q[0:1, 13:14], mg_q[0:1, 12:13], -15.0, 0.0,
                ALU.add, ALU.max), waits=nw, dr=True)
            p.step("vector", lambda v: v.tensor_scalar(
                lam_s[0:1, 2:3], mg_q[0:1, 13:14], -1.0, None, ALU.mult),
                inc="v_mg3", dr=True)
            p.step("tensor", lambda t: t.matmul(
                PRJ[0:128, 0:1], ones_any[0:1, :], lam_s[0:1, 2:3],
                start=True, stop=True, skip_group_check=True),
                waits=[W("v_mg3", p.n("v_mg3")), W("v_prj", p.n("v_prj")),
                       W("c_or", 1)],
                inc="t_mgb")
            p.step("vector", lambda v, gi_=gi: v.tensor_copy(
                mneg[:, gi_:gi_ + 1], PRJ[0:128, 0:1]),
                waits=[W("t_mgb", p.n("t_mgb"))], inc=[("c_mg", 1), ("v_prj", 1)])
        items.append(fin3)
        return items

    for j in range(12):
        plan_proj_chunk(0, j)
    for it in make_mg(0):
        it()
    mg_pending = []

    finst = Bbc.bitcast(BF16)[0:64, 512:1024]
    Bbc_bf = Bbc.bitcast(BF16)[0:128, 0:512]

    def plan_fin(u):
        pr, qt = u // 4, u % 4
        cols = slice(2048 * (pr // 2) + 512 * qt, 2048 * (pr // 2) + 512 * (qt + 1))
        if pr % 2 == 0:
            p.step("vector", lambda v, c_=cols: v.tensor_tensor(
                afin[0:64, c_], Abc[0:64, :], EPI[0:64, 0:512], ALU.mult),
                waits=[W("t_C", u + 1)], inc="v_fin")
        else:
            p.step("vector", lambda v: v.tensor_tensor(
                finst, Abc[0:64, :], EPI[0:64, 0:512], ALU.mult),
                waits=[W("t_C", u + 1)], inc="v_fin")
            p.dma("gpsimd", lambda gp, c_=cols: gp.dma_start(
                out=afin[64:128, c_], in_=finst),
                waits=[W("v_fin", p.n("v_fin"))], inc="d_fin")

    SQ64E = 64.0e-5 ** 0.5

    def plan_sq(pu):
        p.step("scalar", lambda s: s.activation(
            Bbc_bf[64:65, :], Abc[64:65, :], AF.Square, scale=SQ64E),
            waits=[W("v_w64", 2 * pu + 1), W("t_ms", pu)], inc="a_e")
        p.step("scalar", lambda s: s.activation(
            Bbc_bf[0:64, :], Abc[0:64, :], AF.Square),
            waits=[W("v_ab", pu + 1), W("t_ms", pu)], inc="a_e")

    def plan_AB(pu):
        p.step("tensor", lambda t: t.matmul(
            EPI[0:64, 0:512], ones_any[64:65, 0:64], o1s[64:65, :],
            start=True, stop=True, tile_position=(64, 0),
            skip_group_check=True),
            waits=[W("v_Oe", 2 * (pu + 1)), W("v_fin", p.n("v_fin")),
                   W("d_fin", 16 * (p.n("d_fin") // 16)), W("c_or", 1)])
        p.step("tensor", lambda t: t.matmul(
            EPI[64:128, 0:512], ones_any[64:65, 0:64], o0s[64:65, :],
            start=True, stop=True, tile_position=(64, 64),
            skip_group_check=True), inc="t_AB")

    def plan_ms(pu):
        p.step("tensor", lambda t: t.matmul(
            EPI[0:1, 0:512], onesb[0:65, 0:1], Bbc_bf[0:65, :],
            start=True, stop=True, skip_group_check=True),
            waits=[W("a_e", 2 * (pu + 1))], inc="t_ms")

    def plan_lnexp(pu):
        p.step("scalar", lambda s: s.activation(
            o0s[0:1, :], EPI[0:1, 0:512], AF.Ln,
            scale=1.0 / 64.0),
            waits=[W("t_ms", pu + 1)], inc="a_c")
        p.step("scalar", lambda s: s.activation(
            o1s[0:1, :], o0s[0:1, :], AF.Exp, scale=-0.5),
            waits=[W("v_w64", 2 * pu + 2)], inc="a_c")

    def plan_lnbounce(pu):
        p.step("vector", lambda v: v.memset(lam_s[0:1, 6:7], 0.0),
               waits=[W("a_c", 2 * pu + 1)], inc="v_w64")

    def plan_C(pu):
        p.step("tensor", lambda t: t.matmul(
            EPI[0:64, 0:512], sub_r[0:1, 0:64], o1s[0:1, :],
            start=True, stop=True, skip_group_check=True),
            waits=[W("a_c", 2 * (pu + 1)), W("v_ab", pu + 1), W("c_lam", 2)],
            inc="t_C")

    def plan_wo_stage(oc):
        nonlocal n_wc
        for e in range(8):
            wts = [W("v_wc", n_wc)] if e == 0 else []
            p.dma("sync", lambda sy, e_=e, oc_=oc: sy.dma_start(
                out=wstF[:, 128 * e_:128 * (e_ + 1)],
                in_=dwoT[128 * e_:128 * (e_ + 1), 128 * oc_:128 * (oc_ + 1)]),
                waits=wts, inc="d_w")
        wts = [W("d_w", 16 * (p.n("d_w") // 16))]
        if oc == 0:
            wts += [W("t_kq", 96), W("t_vp", 256)]
        p.step("vector", lambda v, oc_=oc: v.tensor_copy(
            AP(tensor=wobf, offset=128 * oc_, ap=[[8192, 128], [1024, 8], [1, 128]]),
            wstF[:, :]), waits=wts, inc="v_wc")
        n_wc += 1

    for gi in range(NG):
        for pi in range(2):
            pr = gi * 2 + pi
            for qt in range(NQT):
                u = pr * 4 + qt
                ur = u - 8 * gi
                kqb = kq0 if gi % 2 == 0 else kq1
                qgb = qg0 if gi % 2 == 0 else qg1
                h0, h1 = 2 * pi, 2 * pi + 1
                def plan_S(r):
                    R = u * NKC + r
                    buf = R % 2
                    Sbuf = SA if buf == 0 else SB
                    wts = []
                    if R >= 2:
                        wts.append(W("a_P", R - 1))
                    if r == 0 and ur == 0:
                        wts.append(W("v_kq", 12 * (gi + 1)))
                    p.step("tensor", lambda t, S_=Sbuf, k_=kqb, q_=qgb,
                           h_=h0, r_=r, qt_=qt: t.matmul(
                        S_[:, 0:512],
                        k_[32 * h_:32 * (h_ + 1), 128 * r_:128 * (r_ + 1)],
                        q_[32 * h_:32 * (h_ + 1), 512 * qt_:512 * (qt_ + 1)],
                        start=True, stop=True, tile_position=(32 * h_, 0),
                        skip_group_check=True), waits=wts)
                    p.step("tensor", lambda t, S_=Sbuf, k_=kqb, q_=qgb,
                           h_=h1, r_=r, qt_=qt: t.matmul(
                        S_[:, 512:1024],
                        k_[32 * h_:32 * (h_ + 1), 128 * r_:128 * (r_ + 1)],
                        q_[32 * h_:32 * (h_ + 1), 512 * qt_:512 * (qt_ + 1)],
                        start=True, stop=True, tile_position=(32 * h_, 0),
                        skip_group_check=True), inc="t_S")
                    ew = [W("t_S", R + 1)]
                    if r == 0:
                        ew.append(W("c_mg", gi + 1))
                    pdst = (P3B[R % 3] if u <= 55
                            else P_AB[:, 1024 * buf:1024 * (buf + 1)])
                    p.step("scalar", lambda s, S_=Sbuf, d_=pdst, g_=gi: s.activation(
                        d_, S_[:, :], AF.Exp,
                        bias=mneg[:, g_:g_ + 1]), waits=ew, inc="a_P")

                def plan_V(r):
                    R = u * NKC + r
                    psrc = (P3B[R % 3] if u <= 55
                            else P_AB[:, 1024 * (R % 2):1024 * (R % 2 + 1)])
                    vw = [W("a_P", R + 1)]
                    if r == 0:
                        vw.append(W("v_Oe", 2 * u))
                        if u == 0:
                            vw.append(W("t_lc", 2))
                    p.step("tensor", lambda t, r_=r, pr_=pr, p_=psrc: t.matmul(
                        O0[0:65, :],
                        v_all[:, 1040 * r_ + 65 * pr_:1040 * r_ + 65 * pr_ + 65],
                        p_[:, 0:512],
                        start=(r_ == 0), stop=(r_ == NKC - 1),
                        skip_group_check=True), waits=vw)
                    p.step("tensor", lambda t, r_=r, pr_=pr, p_=psrc: t.matmul(
                        O1[0:65, :],
                        v_all[:, 1040 * r_ + 65 * pr_:1040 * r_ + 65 * pr_ + 65],
                        p_[:, 512:1024],
                        start=(r_ == 0), stop=(r_ == NKC - 1),
                        skip_group_check=True),
                        inc="t_V" if r == NKC - 1 else None)

                vlag = 2 if u <= 55 else 1
                for r in range(NKC):
                    plan_S(r)
                    if gi + 1 < NG and ur <= 6:
                        if r == 1 and ur in (0, 4):
                            plan_proj_stage(gi + 1, 2 * ur)
                        if 5 <= r <= 12:
                            plan_proj_mm(gi + 1, 10 if ur == 5 else
                                         (11 if ur == 6 else 2 * ur), r - 5)
                        if 19 <= r <= 26 and ur <= 4:
                            plan_proj_mm(gi + 1, 2 * ur + 1, r - 19)
                    if r >= vlag:
                        plan_V(r - vlag)
                    if r == NKC - 1:
                        for rv in range(NKC - vlag, NKC):
                            plan_V(rv)
                    if gi + 1 < NG:
                        if ur <= 6 and r == 13:
                            plan_proj_evac(gi + 1, 10 if ur == 5 else
                                           (11 if ur == 6 else 2 * ur))
                        if ur <= 4 and r == 27:
                            plan_proj_evac(gi + 1, 2 * ur + 1)
                        if ur == 7 and r == 4:
                            mg_pending.extend(make_mg(gi + 1))
                        if ur == 7 and r >= 16 and mg_pending:
                            mg_pending.pop(0)()
                    if u >= 1:
                        pu = u - 1
                        if r == 2:
                            plan_AB(pu)
                        if r == 4:
                            plan_sq(pu)
                        if r == 6:
                            plan_ms(pu)
                        if r == 8:
                            plan_lnexp(pu)
                        if r == 9:
                            plan_lnbounce(pu)
                        if r == 12:
                            plan_C(pu)
                p.step("vector", lambda v: v.tensor_copy(o0s[0:65, :], O0[0:65, :]),
                       waits=[W("t_V", u + 1), W("a_c", 2 * u), W("t_C", u),
                              W("t_AB", u)], inc="v_Oe", dr=True)
                p.step("scalar", lambda s: s.activation(
                    o1s[0:65, :], O1[0:65, :], AF.Identity),
                    waits=[W("t_V", u + 1), W("t_C", u), W("t_AB", u)],
                    inc="v_Oe")
                p.step("vector", lambda v: v.tensor_tensor(
                    Abc[64:65, :], o0s[64:65, :], o1s[64:65, :], ALU.mult),
                    waits=[W("v_Oe", 2 * (u + 1))], inc="v_w64", dr=True)
                if u >= 1:
                    plan_fin(u - 1)
                p.step("vector", lambda v: v.tensor_tensor(
                    Abc[0:64, :], o0s[0:64, :], EPI[0:64, 0:512], ALU.mult),
                    waits=[W("t_AB", u + 1)], dr=True)
                p.step("vector", lambda v: v.scalar_tensor_tensor(
                    Bbc[0:64, :], o1s[0:64, :], lamcol[0:64, 0:1],
                    EPI[64:128, 0:512], ALU.mult, ALU.mult),
                    waits=[W("c_or", 2), W("t_ms", u),
                           W("d_fin", 16 * (p.n("d_fin") // 16))])
                p.step("vector", lambda v: v.tensor_tensor(
                    Abc[0:64, :], Abc[0:64, :], Bbc[0:64, :], ALU.add),
                    inc="v_ab", dr=True)
                if u >= 56:
                    plan_wo_stage(u - 56)
    plan_AB(NU - 1)
    plan_sq(NU - 1)
    plan_ms(NU - 1)
    plan_lnexp(NU - 1)
    plan_lnbounce(NU - 1)
    plan_C(NU - 1)
    plan_fin(NU - 1)

    n_wo = 0
    n_t5 = 0
    for qt in range(NQT):
        for oc in range(8):
            bank = PB[n_wo % 2]
            wts = [W("v_wc", 24 + oc + 1)]
            if n_wo == 0:
                wts += [W("a_P", NU * NKC),
                        W("v_fin", NU), W("d_fin", 16 * 32)]
            if n_wo >= 2:
                wts.append(W("v_woe", n_wo - 1))
            for cc in range(8):
                p.step("tensor", lambda t, b_=bank, cc_=cc, oc_=oc, qt_=qt: t.matmul(
                    b_[0:128, 0:512],
                    wobf[:, 1024 * cc_ + 128 * oc_:1024 * cc_ + 128 * (oc_ + 1)],
                    afin[:, 2048 * cc_ + 512 * qt_:2048 * cc_ + 512 * (qt_ + 1)],
                    start=(cc_ == 0), stop=(cc_ == 7), skip_group_check=True),
                    waits=wts if cc == 0 else [],
                    inc="t_wo" if cc == 7 else None)
            n_wo += 1
            p.step("vector", lambda v, b_=bank, oc_=oc: v.tensor_copy(
                woq[:, 512 * oc_:512 * (oc_ + 1)], b_[0:128, 0:512]),
                waits=[W("t_wo", n_wo)], inc="v_woe")
        for j in range(4):
            jg = qt * 4 + j
            xw = []
            if jg == 0:
                xw += [W("t_kq", 96), W("t_vp", 256)]
            if jg >= 2:
                xw.append(W("v_hp", 8 * (jg - 1)))
            p.dma("sync", lambda sy, jg_=jg: sy.dma_start(
                out=xs5[:, 1024 * (jg_ % 2):1024 * (jg_ % 2 + 1)],
                in_=dx[128 * jg_:128 * (jg_ + 1), :]), waits=xw,
                inc=f"d_x5{jg % 2}")
            for oc in range(8):
                bank = PB[4 + (n_t5 % 2)]
                wts = [W("v_woe", 8 * (qt + 1))]
                if n_t5 == 0:
                    wts.append(W("v_Oe", NU))
                if n_t5 >= 2:
                    wts.append(W("v_hp", p.n("v_hp") - 1))
                p.step("tensor", lambda t, b_=bank, oc_=oc, j_=j: t.transpose(
                    b_[0:128, 0:128],
                    woq[:, 512 * oc_ + 128 * j_:512 * oc_ + 128 * (j_ + 1)],
                    ident_f[:, :]), waits=wts, inc="t_tr5")
                n_t5 += 1
                hw = [W("t_tr5", n_t5), W(f"d_x5{jg % 2}", 16 * (jg // 2 + 1))]
                if oc == 0:
                    if jg == 0:
                        hw.append(W("t_V", NU))
                    if jg >= 2:
                        hw += [W(f"d_hs{jg % 2}", 16 * (jg // 2)),
                               W("v_ln2", 2 * (jg // 2))]
                p.step("vector", lambda v, b_=bank, oc_=oc, jg_=jg: v.tensor_tensor(
                    htile[:, 1024 * (jg_ % 2) + 128 * oc_:
                          1024 * (jg_ % 2) + 128 * (oc_ + 1)],
                    xs5[:, 1024 * (jg_ % 2) + 128 * oc_:
                        1024 * (jg_ % 2) + 128 * (oc_ + 1)],
                    b_[0:128, 0:128], ALU.add), waits=hw,
                    inc=[("v_hp", 1), ("v_h", 1)] if oc == 7 else "v_hp")
            p.dma("gpsimd", lambda gp, jg_=jg: gp.dma_start(
                out=d_h[128 * jg_:128 * (jg_ + 1), :],
                in_=htile[:, 1024 * (jg_ % 2):1024 * (jg_ % 2 + 1)]),
                waits=[W("v_h", jg + 1)], inc=f"d_hs{jg % 2}")
            ht = htile[:, 1024 * (jg % 2):1024 * (jg % 2 + 1)]
            pr2 = jg // 2
            b2 = 2 * (pr2 % 2)
            c2 = b2 + jg % 2
            sq_w = [W("v_h", jg + 1), W("v_wc", 32)]
            if jg % 2 == 0 and pr2 >= 2:
                sq_w.append(W("v_q85", pr2 - 1))
            p.step("scalar", lambda s, h_=ht, c_=c2: (s.drain(), s.activation(
                scr5[:, :], h_, AF.Square, accum_out=stat_q8[:, c_:c_ + 1]))[1],
                waits=sq_w, inc="a_sq5")
            p.step("vector", lambda v, h_=ht, c_=c2: v.reduce_sum(
                stat_s8[:, c_:c_ + 1], h_, AX.X),
                waits=[W("v_h", jg + 1)], dr=(jg % 2 == 0))
            if jg % 2 == 0:
                continue
            p.step("vector", lambda v, b_=b2: v.tensor_scalar(
                stat_m8[:, b_:b_ + 2], stat_s8[:, b_:b_ + 2],
                1.0 / 1024.0, None, ALU.mult), dr=True)
            p.step("vector", lambda v, b_=b2: v.tensor_scalar(
                stat_q8[:, b_:b_ + 2], stat_q8[:, b_:b_ + 2],
                1.0 / 1024.0, None, ALU.mult),
                waits=[W("a_sq5", jg + 1)], inc="v_q85")
            p.step("vector", lambda v, b_=b2: v.tensor_tensor(
                stat_r8[:, b_:b_ + 2], stat_m8[:, b_:b_ + 2],
                stat_m8[:, b_:b_ + 2], ALU.mult), dr=True)
            p.step("vector", lambda v, b_=b2: v.tensor_tensor(
                stat_r8[:, b_:b_ + 2], stat_q8[:, b_:b_ + 2],
                stat_r8[:, b_:b_ + 2], ALU.subtract), inc="v_var5", dr=True)
            sqw5 = [W("v_var5", pr2 + 1)]
            if pr2 >= 2:
                sqw5.append(W("v_ln2", 2 * (pr2 - 1)))
            p.step("scalar", lambda s, b_=b2: s.activation(
                stat_r8[:, b_:b_ + 2], stat_r8[:, b_:b_ + 2],
                AF.Sqrt, bias=epsb[:, 0:1]), waits=sqw5, inc="a_sd5")
            p.step("vector", lambda v, b_=b2: v.reciprocal(
                stat_r8[:, b_:b_ + 2], stat_r8[:, b_:b_ + 2]),
                waits=[W("a_sd5", pr2 + 1)], dr=True)
            for jj in (jg - 1, jg):
                hs = htile[:, 1024 * (jj % 2):1024 * (jj % 2 + 1)]
                p.step("vector", lambda v, h_=hs, jj_=jj, c_=b2 + jj % 2:
                       v.tensor_scalar(
                    ln2[:, 1024 * jj_:1024 * (jj_ + 1)], h_,
                    stat_m8[:, c_:c_ + 1], stat_r8[:, c_:c_ + 1],
                    ALU.subtract, ALU.mult), dr=(jj == jg - 1))
            for jj in (jg - 1, jg):
                p.step("vector", lambda v, jj_=jj: v.reduce_max(
                    amx16[:, jj_:jj_ + 1], ln2[:, 1024 * jj_:1024 * (jj_ + 1)],
                    AX.X, apply_absolute_value=True), inc="v_ln2",
                    dr=(jj == jg - 1))

    p.step("vector", lambda v: v.reduce_max(amax1[:, 0:1], amx16[:, :], AX.X),
           waits=[W("v_ln2", 16)], inc="v_am0", dr=True)
    p.step("tensor", lambda t: t.transpose(EPI[0:1, 0:128], amax1[:, 0:1],
                                           ident_f[:, :]),
           waits=[W("v_am0", 1)], inc="t_am1")
    p.step("vector", lambda v: v.tensor_copy(sc_row[0:1, :], EPI[0:1, 0:128]),
           waits=[W("t_am1", 1)])
    p.step("vector", lambda v: v.reduce_max(ccs[0:1, 0:1], sc_row[0:1, :], AX.X),
           inc="v_am1", dr=True)
    p.dma("gpsimd", lambda gp: gp.dma_start(out=d_cc[0], in_=ccs[:, :]),
          waits=[W("v_am1", 1)], inc="d_cs")
    p.step("gpsimd", lambda gp: gp.collective_compute(
        "AllReduce", ALU.max, replica_groups=[list(range(8))],
        ins=[d_cc[0]], outs=[d_cc[1]]),
        waits=[W("d_cs", 16)], inc="cc")
    p.dma("gpsimd", lambda gp: gp.dma_start(out=ccs[:, :], in_=d_cc[1]),
          waits=[W("cc", 1)], inc="d_ccr")
    p.step("vector", lambda v: v.reciprocal(sc_row[0:1, 0:1], ccs[0:1, 0:1]),
           waits=[W("d_ccr", 16)], dr=True)
    p.step("vector", lambda v: v.tensor_scalar(sc_row[0:1, 0:1], sc_row[0:1, 0:1],
                                               32767.0, None, ALU.mult), inc="v_s2",
           dr=True)
    p.step("tensor", lambda t: t.matmul(
        EPI[0:128, 0:1], ones_any[0:1, :], sc_row[0:1, 0:1],
        start=True, stop=True, skip_group_check=True),
        waits=[W("v_s2", 1), W("v_fin", NU)], inc="t_am1")
    p.step("vector", lambda v: v.tensor_copy(s2col[:, 0:1], EPI[0:128, 0:1]),
           waits=[W("t_am1", 2)], inc="g_s2")

    n_xt = 0
    for jg in range(16):
        wts = [W("g_s2", 1)]
        if jg == 0:
            wts += [W("d_hs0", 16 * 8), W("d_hs1", 16 * 8)]
        p.step("vector", lambda v, jg_=jg: v.tensor_scalar(
            xsq_i[:, :], ln2[:, 1024 * jg_:1024 * (jg_ + 1)], s2col[:, 0:1],
            None, ALU.mult), waits=wts, dr=True)
        p.step("vector", lambda v: v.tensor_copy(xsq_b[:, :], xsq_i[:, :]),
               inc="v_xq", dr=True)
        for ec in range(8):
            bank = PB[n_xt % 4].bitcast(BF16)
            wts = [W("v_xq", jg + 1)]
            if n_xt >= 4:
                wts.append(W("v_xqe", n_xt - 3))
            if n_xt == 0:
                wts.append(W("v_woe", 32))
            p.step("tensor", lambda t, b_=bank, e_=ec: t.transpose(
                b_[0:128, 0:128], xsq_b[:, 128 * e_:128 * (e_ + 1)], ident_b[:, :]),
                waits=wts, inc="t_xqt")
            p.step("vector", lambda v, b_=bank, e_=ec, jg_=jg: v.tensor_copy(
                xsqT[:, 2048 * e_ + 128 * jg_:2048 * e_ + 128 * (jg_ + 1)],
                b_[0:128, 0:128]), waits=[W("t_xqt", n_xt + 1)], inc="v_xqe")
            n_xt += 1

    for cb in range(32):
        b0 = 1024 * (cb % 2)
        for e in range(8):
            wts = []
            if e == 0:
                if cb == 0:
                    wts.append(W("v_h", 16))
                if cb >= 2:
                    wts.append(W("v_wq5d", 8 * (cb - 1)))
            p.dma("sync", lambda sy, e_=e, cb_=cb, b_=b0: sy.dma_start(
                out=wst5[:, b_ + 128 * e_:b_ + 128 * (e_ + 1)],
                in_=dwin[128 * e_:128 * (e_ + 1), 128 * cb_:128 * (cb_ + 1)]),
                waits=wts, inc=f"d_wi{cb % 2}")
        p.step("vector", lambda v, b_=b0: v.tensor_reduce(
            wsc5[:, 0:128],
            AP(tensor=wst5, offset=b_, ap=[[2048, 128], [1, 128], [128, 8]]),
            AX.X, ALU.max, apply_absolute_value=True),
            waits=[W(f"d_wi{cb % 2}", 128 * (cb // 2 + 1))],
            inc="v_wq5", dr=True)
        tw = [W("v_wq5", p.n("v_wq5"))]
        if cb == 0:
            tw.append(W("v_hp", 128))
        if cb >= 1:
            tw.append(W("v_wq5b", cb))
        p.step("tensor", lambda t: t.transpose(PB[4][0:128, 0:128], wsc5[:, 0:128],
                                               ident_f[:, :]), waits=tw, inc="t_wq5")
        p.step("vector", lambda v: v.reduce_max(sc_t[:, 0:1], PB[4][0:128, 0:128],
                                                AX.X),
               waits=[W("t_wq5", p.n("t_wq5"))], dr=True)
        p.step("vector", lambda v, cb_=cb: v.tensor_scalar(
            acolw[:, cb_:cb_ + 1], sc_t[:, 0:1], 1e-8, 1.0 / 127.0,
            ALU.max, ALU.mult),
            inc="v_wq5b" if cb % 2 == 0 else None, dr=True)
        if cb % 2 == 0:
            continue
        p.step("vector", lambda v, cb_=cb: v.reciprocal(
            sc_t2[:, 0:2], acolw[:, cb_ - 1:cb_ + 1]),
            inc="v_wq5b", dr=True)
        for tx, cbx in ((0, cb - 1), (1, cb)):
            bx0 = 1024 * (cbx % 2)
            t2w = [W("v_wq5b", cb + 1)]
            if cbx >= 1:
                t2w.append(W("v_wq5c", cbx))
            p.step("tensor", lambda t, t_=tx: t.transpose(
                PB[5][0:1, 0:128], sc_t2[:, t_:t_ + 1],
                ident_f[:, :]), waits=t2w, inc="t_wq5b")
            p.step("vector", lambda v: v.tensor_copy(sc_row[0:1, :],
                                                     PB[5][0:1, 0:128]),
                   waits=[W("t_wq5b", p.n("t_wq5b"))], inc="v_wq5c")
            gw = [W("v_wq5c", p.n("v_wq5c"))]
            if cbx >= 1:
                gw.append(W("v_wq5d", 8 * cbx))
            if cbx == 0:
                gw += [W("g_s2", 1), W("c_or", 2)]
            p.step("tensor", lambda t: t.matmul(
                EPI[0:128, 0:128], ones_any[0:1, :], sc_row[0:1, 0:128],
                start=True, stop=True, skip_group_check=True),
                waits=gw, inc="g_wq5")
            p.step("vector", lambda v: v.tensor_copy(scbb[:, :],
                                                     EPI[0:128, 0:128]),
                   waits=[W("g_wq5", cbx + 1)], dr=True)
            p.step("vector", lambda v, b_=bx0: v.tensor_tensor(
                wi32b[:, :], wst5[:, b_:b_ + 1024],
                AP(tensor=scbb, offset=0, ap=[[128, 128], [0, 8], [1, 128]]),
                ALU.mult), dr=True)
            p.step("vector", lambda v, cb_=cbx: v.tensor_copy(
                AP(tensor=win_i, offset=128 * cb_,
                   ap=[[32768, 128], [4096, 8], [1, 128]]),
                wi32b[:, :]), inc=[("v_wq5d", 8)], dr=True)

    p.step("vector", lambda v: v.reciprocal(sc_t[:, 0:1], s2col[:, 0:1]),
           waits=[W("g_s2", 1), W("v_wq5d", 256)], dr=True)
    p.step("vector", lambda v: v.tensor_scalar(gsc[:, :], acolw[:, :], sc_t[:, 0:1],
                                               None, ALU.mult), inc="v_gsc",
           dr=True)
    n_f1 = 0
    for fc in range(32):
        for qt in range(NQT):
            bank = PB[n_f1 % 4]
            wts = [W("v_wq5d", 256), W("v_xqe", 128)]
            if n_f1 >= 4:
                wts.append(W("a_gel", n_f1 - 3))
            for e in range(8):
                p.step("tensor", lambda t, b_=bank, e_=e, fc_=fc, qt_=qt: t.matmul(
                    b_[0:128, 0:512],
                    win_i[:, 4096 * e_ + 128 * fc_:4096 * e_ + 128 * (fc_ + 1)],
                    xsqT[:, 2048 * e_ + 512 * qt_:2048 * e_ + 512 * (qt_ + 1)],
                    start=(e_ == 0), stop=(e_ == 7), skip_group_check=True),
                    waits=wts if e == 0 else [],
                    inc="t_f1" if e == 7 else None)
            gw = [W("t_f1", n_f1 + 1), W("v_gsc", 1)]
            if fc >= 2 and qt == 0:
                gw.append(W(f"d_hd{fc % 2}", 16 * (fc // 2)))
            p.step("scalar", lambda s, b_=bank, fc_=fc, qt_=qt: s.activation(
                hidb[:, 2048 * (fc_ % 2) + 512 * qt_:
                     2048 * (fc_ % 2) + 512 * (qt_ + 1)],
                b_[0:128, 0:512], AF.Gelu, scale=gsc[:, fc_:fc_ + 1],
                bias=b_in[:, fc_:fc_ + 1]), waits=gw, inc="a_gel")
            n_f1 += 1
        p.step("vector", lambda v, fc_=fc: v.reduce_max(
            amx32[:, fc_:fc_ + 1], hidb[:, 2048 * (fc_ % 2):2048 * (fc_ % 2 + 1)],
            AX.X, apply_absolute_value=True), waits=[W("a_gel", 4 * (fc + 1))],
            inc="v_ha")
        p.dma("gpsimd", lambda gp, fc_=fc: gp.dma_start(
            out=d_hid[128 * fc_:128 * (fc_ + 1), :],
            in_=hidb[:, 2048 * (fc_ % 2):2048 * (fc_ % 2 + 1)]),
            waits=[W("v_ha", fc + 1)], inc=f"d_hd{fc % 2}")
        buf = wstF if fc % 2 == 0 else wstF2
        dw_w = []
        if fc == 0:
            dw_w += [W("a_sq5", 16), W("v_wc", 32)]
        if fc >= 2:
            dw_w.append(W("v_wo6", fc - 1))
        p.dma("sync", lambda sy, fc_=fc, b_=buf: sy.dma_start(
            out=b_[:, :], in_=dwout[128 * fc_:128 * (fc_ + 1), :]),
            waits=dw_w, inc=f"d_wo{fc % 2}")
        p.step("vector", lambda v, b_=buf: v.reduce_max(
            sc_t[:, 0:1], b_[:, :], AX.X, apply_absolute_value=True),
            waits=[W(f"d_wo{fc % 2}", 16 * (fc // 2 + 1))], dr=True)
        p.step("vector", lambda v, fc_=fc: v.tensor_scalar(
            rocol[:, fc_:fc_ + 1], sc_t[:, 0:1], 1e-8, 1.0 / 127.0,
            ALU.max, ALU.mult), dr=True)
        if fc % 2 == 0:
            continue
        p.step("vector", lambda v, fc_=fc: v.reciprocal(
            sc_t2[:, 0:2], rocol[:, fc_ - 1:fc_ + 1]), dr=True)
        for tx, fcx in ((0, fc - 1), (1, fc)):
            bufx = wstF if fcx % 2 == 0 else wstF2
            wq_w = []
            if fcx == 0:
                wq_w += [W("v_ln2", 16), W("d_hs0", 16 * 8), W("d_hs1", 16 * 8)]
            p.step("vector", lambda v, b_=bufx, t_=tx: v.tensor_scalar(
                wo_i32[:, :], b_[:, :], sc_t2[:, t_:t_ + 1], None, ALU.mult),
                waits=wq_w, dr=True)
            p.step("vector", lambda v, fc_=fcx: v.tensor_copy(
                wout_i[:, 1024 * fc_:1024 * (fc_ + 1)], wo_i32[:, :]),
                inc="v_wo6", dr=True)

    p.step("vector", lambda v: v.reduce_max(amax2[:, 0:1], amx32[:, :], AX.X),
           waits=[W("v_ha", 32)], inc="v_am0", dr=True)
    p.step("tensor", lambda t: t.transpose(EPI[0:1, 0:128], amax2[:, 0:1],
                                           ident_f[:, :]),
           waits=[W("v_am0", 2), W("v_am1", 1)], inc="t_am2")
    p.step("vector", lambda v: v.tensor_copy(sc_row[0:1, :], EPI[0:1, 0:128]),
           waits=[W("t_am2", 1)])
    p.step("vector", lambda v: v.reduce_max(ccs[0:1, 0:1], sc_row[0:1, :], AX.X),
           inc="v_am2", dr=True)
    p.dma("gpsimd", lambda gp: gp.dma_start(out=d_cc[2], in_=ccs[:, :]),
          waits=[W("v_am2", 1)], inc="d_cs")
    p.step("gpsimd", lambda gp: gp.collective_compute(
        "AllReduce", ALU.max, replica_groups=[list(range(8))],
        ins=[d_cc[2]], outs=[d_cc[3]]),
        waits=[W("d_cs", 32)], inc="cc")
    p.dma("gpsimd", lambda gp: gp.dma_start(out=ccs[:, :], in_=d_cc[3]),
          waits=[W("cc", 2)], inc="d_ccr")
    p.step("vector", lambda v: v.reciprocal(sc_row[0:1, 1:2], ccs[0:1, 0:1]),
           waits=[W("d_ccr", 32)], dr=True)
    p.step("vector", lambda v: v.tensor_scalar(sc_row[0:1, 1:2], sc_row[0:1, 1:2],
                                               32767.0, None, ALU.mult), inc="v_s3",
           dr=True)
    p.step("tensor", lambda t: t.matmul(
        EPI[0:128, 0:1], ones_any[0:1, :], sc_row[0:1, 1:2],
        start=True, stop=True, skip_group_check=True),
        waits=[W("v_s3", 1), W("v_am2", 1)], inc="t_am2")
    p.step("vector", lambda v: v.tensor_copy(s3col[:, 0:1], EPI[0:128, 0:1]),
           waits=[W("t_am2", 2)], inc="g_s3")
    p.step("vector", lambda v: v.reciprocal(sc_t[:, 0:1], s3col[:, 0:1]),
           waits=[W("g_s3", 1), W("v_wo6", 32)], dr=True)
    p.step("vector", lambda v: v.tensor_scalar(invh[:, :], rocol[:, :], sc_t[:, 0:1],
                                               None, ALU.mult), inc="v_gsc",
           dr=True)

    n_t7 = 0
    for qt in range(NQT):
        for fc in range(32):
            k = qt * 32 + fc
            wts = []
            if k == 0:
                wts += [W("d_hd0", 16 * 16), W("d_hd1", 16 * 16),
                        W("d_hs0", 16 * 8), W("d_hs1", 16 * 8),
                        W("g_s3", 1)]
            if k >= 2:
                wts.append(W("v_f2e", k - 1))
            p.dma("sync", lambda sy, fc_=fc, qt_=qt, k_=k: sy.dma_start(
                out=hraw7[:, 512 * (k_ % 2):512 * (k_ % 2 + 1)],
                in_=d_hid[128 * fc_:128 * (fc_ + 1), 512 * qt_:512 * (qt_ + 1)]),
                waits=wts, inc=f"d_h7{k % 2}")
            qw = [W(f"d_h7{k % 2}", 16 * (k // 2 + 1)), W("v_gsc", 2)]
            if k >= 2:
                qw.append(W("t_f2", 8 * (k - 1)))
            hqb = hq_i32 if k % 2 == 0 else hq7b
            p.step("vector", lambda v, k_=k, h_=hqb: v.tensor_scalar(
                h_[:, :], hraw7[:, 512 * (k_ % 2):512 * (k_ % 2 + 1)],
                s3col[:, 0:1], None, ALU.mult), waits=qw, dr=(k == 0))
            p.step("vector", lambda v, k_=k, fc_=fc, h_=hqb: v.tensor_scalar(
                rhs7[:, 512 * (k_ % 2):512 * (k_ % 2 + 1)], h_[:, :],
                invh[:, fc_:fc_ + 1], None, ALU.mult), inc="v_f2e", dr=True)
            mw = [W("v_f2e", k + 1)]
            if fc == 0:
                if qt == 0:
                    mw += [W("a_gel", 128), W("v_am2", 1), W("v_wq5c", 32)]
                else:
                    mw.append(W("v_f2o", 8 * qt))
            for oc in range(8):
                p.step("tensor", lambda t, oc_=oc, fc_=fc, k_=k: t.matmul(
                    PB[oc_][0:128, 0:512],
                    wout_i[:, 1024 * fc_ + 128 * oc_:1024 * fc_ + 128 * (oc_ + 1)],
                    rhs7[:, 512 * (k_ % 2):512 * (k_ % 2 + 1)],
                    start=(fc_ == 0), stop=(fc_ == 31), skip_group_check=True),
                    waits=mw if oc == 0 else [], inc="t_f2")
        for oc in range(8):
            ew = [W("t_f2", 8 * 32 * (qt + 1))]
            if qt == 0 and oc == 0:
                ew.append(W("t_f1", 128))
            p.step("vector", lambda v, oc_=oc: v.tensor_scalar(
                outTs[:, 512 * oc_:512 * (oc_ + 1)], PB[oc_][0:128, 0:512],
                b_out[:, oc_:oc_ + 1], None, ALU.add), waits=ew, inc="v_f2o")
        for j in range(4):
            jg = qt * 4 + j
            hw = []
            if jg >= 2:
                hw.append(W("v_out", 8 * (jg - 1)))
            p.dma("sync", lambda sy, jg_=jg: sy.dma_start(
                out=hin7[:, 1024 * (jg_ % 2):1024 * (jg_ % 2 + 1)],
                in_=d_h[128 * jg_:128 * (jg_ + 1), :]), waits=hw,
                inc=f"d_hn{jg % 2}")
            for oc in range(8):
                bank = PB[n_t7 % 2]
                wts = [W("v_f2o", 8 * (qt + 1))]
                if n_t7 >= 2:
                    wts.append(W("v_out", n_t7 - 1))
                p.step("tensor", lambda t, b_=bank, oc_=oc, j_=j: t.transpose(
                    b_[0:128, 0:128],
                    outTs[:, 512 * oc_ + 128 * j_:512 * oc_ + 128 * (j_ + 1)],
                    ident_f[:, :]), waits=wts, inc="t_tr7")
                n_t7 += 1
                ow = [W("t_tr7", n_t7), W(f"d_hn{jg % 2}", 16 * (jg // 2 + 1))]
                if oc == 0 and jg >= 2:
                    ow.append(W(f"d_o{jg % 2}", 16 * (jg // 2)))
                p.step("vector", lambda v, b_=bank, oc_=oc, jg_=jg: v.tensor_tensor(
                    outs7[:, 1024 * (jg_ % 2) + 128 * oc_:
                          1024 * (jg_ % 2) + 128 * (oc_ + 1)],
                    hin7[:, 1024 * (jg_ % 2) + 128 * oc_:
                         1024 * (jg_ % 2) + 128 * (oc_ + 1)],
                    b_[0:128, 0:128], ALU.add), waits=ow, inc="v_out")
            p.dma("gpsimd", lambda gp, jg_=jg: gp.dma_start(
                out=dout[128 * jg_:128 * (jg_ + 1), :],
                in_=outs7[:, 1024 * (jg_ % 2):1024 * (jg_ % 2 + 1)]),
                waits=[W("v_out", 8 * (jg + 1))], inc=f"d_o{jg % 2}")
    p.step("gpsimd", lambda gp: (gp.wait_ge(p.sem("d_o0"), 16 * 8),
                                 gp.wait_ge(p.sem("d_o1"), 16 * 8)) and None)

    with nc.Block() as block:
        @block.sync
        def _(sy):
            p.emit("sync", sy)

        @block.gpsimd
        def _(gp):
            p.emit("gpsimd", gp)

        @block.scalar
        def _(sc):
            p.emit("scalar", sc)

        @block.tensor
        def _(te):
            p.emit("tensor", te)

        @block.vector
        def _(ve):
            p.emit("vector", ve)

    return nc


_NC_CACHE = None


def kernel(L0_attn_query, L0_attn_Wq, L0_attn_Wk, L0_attn_Wv, L0_attn_Wo,
           L0_attn_lambda_q1, L0_attn_lambda_k1, L0_attn_lambda_q2,
           L0_attn_lambda_k2, L0_attn_subln_w, L0_ffn_W_in, L0_ffn_b_in,
           L0_ffn_W_out, L0_ffn_b_out):
    global _NC_CACHE
    f32 = np.float32
    x = np.asarray(L0_attn_query, f32)
    wqT = np.ascontiguousarray(np.asarray(L0_attn_Wq, f32).T)
    wkT = np.ascontiguousarray(np.asarray(L0_attn_Wk, f32).T)
    wvT = np.ascontiguousarray(np.asarray(L0_attn_Wv, f32).T)
    woT = np.ascontiguousarray(np.asarray(L0_attn_Wo, f32).T)
    win = np.ascontiguousarray(np.asarray(L0_ffn_W_in, f32))
    wout = np.ascontiguousarray(np.asarray(L0_ffn_W_out, f32))
    b_in_pp = np.ascontiguousarray(np.asarray(L0_ffn_b_in, f32).reshape(32, 128).T)
    b_out_pp = np.ascontiguousarray(np.asarray(L0_ffn_b_out, f32).reshape(8, 128).T)
    lam_r = np.ascontiguousarray(np.concatenate(
        [np.asarray(a, f32).ravel() for a in
         (L0_attn_lambda_q1, L0_attn_lambda_k1,
          L0_attn_lambda_q2, L0_attn_lambda_k2)])[None, :])
    subln_r = np.ascontiguousarray(np.asarray(L0_attn_subln_w, f32)[None, :])
    ident_bf = np.eye(128, dtype=ml_dtypes.bfloat16)
    ident_f = np.eye(128, dtype=f32)

    if _NC_CACHE is None:
        _NC_CACHE = build_nc()
    nc = _NC_CACHE

    in_maps = []
    for c in range(8):
        b, half = c // 2, c % 2
        xb = x[b]
        xp = np.ascontiguousarray(np.concatenate(
            [xb[TQ * half:TQ * (half + 1)], xb[TQ * (1 - half):TQ * (2 - half)]],
            axis=0))
        in_maps.append({
            "x": xp, "wqT": wqT, "wkT": wkT, "wvT": wvT, "woT": woT,
            "win": win, "wout": wout, "b_in_pp": b_in_pp, "b_out_pp": b_out_pp,
            "lam_r": lam_r, "subln_r": subln_r,
            "ident_bf": ident_bf, "ident_f": ident_f,
        })
    res = run_bass_kernel_spmd(nc, in_maps, core_ids=list(range(8)))
    out = np.empty((B, T, D), f32)
    for c in range(8):
        b, half = c // 2, c % 2
        out[b, TQ * half:TQ * (half + 1)] = res.results[c]["out"]
    return out
